# version 05 (final)
# baseline (speedup 1.0000x reference)
import math
import numpy as np
import ml_dtypes
import concourse.bass as bass
import concourse.mybir as mybir
from concourse.bass_utils import run_bass_kernel_spmd
from concourse.alu_op_type import AluOpType as ALU

F32 = mybir.dt.float32
BF16 = mybir.dt.bfloat16
I32 = mybir.dt.int32
AF = mybir.ActivationFunctionType
AX = mybir.AxisListType

B, T, D = 8, 2048, 1024
NDT = D // 128
P = 128


ENGINES = ("tensor", "vector", "scalar", "gpsimd", "sync")


class Buf:
    __slots__ = ("name", "arena", "c0", "cols", "p0", "pn", "dtype", "prog")

    def __init__(self, prog, name, arena, c0, cols, p0, pn, dtype):
        self.prog, self.name, self.arena = prog, name, arena
        self.c0, self.cols, self.p0, self.pn, self.dtype = c0, cols, p0, pn, dtype

    def ap(self, ps=None, cs=None):
        p0 = self.p0 if ps is None else self.p0 + ps[0]
        pn = self.pn if ps is None else ps[1]
        c0 = self.c0 if cs is None else self.c0 + cs[0]
        cn = self.cols if cs is None else cs[1]
        return self.arena.t[p0:p0 + pn, c0:c0 + cn]

    def view(self, p0, pn):
        return Buf(self.prog, self.name, self.arena, self.c0, self.cols, self.p0 + p0, pn, self.dtype)

    def reg(self, ps=None, cs=None):
        p0 = self.p0 if ps is None else self.p0 + ps[0]
        pn = self.pn if ps is None else ps[1]
        c0 = self.c0 if cs is None else self.c0 + cs[0]
        cn = self.cols if cs is None else cs[1]
        return (self.arena, p0, p0 + pn, c0, c0 + cn)


class Arena:
    def __init__(self, name, t, ncols, dtype, psum=False):
        self.name, self.t, self.ncols, self.dtype, self.psum = name, t, ncols, dtype, psum
        self.free_list = [(0, ncols)]

    def alloc(self, cols, align=16):
        for i, (c0, n) in enumerate(self.free_list):
            a0 = (c0 + align - 1) // align * align
            if a0 + cols <= c0 + n:
                pre = (c0, a0 - c0)
                post = (a0 + cols, (c0 + n) - (a0 + cols))
                repl = []
                if pre[1] > 0:
                    repl.append(pre)
                if post[1] > 0:
                    repl.append(post)
                self.free_list[i:i + 1] = repl
                return a0
        raise MemoryError(f"arena {self.name}: cannot alloc {cols} cols; free={self.free_list}")

    def free(self, c0, cols):
        self.free_list.append((c0, cols))
        self.free_list.sort()
        merged = []
        for c, n in self.free_list:
            if merged and merged[-1][0] + merged[-1][1] == c:
                merged[-1] = (merged[-1][0], merged[-1][1] + n)
            else:
                merged.append((c, n))
        self.free_list = [tuple(x) for x in merged]


class Node:
    __slots__ = ("idx", "engine", "emit", "reads", "writes", "sem_val", "is_dma",
                 "waits", "needs_inc", "tag")

    def __init__(self, idx, engine, emit, reads, writes, is_dma, tag):
        self.idx, self.engine, self.emit = idx, engine, emit
        self.reads, self.writes, self.is_dma = reads, writes, is_dma
        self.sem_val = None
        self.waits = {}
        self.needs_inc = False
        self.tag = tag


class DramRes:

    def __init__(self, name):
        self.name = name


I32RES = None


I32RES = DramRes("i32scratch")


class Prog:
    def __init__(self, nc, f32_cols, bf16_cols):
        self.nc = nc
        self.nodes = []
        self.ctx = []
        self.a32 = None
        self.a16 = None
        self._a32_cols = f32_cols
        self._b16_cols = bf16_cols
        self.hist = {}
        self.psum_banks = []
        self._sems = {}
        self._closed = False

    def open(self, es):
        nc = self.nc
        t32 = es.enter_context(nc.sbuf_tensor("arena32", [128, self._a32_cols], F32))
        t16 = es.enter_context(nc.sbuf_tensor("arena16", [128, self._b16_cols], BF16))
        self.a32 = Arena("a32", t32, self._a32_cols, F32)
        self.a16 = Arena("a16", t16, self._b16_cols, BF16)
        self.psB = []
        for i in range(3):
            t = es.enter_context(nc.psum_tensor(f"psf{i}", [128, 512], F32))
            a = Arena(f"psf{i}", t, 512, F32, psum=True)
            self.psB.append(a)
        self.psW = []
        for i in range(2):
            t = es.enter_context(nc.psum_tensor(f"psw{i}", [128, 1024], F32))
            self.psW.append(Arena(f"psw{i}", t, 1024, F32, psum=True))
        self.psT = []
        for i in range(1):
            t = es.enter_context(nc.psum_tensor(f"psb{i}", [128, 1024], BF16))
            a = Arena(f"psb{i}", t, 1024, BF16, psum=True)
            self.psT.append(a)
        for e in ENGINES:
            self._sems[e] = es.enter_context(nc.semaphore(f"s_{e}"))
        self._lanes = {}
        self._lane_rr = {}
        self._lane_prev = {}
        for q in ("qs", "qc", "qg"):
            self._lanes[q] = []
            self._lane_rr[q] = 0
            for i in range(3):
                ln = f"{q}{i}"
                self._sems[ln] = es.enter_context(nc.semaphore(ln))
                self._lanes[q].append(ln)
                self._lane_prev[ln] = None
        self.i32buf = es.enter_context(nc.sbuf_tensor("i32buf", [64, 512], I32))

    def alloc(self, name, pn, cols, dtype, p0=0):
        ar = self.a32 if dtype in (F32, I32) else self.a16
        c0 = ar.alloc(cols)
        return Buf(self, name, ar, c0, cols, p0, pn, dtype)

    def freebuf(self, buf):
        buf.arena.free(buf.c0, buf.cols)

    def psum(self, which, i):
        if which == "f" and i == 3:
            ar = self.psW[1]
        else:
            ar = {"f": self.psB, "b": self.psT, "w": self.psW}[which][i]
        return Buf(self, f"{ar.name}", ar, 0, ar.ncols, 0, 128, ar.dtype)

    @staticmethod
    def _overlap(r1, r2):
        if r1[0] is not r2[0]:
            return False
        if isinstance(r1[0], DramRes):
            return True
        return not (r1[2] <= r2[1] or r2[2] <= r1[1] or r1[4] <= r2[3] or r2[4] <= r1[3])

    def _expand(self, regs):
        out = []
        for r in regs:
            if isinstance(r, Buf):
                r = r.reg()
            if isinstance(r, DramRes):
                r = (r,)
            ar = r[0]
            if isinstance(ar, Arena) and ar.psum:
                r = (ar, 0, 128, 0, ar.ncols)
            out.append(r)
        return out

    def op(self, engine, emit, reads=(), writes=(), is_dma=False, tag=""):
        reads = self._expand(reads)
        writes = self._expand(writes)
        if is_dma:
            q = is_dma
            lane = self._lanes[q][self._lane_rr[q] % len(self._lanes[q])]
            self._lane_rr[q] += 1
            is_dma = lane
        nd = Node(len(self.nodes), engine, emit, reads, writes, is_dma, tag)
        self.nodes.append(nd)
        if is_dma:
            prev = self._lane_prev[is_dma]
            if prev is not None:
                prev.needs_inc = True
                nd.waits[is_dma] = prev
            self._lane_prev[is_dma] = nd
        deps = set()
        for r in reads:
            h = self.hist.setdefault(r[0], {"w": [], "r": []})
            for n2, r2 in h["w"]:
                if self._overlap(r, r2):
                    deps.add(n2)
            if isinstance(r[0], Arena) and r[0].psum:
                for n2, r2 in h["r"]:
                    if n2.engine != engine and self._overlap(r, r2):
                        deps.add(n2)
        for w in writes:
            h = self.hist.setdefault(w[0], {"w": [], "r": []})
            for n2, r2 in h["w"]:
                if self._overlap(w, r2):
                    deps.add(n2)
            for n2, r2 in h["r"]:
                if self._overlap(w, r2):
                    deps.add(n2)
        for r in reads:
            self.hist[r[0]]["r"].append((nd, r))
        for w in writes:
            h = self.hist[w[0]]
            h["w"] = [e for e in h["w"] if not self._covers(w, e[1])]
            h["r"] = [e for e in h["r"] if not self._covers(w, e[1])]
            h["w"].append((nd, w))
        for n2 in deps:
            if n2.engine == "tensor" and engine == "tensor" and not n2.is_dma and not is_dma:
                continue
            n2.needs_inc = True
            key = n2.engine if not n2.is_dma else n2.is_dma
            cur = nd.waits.get(key)
            if cur is None or n2.idx > cur.idx:
                nd.waits[key] = n2
        return nd

    @staticmethod
    def _covers(w, r):
        if w[0] is not r[0]:
            return False
        if isinstance(w[0], DramRes):
            return True
        return w[1] <= r[1] and w[2] >= r[2] and w[3] <= r[3] and w[4] >= r[4]

    def emit_all(self, block, final_waits=()):
        counters = {}
        for nd in self.nodes:
            key = nd.engine if not nd.is_dma else nd.is_dma
            if nd.is_dma:
                nd.needs_inc = True
            if nd.needs_inc:
                counters[key] = counters.get(key, 0) + (16 if nd.is_dma else 1)
                nd.sem_val = counters[key]
        by_engine = {e: [] for e in ENGINES}
        for nd in self.nodes:
            by_engine[nd.engine].append(nd)
        sems = self._sems
        hw = {}

        def emit_stream(eng_obj, engine_name, nodes):
            for nd in nodes:
                for key, n2 in nd.waits.items():
                    v = n2.sem_val
                    if v is None:
                        continue
                    if hw.get((engine_name, key), 0) >= v:
                        continue
                    eng_obj.wait_ge(sems[key], v)
                    hw[(engine_name, key)] = v
                inst = nd.emit()
                if nd.needs_inc and inst is not None:
                    key = nd.engine if not nd.is_dma else nd.is_dma
                    inst.then_inc(sems[key], 16 if nd.is_dma else 1)
            for key, v in (final_waits if engine_name == "sync" else ()):
                eng_obj.wait_ge(sems[key], v)

        if not final_waits:
            final_waits = [(k, v) for k, v in counters.items()
                           if k[:2] in ("qs", "qc", "qg") and len(k) == 3]
        for name in ENGINES:
            sec = getattr(block, name)
            sec(lambda eng, _n=by_engine[name], _e=name: emit_stream(eng, _e, _n))

    def dma_count(self, q):
        c = 0
        for nd in self.nodes:
            if nd.is_dma == q:
                c += 16
        return c

    def mm(self, out, lhsT, rhs, start=True, stop=True, tag="mm"):
        oap, oreg = out if isinstance(out, tuple) else (out.ap(), out.reg())
        lap, lreg = lhsT if isinstance(lhsT, tuple) else (lhsT.ap(), lhsT.reg())
        rap, rreg = rhs if isinstance(rhs, tuple) else (rhs.ap(), rhs.reg())
        rw = [oreg] if start else [oreg]
        rd = [lreg, rreg] + ([] if start else [oreg])
        return self.op("tensor", lambda: self.nc.tensor.matmul(oap, lap, rap, start=start, stop=stop),
                       reads=rd, writes=rw, tag=tag)

    def transpose(self, out, in_, ident, tag="tp"):
        oap, oreg = out
        iap, ireg = in_
        if isinstance(ident, Buf):
            pn = ireg[2] - ireg[1]
            ib = ireg[1]
            idap, idreg = ident.ap((ib, pn), (ib, pn)), ident.reg((ib, pn), (ib, pn))
        else:
            idap, idreg = ident
            pn = ireg[2] - ireg[1]
            if idreg[2] - idreg[1] != pn:
                idap = idap
        return self.op("tensor", lambda: self.nc.tensor.transpose(oap, iap, idap),
                       reads=[ireg, idreg], writes=[oreg], tag=tag)

    def act(self, out, in_, func, bias=None, scale=None, accum_out=None, engine="scalar", tag="act"):
        oap, oreg = out
        iap, ireg = in_
        kw = {}
        rd = [ireg]
        wr = [oreg]
        if bias is not None:
            if isinstance(bias, tuple):
                kw["bias"] = bias[0]
                rd.append(bias[1])
            else:
                kw["bias"] = bias
        if scale is not None:
            if isinstance(scale, tuple):
                kw["scale"] = scale[0]
                rd.append(scale[1])
            else:
                kw["scale"] = scale
        if accum_out is not None:
            kw["accum_out"] = accum_out[0]
            wr.append(accum_out[1])
        return self.op(engine, lambda: self.nc.scalar.activation(oap, iap, func, **kw),
                       reads=rd, writes=wr, tag=tag)

    def tt(self, out, a, b, alu, engine="vector", tag="tt"):
        oap, oreg = out
        aap, areg = a
        bap, breg = b
        eng = getattr(self.nc, engine)
        return self.op(engine, lambda: eng.tensor_tensor(oap, aap, bap, op=alu),
                       reads=[areg, breg], writes=[oreg], tag=tag)

    def ts(self, out, in_, s1, op0, s2=None, op1=ALU.bypass, accum_out=None, engine="vector", tag="ts"):
        oap, oreg = out
        iap, ireg = in_
        rd = [ireg]
        wr = [oreg]
        if isinstance(s1, tuple):
            rd.append(s1[1]); s1 = s1[0]
        if isinstance(s2, tuple):
            rd.append(s2[1]); s2 = s2[0]
        ao = None
        if accum_out is not None:
            ao = accum_out[0]
            wr.append(accum_out[1])
        eng = getattr(self.nc, engine)
        return self.op(engine, lambda: eng.tensor_scalar(oap, iap, s1, s2, op0, op1=op1, accum_out=ao),
                       reads=rd, writes=wr, tag=tag)

    def stt(self, out, in0, scalar, in1, op0, op1, engine="vector", tag="stt"):
        oap, oreg = out
        i0ap, i0reg = in0
        i1ap, i1reg = in1
        rd = [i0reg, i1reg]
        if isinstance(scalar, tuple):
            rd.append(scalar[1]); scalar = scalar[0]
        eng = getattr(self.nc, engine)
        return self.op(engine, lambda: eng.scalar_tensor_tensor(oap, i0ap, scalar, i1ap, op0, op1),
                       reads=rd, writes=[oreg], tag=tag)

    def copy(self, out, in_, engine="vector", tag="cp"):
        oap, oreg = out
        iap, ireg = in_
        eng = getattr(self.nc, engine)
        return self.op(engine, lambda: eng.tensor_copy(oap, iap), reads=[ireg], writes=[oreg], tag=tag)

    def memset(self, out, val, engine="vector", tag="ms"):
        oap, oreg = out
        eng = getattr(self.nc, engine)
        return self.op(engine, lambda: eng.memset(oap, val), reads=[], writes=[oreg], tag=tag)

    def reduce(self, out, in_, alu, negate=False, tag="red"):
        oap, oreg = out
        iap, ireg = in_
        return self.op("vector", lambda: self.nc.vector.tensor_reduce(oap, iap, AX.X, alu, negate=negate),
                       reads=[ireg], writes=[oreg], tag=tag)

    def max8(self, out, in_, tag="max8"):
        oap, oreg = out
        iap, ireg = in_
        return self.op("vector", lambda: self.nc.vector.max(out=oap, in_=iap), reads=[ireg], writes=[oreg], tag=tag)

    def match_replace(self, out, to_replace, values, imm, tag="mrep"):
        oap, oreg = out
        tap, treg = to_replace
        vap, vreg = values
        return self.op("vector", lambda: self.nc.vector.match_replace(out=oap, in_to_replace=tap, in_values=vap, imm_value=imm),
                       reads=[treg, vreg], writes=[oreg], tag=tag)

    def recip(self, out, in_, tag="recip"):
        oap, oreg = out
        iap, ireg = in_
        return self.op("vector", lambda: self.nc.vector.reciprocal(oap, iap), reads=[ireg], writes=[oreg], tag=tag)

    def dma(self, q, out, in_, accum=False, tag="dma"):
        oap, oreg = out
        iap, ireg = in_
        engine = {"qs": "sync", "qc": "scalar", "qg": "gpsimd"}[q]
        eng = getattr(self.nc, engine)
        kw = {}
        if accum:
            kw["accum_op"] = ALU.add
        return self.op(engine, lambda: eng.dma_start(out=oap, in_=iap, **kw),
                       reads=[ireg] + ([oreg] if accum else []), writes=[oreg], is_dma=q, tag=tag)



def make_consts():
    c = {}
    c["ident_b"] = np.eye(128, dtype=ml_dtypes.bfloat16)
    c["ident_f"] = np.eye(128, dtype=np.float32)
    c["ones_f"] = np.ones((128, 128), np.float32)
    c["e127_f"] = (np.arange(128) == 127).astype(np.float32).reshape(128, 1)
    c["utri_f"] = (np.arange(128)[:, None] <= np.arange(128)[None, :]).astype(np.float32)
    c["maskneg_f"] = np.where(np.arange(128)[None, :] >= np.arange(128)[:, None], 0.0, -1e30).astype(np.float32)
    c["posrow_f"] = np.arange(T, dtype=np.float32).reshape(1, T)
    esel = np.zeros((8, 1024), np.float32)
    for e in range(8):
        esel[e, e * 128:(e + 1) * 128] = 1.0
    c["esel8_f"] = esel
    c["iexp_f"] = (np.arange(64, dtype=np.float32) / 64.0).reshape(64, 1)
    return c


def declare_consts(nc):
    aps = {}
    for name, shape, dt in [("ident_b", [128, 128], BF16), ("ident_f", [128, 128], F32),
                            ("ones_f", [128, 128], F32), ("e127_f", [128, 1], F32),
                            ("utri_f", [128, 128], F32), ("maskneg_f", [128, 128], F32),
                            ("posrow_f", [1, T], F32), ("iexp_f", [64, 1], F32),
                            ("esel8_f", [8, 1024], F32)]:
        aps[name] = nc.dram_tensor(name, shape, dt, kind="ExternalInput").ap()
    return aps


def hilo(v):
    hi = np.float32(v)
    lo = np.float32(v - np.float64(hi))
    return float(hi), float(lo)


LN1E4_HI, LN1E4_LO = hilo(math.log(10000.0))
TWO_PI = 2 * math.pi
TWOPI_HI, TWOPI_LO = hilo(TWO_PI)
INV2PI = float(np.float32(1.0 / TWO_PI))
PI_F = float(np.float32(math.pi))
HALFPI_F = float(np.float32(math.pi / 2))
SCALE_L0 = float(np.float32(1.0 / math.sqrt(128.0)))
SCALE_L1 = 0.125



CONST_SPECS = [("ident_b", 128, 128, BF16), ("ident_f", 128, 128, F32), ("ones_f", 128, 128, F32),
               ("e127_f", 128, 1, F32), ("utri_f", 128, 128, F32), ("maskneg_f", 128, 128, F32),
               ("posrow_f", 1, T, F32), ("iexp_f", 64, 1, F32), ("esel8_f", 8, 1024, F32)]


def load_consts(pr, cap, names=None):
    cn = {}
    for name, pn, cols, dt in CONST_SPECS:
        if names is not None and name not in names:
            continue
        b = pr.alloc(name, pn, cols, dt)
        pr.dma("qs", (b.ap(), b.reg()), (cap[name], DramRes(name)), tag=f"ld_{name}")
        cn[name] = b
    return cn


def t_(buf, ps=None, cs=None):
    return (buf.ap(ps, cs), buf.reg(ps, cs))


def dram(ap_, res):
    return (ap_, res)


def ln_stats(pr, cn, xsrc_tiles, Tc, chunk, with_rms=False, tag="ln"):
    ps_s1 = pr.psum("f", 0)
    ps_s2 = pr.psum("f", 1)
    sq = pr.alloc(f"{tag}_sq", 128, Tc, F32)
    onescol = t_(cn["ones_f"], (0, 128), (0, 1))
    for dt_i in range(NDT):
        xt = xsrc_tiles[dt_i]
        pr.mm(t_(ps_s1, (0, 1), (0, Tc)), onescol, xt, start=(dt_i == 0), stop=(dt_i == NDT - 1), tag=f"{tag}_s1")
    for dt_i in range(NDT):
        xt = xsrc_tiles[dt_i]
        pr.act(t_(sq), xt, AF.Square, tag=f"{tag}_sq")
        pr.mm(t_(ps_s2, (0, 1), (0, Tc)), onescol, t_(sq), start=(dt_i == 0), stop=(dt_i == NDT - 1), tag=f"{tag}_s2")
    pr.freebuf(sq)
    r = pr.alloc(f"{tag}_rows", 1, 6 * Tc, F32)
    m = lambda: t_(r, (0, 1), (0, Tc))
    var = lambda: t_(r, (0, 1), (Tc, Tc))
    tmp = lambda: t_(r, (0, 1), (2 * Tc, Tc))
    a = lambda: t_(r, (0, 1), (3 * Tc, Tc))
    bb = lambda: t_(r, (0, 1), (4 * Tc, Tc))
    t2 = lambda: t_(r, (0, 1), (5 * Tc, Tc))
    pr.ts(m(), t_(ps_s1, (0, 1), (0, Tc)), 1.0 / D, ALU.mult, tag=f"{tag}_m")
    pr.ts(tmp(), t_(ps_s2, (0, 1), (0, Tc)), 1.0 / D, ALU.mult, tag=f"{tag}_e2")
    pr.tt(t2(), m(), m(), ALU.mult, tag=f"{tag}_m2")
    pr.tt(var(), tmp(), t2(), ALU.subtract, tag=f"{tag}_var")
    pr.ts(tmp(), var(), 1e-5, ALU.add, tag=f"{tag}_z")
    pr.act(t2(), tmp(), AF.Ln, tag=f"{tag}_ln")
    pr.act(a(), t2(), AF.Exp, scale=-0.5, tag=f"{tag}_rstd")
    if with_rms:
        pr.recip(t2(), tmp(), tag=f"{tag}_rc")
        pr.tt(tmp(), var(), t2(), ALU.mult, tag=f"{tag}_q1")
        pr.ts(tmp(), tmp(), 1e-6, ALU.add, tag=f"{tag}_q2")
        pr.act(t2(), tmp(), AF.Ln, tag=f"{tag}_ln2")
        pr.act(tmp(), t2(), AF.Exp, scale=-0.5, tag=f"{tag}_r2")
        pr.tt(a(), a(), tmp(), ALU.mult, tag=f"{tag}_a2")
    pr.stt(bb(), m(), -1.0, a(), ALU.mult, ALU.mult, tag=f"{tag}_b")
    return r, a, bb


def bc_row(pr, cn, psum_buf, row, Tc, tag="bc"):
    onesrow = t_(cn["ones_f"], (0, 1), (0, 128))
    return pr.mm(t_(psum_buf, (0, 128), (0, Tc)), onesrow, row, tag=tag)


def build_rope(pr, cn):
    cos = pr.alloc("rope_cos", 64, T, F32)
    sin = pr.alloc("rope_sin", 64, T, F32)
    tmp = pr.alloc("rope_tmp", 64, T, F32)
    inv = pr.alloc("rope_inv", 64, 2, F32)
    invT = pr.alloc("rope_invT", 1, 128, F32)
    iexp = cn["iexp_f"]
    pr.ts(t_(inv, (0, 64), (0, 1)), t_(iexp), LN1E4_HI, ALU.mult, tag="rp_t1")
    pr.stt(t_(inv, (0, 64), (1, 1)), t_(iexp), LN1E4_LO, t_(inv, (0, 64), (0, 1)), ALU.mult, ALU.add, tag="rp_t2")
    pr.act(t_(inv, (0, 64), (0, 1)), t_(inv, (0, 64), (1, 1)), AF.Exp, scale=-1.0, tag="rp_exp")
    psf = pr.psum("w", 0)
    pr.transpose(t_(psf, (0, 1), (0, 64)), t_(inv, (0, 64), (0, 1)), cn["ident_f"], tag="rp_tp")
    pr.copy(t_(invT, (0, 1), (0, 64)), t_(psf, (0, 1), (0, 64)), tag="rp_invT")
    psF = pr.psum("f", 0)
    for c in range(4):
        pr.mm(t_(psF, (0, 64), (0, 512)), t_(invT, (0, 1), (0, 64)),
              (cn["posrow_f"].ap((0, 1), (c * 512, 512)), cn["posrow_f"].reg((0, 1), (c * 512, 512))), tag="rp_f")
        kqap = pr.i32buf[0:64, 0:512]
        pr.op("vector", (lambda o=kqap, i=psF: pr.nc.vector.tensor_scalar(o, i.ap((0, 64), (0, 512)), INV2PI, None, ALU.mult)),
              reads=[psF.reg((0, 64), (0, 512))], writes=[I32RES], tag="rp_k")
        km = t_(tmp, (0, 64), (c * 512, 512))
        pr.op("vector", (lambda o=km[0], i=kqap: pr.nc.vector.tensor_copy(o, i)),
              reads=[I32RES], writes=[km[1]], tag="rp_km")
        m1 = t_(sin, (0, 64), (c * 512, 512))
        pr.stt(m1, km, -TWOPI_HI, t_(psF, (0, 64), (0, 512)), ALU.mult, ALU.add, tag="rp_m1")
        m2 = t_(cos, (0, 64), (c * 512, 512))
        pr.stt(m2, km, -TWOPI_LO, m1, ALU.mult, ALU.add, tag="rp_m2")
        gt = t_(tmp, (0, 64), (c * 512, 512))
        pr.ts(gt, m2, PI_F, ALU.is_gt, tag="rp_gt")
        pr.stt(m1, gt, -TWOPI_HI, m2, ALU.mult, ALU.add, tag="rp_f1")
        pr.stt(m2, gt, -TWOPI_LO, m1, ALU.mult, ALU.add, tag="rp_f2")
        pr.ts(gt, m2, -PI_F, ALU.is_lt, tag="rp_lt")
        pr.stt(m1, gt, TWOPI_HI, m2, ALU.mult, ALU.add, tag="rp_g1")
        pr.stt(m2, gt, TWOPI_LO, m1, ALU.mult, ALU.add, tag="rp_g2")
        pr.act(m1, m2, AF.Sin, tag="rp_sin")
        pr.act(gt, m2, AF.Abs, tag="rp_abs")
        pr.ts(gt, gt, -1.0, ALU.mult, HALFPI_F, ALU.add, tag="rp_hp")
        pr.act(m2, gt, AF.Sin, tag="rp_cos")
    pr.freebuf(tmp)
    pr.freebuf(inv)
    pr.freebuf(invT)
    return cos, sin


def load_wT(pr, cn, name, w_ap, rows, q="qg"):
    res = DramRes(name)
    nr_t = max(1, rows // 128)
    wT = pr.alloc(f"{name}_T", 128, NDT * rows, BF16)
    stage = pr.alloc(f"{name}_st", 128, 1024, BF16)
    pst = pr.psum("b", 0)
    for et in range(nr_t):
        rpart = min(128, rows)
        pr.dma(q, (stage.ap((0, rpart), (0, 1024)), stage.reg((0, rpart), (0, 1024))),
               (w_ap[et * 128:et * 128 + rpart, :], res), tag=f"ld_{name}{et}")
        for dt_i in range(NDT):
            pr.transpose(t_(pst, (0, 128), (0, rpart)), t_(stage, (0, rpart), (dt_i * 128, 128)),
                         cn["ident_b"], tag=f"tp_{name}")
            pr.copy(t_(wT, (0, 128), (dt_i * rows + et * 128, rpart)), t_(pst, (0, 128), (0, rpart)),
                    tag=f"ev_{name}")
    pr.freebuf(stage)
    return wT


def load_w_nat(pr, name, w_ap, q="qg"):
    res = DramRes(name)
    kdim, outdim = w_ap.shape
    nkt = kdim // 128
    wT = pr.alloc(f"{name}_N", 128, nkt * outdim, BF16)
    pr.dma(q, (wT.ap().rearrange("p (t j) -> p t j", t=nkt),
               wT.reg()),
           (w_ap.rearrange("(t p) j -> p t j", p=128), res), tag=f"ld_{name}")
    return wT


def load_vecT(pr, cn, name, v_ap, q="qs"):
    res = DramRes(name)
    st = pr.alloc(f"{name}_vst", 8, 128, F32)
    out = pr.alloc(f"{name}_v", 128, 8, F32)
    pr.dma(q, (st.ap((0, 8), (0, 128)), st.reg((0, 8), (0, 128))),
           (v_ap.rearrange("(t j) -> t j", t=8), res), tag=f"ld_{name}")
    psf = pr.psum("w", 0)
    pr.transpose(t_(psf, (0, 128), (0, 8)), t_(st, (0, 8), (0, 128)), cn["ident_f"], tag=f"tp_{name}")
    pr.copy(t_(out, (0, 128), (0, 8)), t_(psf, (0, 128), (0, 8)), tag=f"ev_{name}")
    pr.freebuf(st)
    return out


def load_vec_small(pr, name, v_ap, n, q="qs"):
    res = DramRes(name)
    out = pr.alloc(f"{name}_vs", n, 1, F32)
    pr.dma(q, (out.ap((0, n), (0, 1)), out.reg((0, n), (0, 1))), (v_ap.rearrange("(n o) -> n o", o=1), res), tag=f"ld_{name}")
    return out


def xcol(dt_i, c4):
    return c4 * 4096 + dt_i * 512


def build_entry_transpose(pr, cn, x_tm, x_fm):
    xstage = pr.alloc("ent_st", 128, 1024, F32)
    cstage = pr.alloc("ent_cs", 128, 4096, F32)
    ps = pr.psum("f", 0)
    xap, xres = x_tm
    oap, ores = x_fm
    for tt in range(16):
        pr.dma("qs", t_(xstage), (xap[tt * 128:(tt + 1) * 128, :], xres), tag="ent_ld")
        for dt_i in range(NDT):
            pr.transpose(t_(ps, (0, 128), (0, 128)), t_(xstage, (0, 128), (dt_i * 128, 128)),
                         cn["ident_f"], tag="ent_tp")
            pr.copy(t_(cstage, (0, 128), (dt_i * 512 + (tt % 4) * 128, 128)), t_(ps, (0, 128), (0, 128)),
                    tag="ent_ev")
        if tt % 4 == 3:
            c4 = tt // 4
            pr.dma("qs", (oap[:, c4 * 4096:(c4 + 1) * 4096], ores), t_(cstage), tag="ent_st")
    pr.freebuf(xstage)
    pr.freebuf(cstage)


def build_exit_transpose(pr, cn, x_fm, out_tm):
    cstage = pr.alloc("ex_cs", 128, 4096, F32)
    ostage = pr.alloc("ex_os", 128, 1024, F32)
    ps = pr.psum("f", 0)
    iap, ires = x_fm
    oap, ores = out_tm
    for c4 in range(4):
        pr.dma("qs", t_(cstage), (iap[:, c4 * 4096:(c4 + 1) * 4096], ires), tag="ex_ld")
        for tsub in range(4):
            tt = c4 * 4 + tsub
            for dt_i in range(NDT):
                pr.transpose(t_(ps, (0, 128), (0, 128)), t_(cstage, (0, 128), (dt_i * 512 + tsub * 128, 128)),
                             cn["ident_f"], tag="ex_tp")
                pr.copy(t_(ostage, (0, 128), (dt_i * 128, 128)), t_(ps, (0, 128), (0, 128)), tag="ex_ev")
            pr.dma("qs", (oap[tt * 128:(tt + 1) * 128, :], ores), t_(ostage), tag="ex_st")
    pr.freebuf(cstage)
    pr.freebuf(ostage)


def build_l0_attn(pr, cn, x_in, W, x_out, stage="full"):
    NH = 8
    if stage.startswith("ln"):
        cos = pr.alloc("rope_cos", 64, T, F32)
        sin = pr.alloc("rope_sin", 64, T, F32)
        pr.memset(t_(cos), 1.0, tag="rp_fake")
        pr.memset(t_(sin), 0.0, tag="rp_fake")
    else:
        cos, sin = build_rope(pr, cn)
    nwT = load_vecT(pr, cn, "l0nw", W["norm_w"][0])
    boutT = load_vecT(pr, cn, "l0bout", W["bout"][0])
    bbeta = load_vec_small(pr, "bbeta", W["bbeta"][0], 8)
    big = load_vec_small(pr, "big", W["big"][0], 8)
    bog = load_vec_small(pr, "bog", W["bog"][0], 8)
    WbT = load_wT(pr, cn, "Wbeta", W["Wbeta"][0], 8)
    WiT = load_wT(pr, cn, "Wig", W["Wig"][0], 8)
    WgT = load_wT(pr, cn, "Wog", W["Wog"][0], 8)
    h_sb = pr.alloc("h", 128, NDT * T, BF16)
    gate3 = pr.alloc("gate3", 72, T, F32)
    logf = gate3.view(0, 8)
    g_sb = gate3.view(32, 8)
    o_sb = gate3.view(64, 8)
    xc = pr.alloc("xc", 128, 4096, F32)
    bca = pr.alloc("bca", 128, 512, F32)
    bcb = pr.alloc("bcb", 128, 512, F32)
    tmpv = pr.alloc("tmp1", 128, 512, F32)
    tmp8a = pr.alloc("tmp8a", 8, 512, F32)
    tmp8b = pr.alloc("tmp8b", 8, 512, F32)
    xiap, xires = x_in
    for c4 in range(4):
        pr.dma("qs", t_(xc), (xiap[:, c4 * 4096:(c4 + 1) * 4096], xires), tag="l0a_ldx")
        tiles = [t_(xc, (0, 128), (dt_i * 512, 512)) for dt_i in range(NDT)]
        rows, a_f, b_f = ln_stats(pr, cn, tiles, 512, c4, with_rms=True, tag=f"l0ln{c4}")
        psA = pr.psum("f", 2)
        psBk = pr.psum("f", 3)
        bc_row(pr, cn, psA, a_f(), 512, tag="l0_bca")
        bc_row(pr, cn, psBk, b_f(), 512, tag="l0_bcb")
        pr.copy(t_(bca), t_(psA, (0, 128), (0, 512)), tag="l0_bcae")
        pr.copy(t_(bcb), t_(psBk, (0, 128), (0, 512)), tag="l0_bcbe")
        pr.freebuf(rows)
        for dt_i in range(NDT):
            pr.tt(t_(tmpv), tiles[dt_i], t_(bca), ALU.mult, tag="l0_h1")
            pr.tt(t_(tmpv), t_(tmpv), t_(bcb), ALU.add, tag="l0_h2")
            pr.ts((h_sb.ap((0, 128), (dt_i * T + c4 * 512, 512)), h_sb.reg((0, 128), (dt_i * T + c4 * 512, 512))),
                  t_(tmpv), (nwT.ap((0, 128), (dt_i, 1)), nwT.reg((0, 128), (dt_i, 1))), ALU.mult, tag="l0_h3")
        gsel = {"lnA": (), "lnB": (("g", WiT, big),)}
        if stage.startswith("lnC"):
            gsel[stage] = (("b", WbT, bbeta),)
        for kind, wT, bias in gsel.get(stage, (("b", WbT, bbeta), ("g", WiT, big), ("o", WgT, bog))):
            psg = pr.psum("f", 2)
            for dt_i in range(NDT):
                pr.mm(t_(psg, (0, 8), (0, 512)),
                      (wT.ap((0, 128), (dt_i * 8, 8)), wT.reg((0, 128), (dt_i * 8, 8))),
                      (h_sb.ap((0, 128), (dt_i * T + c4 * 512, 512)), h_sb.reg((0, 128), (dt_i * T + c4 * 512, 512))),
                      start=(dt_i == 0), stop=(dt_i == NDT - 1), tag=f"l0_g{kind}")
            bcol = (bias.ap((0, 8), (0, 1)), bias.reg((0, 8), (0, 1)))
            if kind == "b":
                depth = int(stage[3]) if stage.startswith("lnC") and len(stage) > 3 else 9
                pr.act(t_(tmp8a, (0, 8), (0, 512)), t_(psg, (0, 8), (0, 512)), AF.Abs, bias=bcol, tag="l0_abs")
                if depth >= 2:
                    pr.act(t_(tmp8a, (0, 8), (0, 512)), t_(tmp8a, (0, 8), (0, 512)), AF.Exp, scale=-1.0, tag="l0_e")
                if depth >= 3:
                    pr.act(t_(tmp8a, (0, 8), (0, 512)), t_(tmp8a, (0, 8), (0, 512)), AF.Ln, bias=1.0, tag="l0_l")
                if depth >= 4:
                    pr.ts(t_(tmp8b, (0, 8), (0, 512)), t_(psg, (0, 8), (0, 512)), bcol, ALU.add, 0.0, ALU.min, tag="l0_mn")
                if depth >= 5:
                    pr.tt((logf.ap((0, 8), (c4 * 512, 512)), logf.reg((0, 8), (c4 * 512, 512))),
                          t_(tmp8b, (0, 8), (0, 512)), t_(tmp8a, (0, 8), (0, 512)), ALU.subtract, tag="l0_lf")
                else:
                    pr.copy((logf.ap((0, 8), (c4 * 512, 512)), logf.reg((0, 8), (c4 * 512, 512))),
                            t_(tmp8a, (0, 8), (0, 512)), tag="l0_lfcp")
            else:
                dst = g_sb if kind == "g" else o_sb
                pr.act((dst.ap((0, 8), (c4 * 512, 512)), dst.reg((0, 8), (c4 * 512, 512))),
                       t_(psg, (0, 8), (0, 512)), AF.Sigmoid, bias=bcol, tag=f"l0_sig{kind}")
    pr.freebuf(tmp8a)
    pr.freebuf(tmp8b)
    pr.freebuf(bca)
    pr.freebuf(bcb)
    pr.freebuf(tmpv)
    pr.freebuf(xc)
    if stage.startswith("ln"):
        xoap_, xores_ = x_out
        dmp = pr.alloc("dmp", 128, 4096, F32)
        for c4 in range(4):
            for dt_i in range(NDT):
                pr.copy(t_(dmp, (0, 128), (dt_i * 512, 512)),
                        (h_sb.ap((0, 128), (dt_i * T + c4 * 512, 512)), h_sb.reg((0, 128), (dt_i * T + c4 * 512, 512))), tag="dmp")
            pr.dma("qs", (xoap_[:, c4 * 4096:(c4 + 1) * 4096], xores_), t_(dmp), tag="dmp_st")
        return
    Gt = pr.alloc("Gt", 128, 128, F32)
    Ot = pr.alloc("Ot", 128, 128, F32)
    lfT = pr.alloc("lfT", 128, 128, F32)
    psf4 = pr.psum("w", 0)
    for src, dst, scale in ((logf, lfT, None), (g_sb, Gt, SCALE_L0), (o_sb, Ot, None)):
        for c in range(16):
            pr.transpose(t_(psf4, (0, 128), (0, 8)),
                         (src.ap((0, 8), (c * 128, 128)), src.reg((0, 8), (c * 128, 128))),
                         cn["ident_f"], tag="l0_gtp")
            dstt = (dst.ap((0, 128), (c * 8, 8)), dst.reg((0, 128), (c * 8, 8)))
            if scale is None:
                pr.copy(dstt, t_(psf4, (0, 128), (0, 8)), tag="l0_gte")
            else:
                pr.ts(dstt, t_(psf4, (0, 128), (0, 8)), scale, ALU.mult, tag="l0_gte")
    pr.freebuf(gate3)
    Lc = pr.alloc("Lc", 128, 128, F32)
    negLc = pr.alloc("negLc", 128, 128, F32)
    expLc = pr.alloc("expLc", 128, 128, F32)
    Rt = pr.alloc("Rt", 128, 128, F32)
    expll = pr.alloc("expll", 128, 128, F32)
    ps0 = pr.psum("f", 0)
    pr.mm(t_(ps0, (0, 128), (0, 128)), t_(cn["utri_f"]), t_(lfT), tag="l0_cum")
    pr.copy(t_(Lc), t_(ps0, (0, 128), (0, 128)), tag="l0_Lc")
    pr.ts(t_(negLc), t_(Lc), -1.0, ALU.mult, tag="l0_negLc")
    pr.act(t_(expLc), t_(Lc), AF.Exp, tag="l0_expLc")
    ps1 = pr.psum("f", 1)
    pr.mm(t_(ps1, (0, 1), (0, 128)), t_(cn["e127_f"]), t_(Lc), tag="l0_ll")
    llrow = pr.alloc("llrow", 1, 128, F32)
    pr.copy(t_(llrow, (0, 1), (0, 128)), t_(ps1, (0, 1), (0, 128)), tag="l0_llr")
    pr.mm(t_(ps1, (0, 128), (0, 128)), (cn["ones_f"].ap((0, 1), (0, 128)), cn["ones_f"].reg((0, 1), (0, 128))),
          t_(llrow, (0, 1), (0, 128)), tag="l0_bcll")
    bcll = pr.alloc("bcll", 128, 128, F32)
    pr.copy(t_(bcll), t_(ps1, (0, 128), (0, 128)), tag="l0_bclle")
    pr.tt(t_(Rt), t_(bcll), t_(Lc), ALU.subtract, tag="l0_Rarg")
    pr.act(t_(Rt), t_(Rt), AF.Exp, tag="l0_R")
    pr.act(t_(expll), t_(bcll), AF.Exp, tag="l0_expll")
    pr.freebuf(bcll)
    pr.freebuf(llrow)
    pr.freebuf(lfT)
    y_sb = pr.alloc("y", 128, NDT * T, BF16)
    rt1 = pr.alloc("rt1", 64, 512, F32)
    rt2 = pr.alloc("rt2", 64, 512, F32)
    M32 = pr.alloc("M32", 128, 256, F32)
    Mbf = pr.alloc("Mbf", 128, 128, BF16)
    row_sb = pr.alloc("row", 1, 128, F32)
    darg = pr.alloc("darg", 128, 128, F32)
    dtile = pr.alloc("dtile", 128, 128, F32)
    Pbf = pr.alloc("Pbf", 128, 128, BF16)
    vtl = pr.alloc("vtl", 128, 128, BF16)
    ktl = pr.alloc("ktl", 128, 128, BF16)
    t128 = pr.alloc("t128", 128, 128, F32)
    yTo = pr.alloc("yTo", 128, 128, BF16)
    psSt = pr.psum("f", 0)
    psY = pr.psum("f", 1)
    psYI = pr.psum("f", 2)
    psM = pr.psum("f", 3)
    psAux = pr.psum("w", 0)
    pbV = pr.psum("b", 0)
    pbK = pr.psum("b", 0)
    pbB = pr.psum("b", 0)

    def load_w_headslice(wname, et):
        res = DramRes(wname)
        stage = pr.alloc(f"{wname}hs", 128, 1024, BF16)
        out = pr.alloc(f"{wname}hT", 128, 1024, BF16)
        pr.dma("qg", t_(stage), (W[wname][0][et * 128:(et + 1) * 128, :], res), tag=f"ldh_{wname}")
        for dt_i in range(NDT):
            pr.transpose(t_(pbB, (0, 128), (0, 128)), t_(stage, (0, 128), (dt_i * 128, 128)),
                         cn["ident_b"], tag=f"tph_{wname}")
            pr.copy(t_(out, (0, 128), (dt_i * 128, 128)), t_(pbB, (0, 128), (0, 128)), tag=f"evh_{wname}")
        pr.freebuf(stage)
        return out

    qh = pr.alloc("qh", 128, T, BF16)
    kh = pr.alloc("kh", 128, T, BF16)
    vh = pr.alloc("vh", 128, T, BF16)
    for hh in range(NH):
        for wname, dst, rope in (("Wq", qh, True), ("Wk", kh, True), ("Wv", vh, False)):
            wT = load_w_headslice(wname, hh)
            for c4 in range(4):
                psp = pr.psum("f", c4 % 2)
                for dt_i in range(NDT):
                    pr.mm(t_(psp, (0, 128), (0, 512)),
                          (wT.ap((0, 128), (dt_i * 128, 128)), wT.reg((0, 128), (dt_i * 128, 128))),
                          (h_sb.ap((0, 128), (dt_i * T + c4 * 512, 512)), h_sb.reg((0, 128), (dt_i * T + c4 * 512, 512))),
                          start=(dt_i == 0), stop=(dt_i == NDT - 1), tag=f"l0_p{wname}")
                dcol = c4 * 512
                if rope:
                    cosc = (cos.ap((0, 64), (c4 * 512, 512)), cos.reg((0, 64), (c4 * 512, 512)))
                    sinc = (sin.ap((0, 64), (c4 * 512, 512)), sin.reg((0, 64), (c4 * 512, 512)))
                    up = t_(psp, (0, 64), (0, 512))
                    lo = t_(psp, (64, 64), (0, 512))
                    pr.tt(t_(rt1, (0, 64), (0, 512)), up, cosc, ALU.mult, tag="l0_r1")
                    pr.tt(t_(rt2, (0, 64), (0, 512)), lo, sinc, ALU.mult, tag="l0_r2")
                    pr.tt((dst.ap((0, 64), (dcol, 512)), dst.reg((0, 64), (dcol, 512))),
                          t_(rt1, (0, 64), (0, 512)), t_(rt2, (0, 64), (0, 512)), ALU.subtract, tag="l0_r3")
                    pr.tt(t_(rt1, (0, 64), (0, 512)), up, sinc, ALU.mult, tag="l0_r4")
                    pr.tt(t_(rt2, (0, 64), (0, 512)), lo, cosc, ALU.mult, tag="l0_r5")
                    pr.tt((dst.ap((64, 64), (dcol, 512)), dst.reg((64, 64), (dcol, 512))),
                          t_(rt1, (0, 64), (0, 512)), t_(rt2, (0, 64), (0, 512)), ALU.add, tag="l0_r6")
                else:
                    pr.copy((dst.ap((0, 128), (dcol, 512)), dst.reg((0, 128), (dcol, 512))),
                            t_(psp, (0, 128), (0, 512)), tag="l0_vev")
            pr.freebuf(wT)
        ping, pong = 0, 128
        pr.memset(t_(M32, (0, 128), (0, 128)), 0.0, tag="l0_m0")
        pr.memset(t_(Mbf), 0.0, tag="l0_m0b")
        for c in range(16):
            hc = c * 8 + hh
            col = c * 128
            qc = (qh.ap((0, 128), (col, 128)), qh.reg((0, 128), (col, 128)))
            kc = (kh.ap((0, 128), (col, 128)), kh.reg((0, 128), (col, 128)))
            vc = (vh.ap((0, 128), (col, 128)), vh.reg((0, 128), (col, 128)))
            Gcol = (Gt.ap((0, 128), (hc, 1)), Gt.reg((0, 128), (hc, 1)))
            Ocol = (Ot.ap((0, 128), (hc, 1)), Ot.reg((0, 128), (hc, 1)))
            Rcol = (Rt.ap((0, 128), (hc, 1)), Rt.reg((0, 128), (hc, 1)))
            eLcol = (expLc.ap((0, 128), (hc, 1)), expLc.reg((0, 128), (hc, 1)))
            eLLcol = (expll.ap((0, 128), (hc, 1)), expll.reg((0, 128), (hc, 1)))
            nLcol = (negLc.ap((0, 128), (hc, 1)), negLc.reg((0, 128), (hc, 1)))
            pr.mm(t_(psSt, (0, 128), (0, 128)), kc, qc, tag="l0_st")
            pr.transpose(t_(psAux, (0, 1), (0, 128)),
                         (Lc.ap((0, 128), (hc, 1)), Lc.reg((0, 128), (hc, 1))), cn["ident_f"], tag="l0_ctp")
            pr.copy(t_(row_sb, (0, 1), (0, 128)), t_(psAux, (0, 1), (0, 128)), tag="l0_crow")
            pr.mm(t_(psAux, (0, 128), (0, 128)),
                  (cn["ones_f"].ap((0, 1), (0, 128)), cn["ones_f"].reg((0, 1), (0, 128))),
                  t_(row_sb, (0, 1), (0, 128)), tag="l0_bcLi")
            pr.stt(t_(darg), t_(psAux, (0, 128), (0, 128)), nLcol, t_(cn["maskneg_f"]), ALU.add, ALU.add, tag="l0_darg")
            pr.act(t_(dtile), t_(darg), AF.Exp, tag="l0_dexp")
            pr.tt(t_(Pbf), t_(psSt, (0, 128), (0, 128)), t_(dtile), ALU.mult, tag="l0_P")
            pr.transpose(t_(pbV, (0, 128), (0, 128)), vc, cn["ident_b"], tag="l0_vT")
            pr.ts(t_(vtl), t_(pbV, (0, 128), (0, 128)), Gcol, ALU.mult, tag="l0_vts")
            pr.transpose(t_(pbK, (0, 128), (0, 128)), kc, cn["ident_b"], tag="l0_kT")
            pr.ts(t_(ktl), t_(pbK, (0, 128), (0, 128)), Rcol, ALU.mult, tag="l0_kts")
            pr.mm(t_(psY, (0, 128), (0, 128)), t_(Pbf), t_(vtl), tag="l0_yintra")
            pr.mm(t_(psYI, (0, 128), (0, 128)), qc, t_(Mbf), tag="l0_yinter")
            pr.ts(t_(t128), t_(psYI, (0, 128), (0, 128)), eLcol, ALU.mult, Ocol, ALU.mult, tag="l0_yi2")
            pr.stt(t_(yTo), t_(psY, (0, 128), (0, 128)), Ocol, t_(t128), ALU.mult, ALU.add, tag="l0_ycmb")
            pr.transpose(t_(pbB, (0, 128), (0, 128)), t_(yTo), cn["ident_b"], tag="l0_yb")
            pr.copy((y_sb.ap((0, 128), (hh * T + col, 128)), y_sb.reg((0, 128), (hh * T + col, 128))),
                    t_(pbB, (0, 128), (0, 128)), tag="l0_yev")
            pr.mm(t_(psM, (0, 128), (0, 128)), t_(ktl), t_(vtl), tag="l0_mup")
            pr.stt(t_(M32, (0, 128), (pong, 128)), t_(M32, (0, 128), (ping, 128)), eLLcol,
                   t_(psM, (0, 128), (0, 128)), ALU.mult, ALU.add, tag="l0_mnew")
            pr.copy(t_(Mbf), t_(M32, (0, 128), (pong, 128)), tag="l0_mbf")
            ping, pong = pong, ping
    for b_ in (M32, Mbf, row_sb, darg, dtile, Pbf, vtl, ktl, t128, yTo, Gt, Ot, Lc, negLc, expLc, Rt, expll,
               qh, kh, vh, rt1, rt2, cos, sin):
        pr.freebuf(b_)
    pr.freebuf(h_sb)
    woT = load_wT(pr, cn, "Wout", W["Wout"][0], 1024)
    xc2 = pr.alloc("xc2", 128, 4096, F32)
    ost = pr.alloc("ost", 128, 1024, F32)
    xoap, xores = x_out
    for c4 in range(4):
        pr.dma("qs", t_(xc2), (xiap[:, c4 * 4096:(c4 + 1) * 4096], xires), tag="l0_ldx2")
        for et in range(NDT):
            psp = pr.psum("f", et % 2)
            for dt_i in range(NDT):
                pr.mm(t_(psp, (0, 128), (0, 512)),
                      (woT.ap((0, 128), (dt_i * 1024 + et * 128, 128)), woT.reg((0, 128), (dt_i * 1024 + et * 128, 128))),
                      (y_sb.ap((0, 128), (dt_i * T + c4 * 512, 512)), y_sb.reg((0, 128), (dt_i * T + c4 * 512, 512))),
                      start=(dt_i == 0), stop=(dt_i == NDT - 1), tag="l0_po")
            stg = (ost.ap((0, 128), ((et % 2) * 512, 512)), ost.reg((0, 128), ((et % 2) * 512, 512)))
            pr.stt(stg, t_(psp, (0, 128), (0, 512)),
                   (boutT.ap((0, 128), (et, 1)), boutT.reg((0, 128), (et, 1))),
                   t_(xc2, (0, 128), (et * 512, 512)), ALU.add, ALU.add, tag="l0_res")
            pr.dma("qs", (xoap[:, c4 * 4096 + et * 512:c4 * 4096 + (et + 1) * 512], xores), stg, tag="l0_stx")
    pr.freebuf(woT)
    pr.freebuf(xc2)
    pr.freebuf(ost)
    pr.freebuf(y_sb)
    for b_ in (nwT, boutT, bbeta, big, bog, WbT, WiT, WgT):
        pr.freebuf(b_)



from contextlib import ExitStack

L0A_WSPECS = [("norm_w", [D]), ("Wq", [D, D]), ("Wk", [D, D]), ("Wv", [D, D]),
              ("Wbeta", [8, D]), ("bbeta", [8]), ("Wig", [8, D]), ("big", [8]),
              ("Wog", [8, D]), ("bog", [8]), ("Wout", [D, D]), ("bout", [D])]


def build_launch_l0_attn(first=True, stage="full"):
    nc = bass.Bass("TRN2", target_bir_lowering=False)
    cap = declare_consts(nc)
    W = {}
    for name, shape in L0A_WSPECS:
        ap = nc.dram_tensor(f"L0_attn_{name}", shape, F32, kind="ExternalInput").ap()
        W[name] = (ap, DramRes(f"L0_attn_{name}"))
    xout = nc.dram_tensor("xout_fm", [128, 16384], F32, kind="ExternalOutput").ap()
    xout_t = (xout, DramRes("xout_fm"))
    es = ExitStack()
    pr = Prog(nc, 19968, 45056)
    pr.open(es)
    cn = load_consts(pr, cap)
    if first:
        x_tm = nc.dram_tensor("x", [T, D], F32, kind="ExternalInput").ap()
        xfm = nc.dram_tensor("xfm", [128, 16384], F32, kind="Internal").ap()
        xfm_t = (xfm, DramRes("xfm"))
        build_entry_transpose(pr, cn, (x_tm, DramRes("x")), xfm_t)
    else:
        xfm = nc.dram_tensor("xin_fm", [128, 16384], F32, kind="ExternalInput").ap()
        xfm_t = (xfm, DramRes("xin_fm"))
    if stage == "entry":
        tmp = pr.alloc("cpy", 128, 4096, F32)
        for c4 in range(4):
            pr.dma("qs", t_(tmp), (xfm[:, c4 * 4096:(c4 + 1) * 4096], xfm_t[1]), tag="cp_ld")
            pr.dma("qs", (xout[:, c4 * 4096:(c4 + 1) * 4096], xout_t[1]), t_(tmp), tag="cp_st")
    else:
        build_l0_attn(pr, cn, xfm_t, W, xout_t, stage=stage)
    blk = es.enter_context(nc.Block())
    pr.emit_all(blk)
    es.close()
    return nc


def _finish(pr, nc, es):
    blk = es.enter_context(nc.Block())
    pr.emit_all(blk)
    es.close()
    return nc


L1F_WSPECS = [("W_r", [D, D]), ("W_k", [D, D]), ("W_v", [D, D]), ("mu_r", [D]), ("mu_k", [D])]


def build_l1_ffn(pr, cn, x_in, W, x_out):
    WrT = load_wT(pr, cn, "W_r", W["W_r"][0], 1024)
    WkT = load_wT(pr, cn, "W_k", W["W_k"][0], 1024)
    WvT = load_wT(pr, cn, "W_v", W["W_v"][0], 1024)
    mur = load_vecT(pr, cn, "mu_r", W["mu_r"][0])
    muk = load_vecT(pr, cn, "mu_k", W["mu_k"][0])
    mur1 = pr.alloc("mur1", 128, 8, F32)
    muk1 = pr.alloc("muk1", 128, 8, F32)
    pr.ts(t_(mur1), t_(mur), -1.0, ALU.mult, 1.0, ALU.add, tag="f1_m1")
    pr.ts(t_(muk1), t_(muk), -1.0, ALU.mult, 1.0, ALU.add, tag="f1_m2")
    xc = pr.alloc("f1xc", 128, 4096, F32)
    tbuf = pr.alloc("f1t", 128, 4096, F32)
    prevc = pr.alloc("f1pc", 128, 8, F32)
    bca = pr.alloc("f1bca", 128, 512, F32)
    bcb = pr.alloc("f1bcb", 128, 512, F32)
    tmpv = pr.alloc("f1tmp", 128, 512, F32)
    xr = pr.alloc("f1xr", 128, 4096, BF16)
    xk = pr.alloc("f1xk", 128, 4096, BF16)
    r_sb = pr.alloc("f1r", 128, 4096, F32)
    ka = pr.alloc("f1ka", 128, 4096, BF16)
    stage = pr.alloc("f1st", 128, 1024, F32)
    pr.memset(t_(prevc), 0.0, tag="f1_pc0")
    xiap, xires = x_in
    xoap, xores = x_out
    for c4 in range(4):
        if c4 > 0:
            for dt_i in range(NDT):
                pr.copy(t_(prevc, (0, 128), (dt_i, 1)), t_(tbuf, (0, 128), (dt_i * 512 + 511, 1)), tag="f1_pcs")
        pr.dma("qs", t_(xc), (xiap[:, c4 * 4096:(c4 + 1) * 4096], xires), tag="f1_ldx")
        tiles = [t_(xc, (0, 128), (dt_i * 512, 512)) for dt_i in range(NDT)]
        rows, a_f, b_f = ln_stats(pr, cn, tiles, 512, c4, with_rms=False, tag=f"f1ln{c4}")
        psA = pr.psum("f", 2)
        psB_ = pr.psum("f", 3)
        bc_row(pr, cn, psA, a_f(), 512, tag="f1_bca")
        bc_row(pr, cn, psB_, b_f(), 512, tag="f1_bcb")
        pr.copy(t_(bca), t_(psA, (0, 128), (0, 512)), tag="f1_bcae")
        pr.copy(t_(bcb), t_(psB_, (0, 128), (0, 512)), tag="f1_bcbe")
        pr.freebuf(rows)
        for dt_i in range(NDT):
            tt_ = (tbuf.ap((0, 128), (dt_i * 512, 512)), tbuf.reg((0, 128), (dt_i * 512, 512)))
            pr.tt(tt_, tiles[dt_i], t_(bca), ALU.mult, tag="f1_t1")
            pr.tt(tt_, tt_, t_(bcb), ALU.add, tag="f1_t2")
        for dt_i in range(NDT):
            tfull = (tbuf.ap((0, 128), (dt_i * 512, 512)), tbuf.reg((0, 128), (dt_i * 512, 512)))
            tshift = (tbuf.ap((0, 128), (dt_i * 512, 511)), tbuf.reg((0, 128), (dt_i * 512, 511)))
            pcol = (prevc.ap((0, 128), (dt_i, 1)), prevc.reg((0, 128), (dt_i, 1)))
            for mu, mu1, dst in ((mur, mur1, xr), (muk, muk1, xk)):
                mcol = (mu.ap((0, 128), (dt_i, 1)), mu.reg((0, 128), (dt_i, 1)))
                m1col = (mu1.ap((0, 128), (dt_i, 1)), mu1.reg((0, 128), (dt_i, 1)))
                pr.ts(t_(tmpv, (0, 128), (1, 511)), tshift, m1col, ALU.mult, tag="f1_z1")
                pr.ts(t_(tmpv, (0, 128), (0, 1)), pcol, m1col, ALU.mult, tag="f1_z0")
                pr.stt((dst.ap((0, 128), (dt_i * 512, 512)), dst.reg((0, 128), (dt_i * 512, 512))),
                       tfull, mcol, t_(tmpv, (0, 128), (0, 512)), ALU.mult, ALU.add, tag="f1_mix")
        for ot in range(NDT):
            psp = pr.psum("f", ot % 2)
            for dt_i in range(NDT):
                pr.mm(t_(psp, (0, 128), (0, 512)),
                      (WrT.ap((0, 128), (dt_i * 1024 + ot * 128, 128)), WrT.reg((0, 128), (dt_i * 1024 + ot * 128, 128))),
                      (xr.ap((0, 128), (dt_i * 512, 512)), xr.reg((0, 128), (dt_i * 512, 512))),
                      start=(dt_i == 0), stop=(dt_i == NDT - 1), tag="f1_mr")
            pr.act((r_sb.ap((0, 128), (ot * 512, 512)), r_sb.reg((0, 128), (ot * 512, 512))),
                   t_(psp, (0, 128), (0, 512)), AF.Sigmoid, tag="f1_sig")
        for ot in range(NDT):
            psp = pr.psum("f", ot % 2)
            for dt_i in range(NDT):
                pr.mm(t_(psp, (0, 128), (0, 512)),
                      (WkT.ap((0, 128), (dt_i * 1024 + ot * 128, 128)), WkT.reg((0, 128), (dt_i * 1024 + ot * 128, 128))),
                      (xk.ap((0, 128), (dt_i * 512, 512)), xk.reg((0, 128), (dt_i * 512, 512))),
                      start=(dt_i == 0), stop=(dt_i == NDT - 1), tag="f1_mk")
            pr.act(t_(tmpv), t_(psp, (0, 128), (0, 512)), AF.Relu, tag="f1_relu")
            pr.tt((ka.ap((0, 128), (ot * 512, 512)), ka.reg((0, 128), (ot * 512, 512))),
                  t_(tmpv), t_(tmpv), ALU.mult, tag="f1_sq")
        for ot in range(NDT):
            psp = pr.psum("f", ot % 2)
            for dt_i in range(NDT):
                pr.mm(t_(psp, (0, 128), (0, 512)),
                      (WvT.ap((0, 128), (dt_i * 1024 + ot * 128, 128)), WvT.reg((0, 128), (dt_i * 1024 + ot * 128, 128))),
                      (ka.ap((0, 128), (dt_i * 512, 512)), ka.reg((0, 128), (dt_i * 512, 512))),
                      start=(dt_i == 0), stop=(dt_i == NDT - 1), tag="f1_mv")
            stg = (stage.ap((0, 128), ((ot % 2) * 512, 512)), stage.reg((0, 128), ((ot % 2) * 512, 512)))
            pr.tt(stg, t_(psp, (0, 128), (0, 512)),
                  (r_sb.ap((0, 128), (ot * 512, 512)), r_sb.reg((0, 128), (ot * 512, 512))), ALU.mult, tag="f1_rm")
            pr.tt(stg, stg, t_(xc, (0, 128), (ot * 512, 512)), ALU.add, tag="f1_res")
            pr.dma("qs", (xoap[:, c4 * 4096 + ot * 512:c4 * 4096 + (ot + 1) * 512], xores), stg, tag="f1_stx")
    for b_ in (WrT, WkT, WvT, mur, muk, mur1, muk1, xc, tbuf, prevc, bca, bcb, tmpv, xr, xk, r_sb, ka, stage):
        pr.freebuf(b_)


def build_launch_l1_ffn(last=True):
    nc = bass.Bass("TRN2", target_bir_lowering=False)
    cap = declare_consts(nc)
    W = {}
    for name, shape in L1F_WSPECS:
        ap = nc.dram_tensor(f"L1_ffn_{name}", shape, F32, kind="ExternalInput").ap()
        W[name] = (ap, DramRes(f"L1_ffn_{name}"))
    xin = nc.dram_tensor("xin_fm", [128, 16384], F32, kind="ExternalInput").ap()
    xin_t = (xin, DramRes("xin_fm"))
    es = ExitStack()
    pr = Prog(nc, 22528, 40960)
    pr.open(es)
    cn = load_consts(pr, cap)
    if last:
        xout = nc.dram_tensor("xout_fm", [128, 16384], F32, kind="Internal").ap()
        xout_t = (xout, DramRes("xout_fm"))
        out_tm = nc.dram_tensor("out", [T, D], F32, kind="ExternalOutput").ap()
        build_l1_ffn(pr, cn, xin_t, W, xout_t)
        build_exit_transpose(pr, cn, xout_t, (out_tm, DramRes("out")))
    else:
        xout = nc.dram_tensor("xout_fm", [128, 16384], F32, kind="ExternalOutput").ap()
        xout_t = (xout, DramRes("xout_fm"))
        build_l1_ffn(pr, cn, xin_t, W, xout_t)
    blk = es.enter_context(nc.Block())
    pr.emit_all(blk)
    es.close()
    return nc


L0F_WSPECS = [("router_w", [D, 8]), ("w1", [8, D, 2816]), ("v1", [8, D, 2816]),
              ("w2", [8, 2816, D]), ("bias", [D])]
FT = 22
FGROUPS = [(0, 4), (4, 4), (8, 4), (12, 4), (16, 4), (20, 2)]


def build_l0_ffn(pr, cn, x_in, W, x_out, stage="full"):
    NEXP = {"e1": 1, "e2": 2, "e1w": 1, "e1a": 1, "e1b": 1}.get(stage, 8)
    SKIP_H = stage == "e1w"
    SKIP_W2 = stage in ("e1w", "e1a")
    SKIP_ACCD = stage in ("e1w", "e1a", "e1b")
    if stage in ("min1", "min2"):
        xoap_, xores_ = x_out
        mb = pr.alloc("dbgmb", 128, 512, F32)
        pr.memset(t_(mb), 3.25, tag="dbg_m1")
        nst = 1 if stage == "min2" else 32
        for i in range(nst):
            c4, dt_i = i // 8, i % 8
            pr.dma("qs", (xoap_[:, c4 * 4096 + dt_i * 512:c4 * 4096 + (dt_i + 1) * 512], xores_),
                   t_(mb), tag="dbg_m1st")
        return
    biasT = load_vecT(pr, cn, "l0fb", W["bias"][0])
    rbuf = pr.alloc("rw", 128, 64, F32)
    pr.dma("qs", (rbuf.ap().rearrange("p (t j) -> p t j", t=8),
                  rbuf.reg()),
           (W["router_w"][0].rearrange("(t p) j -> p t j", p=128), W["router_w"][1]), tag="ld_rw")
    xiap, xires = x_in
    xoap, xores = x_out
    xc = pr.alloc("f0xc", 128, 4096, F32)
    ln32 = pr.alloc("f0ln", 128, 512, F32)
    bca = pr.alloc("f0bca", 128, 512, F32)
    bcb = pr.alloc("f0bcb", 128, 512, F32)
    xh = pr.alloc("f0xh", 128, NDT * T, BF16)
    g_em = pr.alloc("f0gem", 8, T, F32)
    tokt = pr.alloc("f0tok", 128, 8, F32)
    rsb = pr.alloc("f0rsb", 8, 512, F32)
    msk = pr.alloc("f0msk", 128, 8, F32)
    top8 = pr.alloc("f0top", 128, 8, F32)
    cola = pr.alloc("f0col", 128, 2, F32)
    psR = pr.psum("w", 0)
    if stage == "min0":
        pr.memset(t_(xc, (0, 128), (0, 512)), 3.25, tag="dbg_m0")
        for c4 in range(4):
            for dt_i in range(NDT):
                pr.dma("qs", (xoap[:, c4 * 4096 + dt_i * 512:c4 * 4096 + (dt_i + 1) * 512], xores),
                       t_(xc, (0, 128), (0, 512)), tag="dbg_m0st")
        return
    nchunk = 1 if stage == "ln1" else 4
    for c4 in range(nchunk):
        pr.dma("qs", t_(xc), (xiap[:, c4 * 4096:(c4 + 1) * 4096], xires), tag="f0_ldx")
        tiles = [t_(xc, (0, 128), (dt_i * 512, 512)) for dt_i in range(NDT)]
        rows, a_f, b_f = ln_stats(pr, cn, tiles, 512, c4, with_rms=False, tag=f"f0ln{c4}")
        psA = pr.psum("f", 2)
        psB_ = pr.psum("f", 3)
        bc_row(pr, cn, psA, a_f(), 512, tag="f0_bca")
        bc_row(pr, cn, psB_, b_f(), 512, tag="f0_bcb")
        pr.copy(t_(bca), t_(psA, (0, 128), (0, 512)), tag="f0_bcae")
        pr.copy(t_(bcb), t_(psB_, (0, 128), (0, 512)), tag="f0_bcbe")
        pr.freebuf(rows)
        for dt_i in range(NDT):
            pr.tt(t_(ln32), tiles[dt_i], t_(bca), ALU.mult, tag="f0_l1")
            pr.tt(t_(ln32), t_(ln32), t_(bcb), ALU.add, tag="f0_l2")
            pr.copy((xh.ap((0, 128), (dt_i * T + c4 * 512, 512)), xh.reg((0, 128), (dt_i * T + c4 * 512, 512))),
                    t_(ln32), tag="f0_cast")
            if stage not in ("lnx", "lnz"):
                pr.mm(t_(psR, (0, 8), (0, 512)),
                      (rbuf.ap((0, 128), (dt_i * 8, 8)), rbuf.reg((0, 128), (dt_i * 8, 8))),
                      t_(ln32), start=(dt_i == 0), stop=(dt_i == NDT - 1), tag="f0_rmm")
        if stage in ("lnx", "lnz"):
            continue
        pr.copy(t_(rsb, (0, 8), (0, 512)), t_(psR, (0, 8), (0, 512)), tag="f0_rev")
        for sub in range(4):
            pst = pr.psum("f", 2)
            pr.transpose(t_(pst, (0, 128), (0, 8)), t_(rsb, (0, 8), (sub * 128, 128)), cn["ident_f"], tag="f0_rtp")
            pr.copy(t_(tokt), t_(pst, (0, 128), (0, 8)), tag="f0_tok")
            pr.reduce((cola.ap((0, 128), (0, 1)), cola.reg((0, 128), (0, 1))), t_(tokt), ALU.max, negate=True, tag="f0_rmax")
            pr.act(t_(tokt), t_(tokt), AF.Exp,
                   bias=(cola.ap((0, 128), (0, 1)), cola.reg((0, 128), (0, 1))),
                   accum_out=(cola.ap((0, 128), (1, 1)), cola.reg((0, 128), (1, 1))), tag="f0_exp")
            pr.recip((cola.ap((0, 128), (1, 1)), cola.reg((0, 128), (1, 1))),
                     (cola.ap((0, 128), (1, 1)), cola.reg((0, 128), (1, 1))), tag="f0_rcp")
            pr.ts(t_(tokt), t_(tokt), (cola.ap((0, 128), (1, 1)), cola.reg((0, 128), (1, 1))), ALU.mult, tag="f0_nrm")
            if stage != "lnr":
                pr.max8(t_(top8), t_(tokt), tag="f0_max8")
                pr.ts(t_(msk), t_(tokt), (top8.ap((0, 128), (1, 1)), top8.reg((0, 128), (1, 1))), ALU.is_ge, tag="f0_sel")
                pr.tt(t_(tokt), t_(tokt), t_(msk), ALU.mult, tag="f0_gsel")
            pst2 = pr.psum("f", 3)
            pr.transpose(t_(pst2, (0, 8), (0, 128)), t_(tokt), cn["ident_f"], tag="f0_gtp")
            pr.copy((g_em.ap((0, 8), (c4 * 512 + sub * 128, 128)), g_em.reg((0, 8), (c4 * 512 + sub * 128, 128))),
                    t_(pst2, (0, 8), (0, 128)), tag="f0_gev")
    pr.freebuf(ln32)
    pr.freebuf(bca)
    pr.freebuf(bcb)
    pr.freebuf(tokt)
    pr.freebuf(rsb)
    pr.freebuf(msk)
    pr.freebuf(top8)
    pr.freebuf(cola)
    pr.freebuf(xc)
    moeacc = pr.nc.dram_tensor("moeacc_hbm", [128, 16384], F32, kind="Internal").ap()
    moeacc_res = DramRes("moeacc_hbm")
    accst = pr.alloc("f0accst", 128, 4096, F32)
    bcg = pr.alloc("f0bcg", 128, 4096, F32)
    gtmp = pr.alloc("f0gt", 128, 512, F32)
    ttmp = pr.alloc("f0tt", 128, 512, F32)
    hbuf = pr.alloc("f0h", 128, 4 * 512, BF16)
    wbufs = [None, None]
    w1ap, w1res = W["w1"]
    v1ap, v1res = W["v1"]
    w2ap, w2res = W["w2"]

    def load_group(e, g, slot):
        f0, nf = FGROUPS[g]
        gc = nf * 128
        if wbufs[slot] is not None:
            for b_ in wbufs[slot]:
                pr.freebuf(b_)
        w1b = pr.alloc(f"w1_{slot}", 128, NDT * gc, BF16)
        v1b = pr.alloc(f"v1_{slot}", 128, NDT * gc, BF16)
        w2b = pr.alloc(f"w2_{slot}", 128, nf * 1024, BF16)
        pr.dma("qg", (w1b.ap().rearrange("p (t j) -> p t j", t=NDT), w1b.reg()),
               (w1ap[e, :, f0 * 128:f0 * 128 + gc].rearrange("(t p) j -> p t j", p=128), w1res), tag="f0_lw1")
        pr.dma("qg", (v1b.ap().rearrange("p (t j) -> p t j", t=NDT), v1b.reg()),
               (v1ap[e, :, f0 * 128:f0 * 128 + gc].rearrange("(t p) j -> p t j", p=128), v1res), tag="f0_lv1")
        pr.dma("qg", (w2b.ap().rearrange("p (t j) -> p t j", t=nf), w2b.reg()),
               (w2ap[e, f0 * 128:f0 * 128 + gc, :].rearrange("(t p) j -> p t j", p=128), w2res), tag="f0_lw2")
        wbufs[slot] = (w1b, v1b, w2b)
        return wbufs[slot]

    ng = len(FGROUPS)
    if stage == "ln1":
        for c4 in range(4):
            for dt_i in range(NDT):
                pr.dma("qs", (xoap[:, c4 * 4096 + dt_i * 512:c4 * 4096 + (dt_i + 1) * 512], xores),
                       (xh.ap((0, 128), (0, 512)), xh.reg((0, 128), (0, 512))), tag="dbg1_st")
        return
    if stage == "lnz":
        for c4 in range(4):
            for dt_i in range(NDT):
                pr.copy(t_(bcg, (0, 128), (0, 512)),
                        (xh.ap((0, 128), (dt_i * T + c4 * 512, 512)), xh.reg((0, 128), (dt_i * T + c4 * 512, 512))), tag="dbgz")
                pr.dma("qs", (xoap[:, c4 * 4096 + dt_i * 512:c4 * 4096 + (dt_i + 1) * 512], xores),
                       t_(bcg, (0, 128), (0, 512)), tag="dbgz_st")
        for b_ in (biasT, rbuf, xh, g_em, accst, bcg, gtmp, ttmp, hbuf):
            pr.freebuf(b_)
        return
    if stage in ("ln", "lnx", "lnr"):
        for c4 in range(4):
            psG = pr.psum("w", 0)
            pr.mm(t_(psG, (0, 128), (0, 512)),
                  (cn["esel8_f"].ap((0, 8), (0, 128)), cn["esel8_f"].reg((0, 8), (0, 128))),
                  (g_em.ap((0, 8), (c4 * 512, 512)), g_em.reg((0, 8), (c4 * 512, 512))), tag="dbg_bcg")
            pr.copy(t_(bcg, (0, 128), (0, 512)), t_(psG, (0, 128), (0, 512)), tag="dbg_e")
            for dt_i in range(NDT):
                pr.dma("qs", (xoap[:, c4 * 4096 + dt_i * 512:c4 * 4096 + (dt_i + 1) * 512], xores),
                       t_(bcg, (0, 128), (0, 512)), tag="dbg_st")
        for b_ in (biasT, rbuf, xh, g_em, accst, bcg, gtmp, ttmp, hbuf):
            pr.freebuf(b_)
        return
    E = NEXP
    cur = load_group(0, 0, 0)
    for e in range(E):
        for c4 in range(4):
            psG = pr.psum("w", 0)
            pr.mm(t_(psG, (0, 128), (0, 512)),
                  (cn["esel8_f"].ap((0, 8), (e * 128, 128)), cn["esel8_f"].reg((0, 8), (e * 128, 128))),
                  (g_em.ap((0, 8), (c4 * 512, 512)), g_em.reg((0, 8), (c4 * 512, 512))), tag="f0_bcg")
            pr.copy(t_(bcg, (0, 128), (c4 * 1024, 512)), t_(psG, (0, 128), (0, 512)), tag="f0_bcge")
        for g in range(ng):
            w1b, v1b, w2b = cur
            f0, nf = FGROUPS[g]
            gc = nf * 128
            ne, ngx = (e, g + 1) if g + 1 < ng else (e + 1, 0)
            if ne < E:
                nxt = load_group(ne, ngx, 1 - (g + e * ng) % 2)
            for c4 in range(4):
                if SKIP_H:
                    continue
                ps1 = pr.psum("f", 0)
                ps2 = pr.psum("f", 1)
                for ftl in range(nf):
                    for dt_i in range(NDT):
                        pr.mm(t_(ps1, (0, 128), (0, 512)),
                              (w1b.ap((0, 128), (dt_i * gc + ftl * 128, 128)), w1b.reg((0, 128), (dt_i * gc + ftl * 128, 128))),
                              (xh.ap((0, 128), (dt_i * T + c4 * 512, 512)), xh.reg((0, 128), (dt_i * T + c4 * 512, 512))),
                              start=(dt_i == 0), stop=(dt_i == NDT - 1), tag="f0_h1")
                        pr.mm(t_(ps2, (0, 128), (0, 512)),
                              (v1b.ap((0, 128), (dt_i * gc + ftl * 128, 128)), v1b.reg((0, 128), (dt_i * gc + ftl * 128, 128))),
                              (xh.ap((0, 128), (dt_i * T + c4 * 512, 512)), xh.reg((0, 128), (dt_i * T + c4 * 512, 512))),
                              start=(dt_i == 0), stop=(dt_i == NDT - 1), tag="f0_h2")
                    pr.act(t_(gtmp), t_(ps1, (0, 128), (0, 512)), AF.Gelu, tag="f0_gelu")
                    pr.tt((hbuf.ap((0, 128), (ftl * 512, 512)), hbuf.reg((0, 128), (ftl * 512, 512))),
                          t_(gtmp), t_(ps2, (0, 128), (0, 512)), ALU.mult, tag="f0_hmul")
                if SKIP_W2:
                    continue
                for dt_i in range(NDT):
                    ps3 = pr.psum("f", 2 + dt_i % 2)
                    for ftl in range(nf):
                        pr.mm(t_(ps3, (0, 128), (0, 512)),
                              (w2b.ap((0, 128), (ftl * 1024 + dt_i * 128, 128)), w2b.reg((0, 128), (ftl * 1024 + dt_i * 128, 128))),
                              (hbuf.ap((0, 128), (ftl * 512, 512)), hbuf.reg((0, 128), (ftl * 512, 512))),
                              start=(ftl == 0), stop=(ftl == nf - 1), tag="f0_w2")
                    accsl = (accst.ap((0, 128), (dt_i * 512, 512)), accst.reg((0, 128), (dt_i * 512, 512)))
                    bcgsl = (bcg.ap((0, 128), (c4 * 1024, 512)), bcg.reg((0, 128), (c4 * 1024, 512)))
                    pr.tt(accsl, t_(ps3, (0, 128), (0, 512)), bcgsl, ALU.mult, tag="f0_og")
                if not SKIP_ACCD:
                    for hf in range(2):
                        pr.dma("qg", (moeacc[:, c4 * 4096 + hf * 2048:c4 * 4096 + (hf + 1) * 2048], moeacc_res),
                               (accst.ap((0, 128), (hf * 2048, 2048)), accst.reg((0, 128), (hf * 2048, 2048))),
                               accum=not (e == 0 and g == 0), tag="f0_accd")
            cur = nxt if ne < E else cur
    for b_ in (bcg, gtmp, ttmp, hbuf):
        pr.freebuf(b_)
    accld = accst
    for slot in wbufs:
        if slot is not None:
            for b_ in slot:
                pr.freebuf(b_)
    wbufs[0] = wbufs[1] = None
    xc2 = pr.alloc("f0xc2", 128, 4096, F32)
    stage = pr.alloc("f0st", 128, 1024, F32)
    for c4 in range(4):
        pr.dma("qs", t_(xc2), (xiap[:, c4 * 4096:(c4 + 1) * 4096], xires), tag="f0_ldx2")
        if SKIP_ACCD:
            pr.memset(t_(accld), 0.0, tag="f0_acc0m")
        else:
            pr.dma("qg", t_(accld), (moeacc[:, c4 * 4096:(c4 + 1) * 4096], moeacc_res), tag="f0_accl")
        for dt_i in range(NDT):
            stg = (stage.ap((0, 128), ((dt_i % 2) * 512, 512)), stage.reg((0, 128), ((dt_i % 2) * 512, 512)))
            pr.stt(stg, (accld.ap((0, 128), (dt_i * 512, 512)), accld.reg((0, 128), (dt_i * 512, 512))),
                   (biasT.ap((0, 128), (dt_i, 1)), biasT.reg((0, 128), (dt_i, 1))),
                   t_(xc2, (0, 128), (dt_i * 512, 512)), ALU.add, ALU.add, tag="f0_cmb")
            pr.dma("qs", (xoap[:, c4 * 4096 + dt_i * 512:c4 * 4096 + (dt_i + 1) * 512], xores), stg, tag="f0_stx")
    for b_ in (biasT, rbuf, xh, g_em, accst, xc2, stage):
        pr.freebuf(b_)


def build_launch_l0_ffn(stage="full"):
    nc = bass.Bass("TRN2", target_bir_lowering=False)
    cap = declare_consts(nc)
    W = {}
    for name, shape in L0F_WSPECS:
        ap = nc.dram_tensor(f"L0_ffn_{name}", shape, F32, kind="ExternalInput").ap()
        W[name] = (ap, DramRes(f"L0_ffn_{name}"))
    xin = nc.dram_tensor("xin_fm", [128, 16384], F32, kind="ExternalInput").ap()
    xout = nc.dram_tensor("xout_fm", [128, 16384], F32, kind="ExternalOutput").ap()
    es = ExitStack()
    pr = Prog(nc, 26624, 45056)
    pr.open(es)
    import os
    cn = load_consts(pr, cap, names=(() if os.environ.get("NOCONSTS") else None))
    build_l0_ffn(pr, cn, (xin, DramRes("xin_fm")), W, (xout, DramRes("xout_fm")), stage=stage)
    blk = es.enter_context(nc.Block())
    pr.emit_all(blk)
    es.close()
    return nc


L1A_WSPECS = [("wq_w", [D, D]), ("wq_b", [D]), ("wk_w", [D, D]), ("wk_b", [D]),
              ("wv_w", [D, D]), ("wv_b", [D]), ("wo_w", [D, D]), ("wo_b", [D]),
              ("wqi_w", [D, 64]), ("wqi_b", [64]), ("wki_w", [D, 64]), ("wki_b", [64])]
AL = 1536
NH1, DH1 = 16, 64


def build_l1_attn(pr, cn, x_in, W, x_out):
    xiap, xires = x_in
    xoap, xores = x_out
    wqi = pr.alloc("wqi", 128, 512, F32)
    wki = pr.alloc("wki", 128, 512, F32)
    for name, buf in (("wqi_w", wqi), ("wki_w", wki)):
        pr.dma("qs", (buf.ap().rearrange("p (t j) -> p t j", t=8), buf.reg()),
               (W[name][0].rearrange("(t p) j -> p t j", p=128), W[name][1]), tag=f"ld_{name}")
    qib = load_vec_small(pr, "wqi_b", W["wqi_b"][0], 64)
    kib = load_vec_small(pr, "wki_b", W["wki_b"][0], 64)
    qb = load_vecT(pr, cn, "wq_b", W["wq_b"][0])
    kb = load_vecT(pr, cn, "wk_b", W["wk_b"][0])
    vb = load_vecT(pr, cn, "wv_b", W["wv_b"][0])
    ob = load_vecT(pr, cn, "wo_b", W["wo_b"][0])
    qbs = pr.alloc("qbs", 128, 8, F32)
    pr.ts(t_(qbs), t_(qb), SCALE_L1, ALU.mult, tag="a1_qbs")
    xc = pr.alloc("a1xc", 128, 4096, F32)
    ln32 = pr.alloc("a1ln", 128, 512, F32)
    bca = pr.alloc("a1bca", 128, 512, F32)
    bcb = pr.alloc("a1bcb", 128, 512, F32)
    xh = pr.alloc("a1xh", 128, 4096, BF16)
    xhh = pr.nc.dram_tensor("a1xh_hbm", [128, NDT * T], BF16, kind="Internal").ap()
    xhh_res = DramRes("a1xh_hbm")
    qidx = pr.alloc("a1qi", 64, T, F32)
    kidx = pr.alloc("a1ki", 64, AL, F32)
    psQ = pr.psum("f", 2)
    psK = pr.psum("f", 3)
    for c4 in range(4):
        pr.dma("qs", t_(xc), (xiap[:, c4 * 4096:(c4 + 1) * 4096], xires), tag="a1_ldx")
        tiles = [t_(xc, (0, 128), (dt_i * 512, 512)) for dt_i in range(NDT)]
        rows, a_f, b_f = ln_stats(pr, cn, tiles, 512, c4, with_rms=False, tag=f"a1ln{c4}")
        psA = pr.psum("f", 0)
        psB_ = pr.psum("f", 1)
        bc_row(pr, cn, psA, a_f(), 512, tag="a1_bca")
        bc_row(pr, cn, psB_, b_f(), 512, tag="a1_bcb")
        pr.copy(t_(bca), t_(psA, (0, 128), (0, 512)), tag="a1_bcae")
        pr.copy(t_(bcb), t_(psB_, (0, 128), (0, 512)), tag="a1_bcbe")
        pr.freebuf(rows)
        for dt_i in range(NDT):
            pr.tt(t_(ln32), tiles[dt_i], t_(bca), ALU.mult, tag="a1_l1")
            pr.tt(t_(ln32), t_(ln32), t_(bcb), ALU.add, tag="a1_l2")
            pr.copy((xh.ap((0, 128), (dt_i * 512, 512)), xh.reg((0, 128), (dt_i * 512, 512))),
                    t_(ln32), tag="a1_cast")
            pr.mm(t_(psQ, (0, 64), (0, 512)),
                  (wqi.ap((0, 128), (dt_i * 64, 64)), wqi.reg((0, 128), (dt_i * 64, 64))),
                  t_(ln32), start=(dt_i == 0), stop=(dt_i == NDT - 1), tag="a1_qimm")
            if c4 < 3:
                pr.mm(t_(psK, (0, 64), (0, 512)),
                      (wki.ap((0, 128), (dt_i * 64, 64)), wki.reg((0, 128), (dt_i * 64, 64))),
                      t_(ln32), start=(dt_i == 0), stop=(dt_i == NDT - 1), tag="a1_kimm")
        pr.dma("qg", (xhh[:, c4 * 4096:(c4 + 1) * 4096], xhh_res), t_(xh), tag="a1_xhst")
        pr.ts((qidx.ap((0, 64), (c4 * 512, 512)), qidx.reg((0, 64), (c4 * 512, 512))),
              t_(psQ, (0, 64), (0, 512)), (qib.ap((0, 64), (0, 1)), qib.reg((0, 64), (0, 1))), ALU.add, tag="a1_qie")
        if c4 < 3:
            pr.ts((kidx.ap((0, 64), (c4 * 512, 512)), kidx.reg((0, 64), (c4 * 512, 512))),
                  t_(psK, (0, 64), (0, 512)), (kib.ap((0, 64), (0, 1)), kib.reg((0, 64), (0, 1))), ALU.add, tag="a1_kie")
    pr.freebuf(ln32)
    pr.freebuf(bca)
    pr.freebuf(bcb)
    pr.freebuf(xc)
    pr.freebuf(wqi)
    pr.freebuf(wki)

    def proj(wname, dst, bias_buf, scale, hbm=None):
        wn = load_w_nat(pr, wname, W[wname][0])
        for c4 in range(4):
            pr.dma("qg", t_(xh), (xhh[:, c4 * 4096:(c4 + 1) * 4096], xhh_res), tag="a1_xhld")
            for ot in range(NDT):
                psp = pr.psum("f", ot % 2)
                for dt_i in range(NDT):
                    pr.mm(t_(psp, (0, 128), (0, 512)),
                          (wn.ap((0, 128), (dt_i * 1024 + ot * 128, 128)), wn.reg((0, 128), (dt_i * 1024 + ot * 128, 128))),
                          (xh.ap((0, 128), (dt_i * 512, 512)), xh.reg((0, 128), (dt_i * 512, 512))),
                          start=(dt_i == 0), stop=(dt_i == NDT - 1), tag=f"a1_p{wname}")
                kw = dict(bias=(bias_buf.ap((0, 128), (ot, 1)), bias_buf.reg((0, 128), (ot, 1))))
                if scale is not None:
                    kw["scale"] = scale
                dsl = ((dst.ap((0, 128), (ot * 512, 512)), dst.reg((0, 128), (ot * 512, 512))) if hbm is not None
                       else (dst.ap((0, 128), (ot * T + c4 * 512, 512)), dst.reg((0, 128), (ot * T + c4 * 512, 512))))
                pr.act(dsl, t_(psp, (0, 128), (0, 512)), AF.Identity, tag=f"a1_e{wname}", **kw)
            if hbm is not None:
                pr.dma("qg", (hbm[0][:, c4 * 4096:(c4 + 1) * 4096], hbm[1]), t_(dst), tag=f"a1_{wname}st")
        pr.freebuf(wn)

    v_sb = pr.alloc("a1v", 128, NDT * T, BF16)
    proj("wv_w", v_sb, vb, None)
    vT = pr.alloc("a1vT", 128, NH1 * 16 * 65, BF16)
    pbt = pr.psum("b", 0)
    for hh in range(NH1):
        for jt in range(16):
            src_ = (v_sb.ap(((hh % 2) * 64, 64), ((hh // 2) * T + jt * 128, 128)),
                    v_sb.reg(((hh % 2) * 64, 64), ((hh // 2) * T + jt * 128, 128)))
            pr.transpose(t_(pbt, (0, 128), (0, 64)), src_, cn["ident_b"], tag="a1_vtp")
            col = (hh * 16 + jt) * 65
            pr.copy((vT.ap((0, 128), (col, 64)), vT.reg((0, 128), (col, 64))),
                    t_(pbt, (0, 128), (0, 64)), tag="a1_vte")
            pr.memset((vT.ap((0, 128), (col + 64, 1)), vT.reg((0, 128), (col + 64, 1))), 1.0, tag="a1_vt1")
    pr.freebuf(v_sb)
    qhbm = pr.nc.dram_tensor("a1q_hbm", [128, NDT * T], BF16, kind="Internal").ap()
    qhbm_res = DramRes("a1q_hbm")
    q_st = pr.alloc("a1qst", 128, 4096, BF16)
    proj("wq_w", q_st, qbs, SCALE_L1, hbm=(qhbm, qhbm_res))
    pr.freebuf(q_st)
    k_sb = pr.alloc("a1k", 128, NDT * T, BF16)
    proj("wk_w", k_sb, kb, None)
    pr.freebuf(xh)

    scrs = pr.alloc("a1scr", 128, AL, F32)
    scr2 = pr.alloc("a1scr2", 128, AL, F32)
    top8 = pr.alloc("a1top", 128, 8, F32)
    taucol = pr.alloc("a1tau", 128, 16, F32)
    for g in range(16):
        for j5 in range(3):
            psp = pr.psum("f", j5 % 2)
            pr.mm(t_(psp, (0, 128), (0, 512)),
                  (qidx.ap((0, 64), (g * 128, 128)), qidx.reg((0, 64), (g * 128, 128))),
                  (kidx.ap((0, 64), (j5 * 512, 512)), kidx.reg((0, 64), (j5 * 512, 512))), tag="a1_sq")
            pr.act((scrs.ap((0, 128), (j5 * 512, 512)), scrs.reg((0, 128), (j5 * 512, 512))),
                   t_(psp, (0, 128), (0, 512)), AF.Relu, tag="a1_srelu")
        src_ = scrs
        for rnd in range(8):
            pr.max8(t_(top8), t_(src_), tag="a1_max8")
            pr.match_replace(t_(scr2), t_(top8), t_(src_), -1.0, tag="a1_mrep")
            src_ = scr2
        pr.copy((taucol.ap((0, 128), (g, 1)), taucol.reg((0, 128), (g, 1))),
                (top8.ap((0, 128), (7, 1)), top8.reg((0, 128), (7, 1))), tag="a1_tau")
    pr.freebuf(scrs)
    pr.freebuf(scr2)
    pr.freebuf(top8)

    wo = load_w_nat(pr, "wo_w", W["wo_w"][0])
    q_g = pr.alloc("a1qg", 128, 4096, BF16)
    amask = pr.alloc("a1am", 128, 12 * 512, BF16)
    taurow = pr.alloc("a1trow", 1, 512, F32)
    bctau = pr.alloc("a1bct", 128, 512, F32)
    getmp = pr.alloc("a1ge", 128, 512, F32)
    y_g = pr.alloc("a1y", 128, NDT * 512, BF16)
    exp_sb = pr.alloc("a1ex", 128, 2 * 1024, BF16)
    bcr = pr.alloc("a1bcr", 64, 512, F32)
    recrow = pr.alloc("a1rr", 1, 512, F32, p0=64)
    xc2 = pr.alloc("a1xc2", 128, 4096, F32)
    stage = pr.alloc("a1st", 128, 1024, F32)
    psY = pr.psum("f", 0)
    psBC = pr.psum("f", 1)
    psT1 = pr.psum("f", 2)
    for g in range(4):
        pr.dma("qg", t_(q_g), (qhbm[:, g * 4096:(g + 1) * 4096], qhbm_res), tag="a1_qgld")
        for sub in range(4):
            gg = g * 4 + sub
            pr.transpose(t_(psT1, (0, 1), (0, 128)),
                         (taucol.ap((0, 128), (gg, 1)), taucol.reg((0, 128), (gg, 1))), cn["ident_f"], tag="a1_ttp")
            pr.copy((taurow.ap((0, 1), (sub * 128, 128)), taurow.reg((0, 1), (sub * 128, 128))),
                    t_(psT1, (0, 1), (0, 128)), tag="a1_tre")
        bc_row(pr, cn, psBC, (taurow.ap((0, 1), (0, 512)), taurow.reg((0, 1), (0, 512))), 512, tag="a1_bct")
        pr.copy(t_(bctau), t_(psBC, (0, 128), (0, 512)), tag="a1_bcte")
        for jt in range(12):
            psp = pr.psum("f", 2 + jt % 2)
            pr.mm(t_(psp, (0, 128), (0, 512)),
                  (kidx.ap((0, 64), (jt * 128, 128)), kidx.reg((0, 64), (jt * 128, 128))),
                  (qidx.ap((0, 64), (g * 512, 512)), qidx.reg((0, 64), (g * 512, 512))), tag="a1_st")
            pr.tt(t_(getmp), t_(psp, (0, 128), (0, 512)), t_(bctau), ALU.is_ge, tag="a1_ge")
            pr.ts((amask.ap((0, 128), (jt * 512, 512)), amask.reg((0, 128), (jt * 512, 512))),
                  t_(getmp), 1.0, ALU.subtract, 1e30, ALU.mult, tag="a1_am")
        for hh in range(NH1):
            pp, pb_ = (hh % 2) * 64, (hh // 2) * T
            qsl = (q_g.ap((pp, 64), ((hh // 2) * 512, 512)), q_g.reg((pp, 64), ((hh // 2) * 512, 512)))
            for jp in range(8):
                psw = pr.psum("w", jp % 2)
                for sub in range(2):
                    jt = jp * 2 + sub
                    ksl = (k_sb.ap((pp, 64), (pb_ + jt * 128, 128)), k_sb.reg((pp, 64), (pb_ + jt * 128, 128)))
                    pr.mm(t_(psw, (0, 128), (sub * 512, 512)), ksl, qsl, tag="a1_qk")
                    if jt < 12:
                        pr.tt(t_(psw, (0, 128), (sub * 512, 512)), t_(psw, (0, 128), (sub * 512, 512)),
                              (amask.ap((0, 128), (jt * 512, 512)), amask.reg((0, 128), (jt * 512, 512))),
                              ALU.add, tag="a1_ama")
                exsl = (exp_sb.ap((0, 128), ((jp % 2) * 1024, 1024)), exp_sb.reg((0, 128), ((jp % 2) * 1024, 1024)))
                pr.act(exsl, t_(psw, (0, 128), (0, 1024)), AF.Exp, tag="a1_exp")
                for sub in range(2):
                    jt = jp * 2 + sub
                    col = (hh * 16 + jt) * 65
                    pr.mm(t_(psY, (0, 65), (0, 512)),
                          (vT.ap((0, 128), (col, 65)), vT.reg((0, 128), (col, 65))),
                          (exp_sb.ap((0, 128), ((jp % 2) * 1024 + sub * 512, 512)),
                           exp_sb.reg((0, 128), ((jp % 2) * 1024 + sub * 512, 512))),
                          start=(jt == 0), stop=(jt == 15), tag="a1_av")
            pr.recip(t_(recrow, (0, 1), (0, 512)), t_(psY, (64, 1), (0, 512)), tag="a1_rcp")
            pr.mm(t_(psBC, (0, 64), (0, 512)),
                  (cn["ones_f"].ap((64, 1), (0, 64)), cn["ones_f"].reg((64, 1), (0, 64))),
                  t_(recrow, (0, 1), (0, 512)), tag="a1_bcr")
            pr.copy(t_(bcr, (0, 64), (0, 512)), t_(psBC, (0, 64), (0, 512)), tag="a1_bcre")
            pr.tt((y_g.ap((pp, 64), ((hh // 2) * 512, 512)), y_g.reg((pp, 64), ((hh // 2) * 512, 512))),
                  t_(psY, (0, 64), (0, 512)), t_(bcr, (0, 64), (0, 512)), ALU.mult, tag="a1_yn")
        pr.dma("qs", t_(xc2), (xiap[:, g * 4096:(g + 1) * 4096], xires), tag="a1_ldx2")
        for ot in range(NDT):
            psp = pr.psum("f", 2 + ot % 2)
            for dt_i in range(NDT):
                pr.mm(t_(psp, (0, 128), (0, 512)),
                      (wo.ap((0, 128), (dt_i * 1024 + ot * 128, 128)), wo.reg((0, 128), (dt_i * 1024 + ot * 128, 128))),
                      (y_g.ap((0, 128), (dt_i * 512, 512)), y_g.reg((0, 128), (dt_i * 512, 512))),
                      start=(dt_i == 0), stop=(dt_i == NDT - 1), tag="a1_po")
            stg = (stage.ap((0, 128), ((ot % 2) * 512, 512)), stage.reg((0, 128), ((ot % 2) * 512, 512)))
            pr.stt(stg, t_(psp, (0, 128), (0, 512)),
                   (ob.ap((0, 128), (ot, 1)), ob.reg((0, 128), (ot, 1))),
                   t_(xc2, (0, 128), (ot * 512, 512)), ALU.add, ALU.add, tag="a1_res")
            pr.dma("qs", (xoap[:, g * 4096 + ot * 512:g * 4096 + (ot + 1) * 512], xores), stg, tag="a1_stx")
    for b_ in (qidx, kidx, taucol, amask, taurow, bctau, getmp, exp_sb, y_g, bcr, recrow, wo, xc2, stage,
               q_g, k_sb, vT, qib, kib, qb, kb, vb, ob, qbs):
        pr.freebuf(b_)


def build_launch_l1_attn():
    nc = bass.Bass("TRN2", target_bir_lowering=False)
    cap = declare_consts(nc)
    W = {}
    for name, shape in L1A_WSPECS:
        ap = nc.dram_tensor(f"L1_attn_{name}", shape, F32, kind="ExternalInput").ap()
        W[name] = (ap, DramRes(f"L1_attn_{name}"))
    xin = nc.dram_tensor("xin_fm", [128, 16384], F32, kind="ExternalInput").ap()
    xout = nc.dram_tensor("xout_fm", [128, 16384], F32, kind="ExternalOutput").ap()
    es = ExitStack()
    pr = Prog(nc, 16384, 70400)
    pr.open(es)
    cn = load_consts(pr, cap, names=("ident_b", "ident_f", "ones_f"))
    build_l1_attn(pr, cn, (xin, DramRes("xin_fm")), W, (xout, DramRes("xout_fm")))
    blk = es.enter_context(nc.Block())
    pr.emit_all(blk)
    es.close()
    return nc



_NC_CACHE = {}


def _get_ncs():
    if "v1" not in _NC_CACHE:
        _NC_CACHE["v1"] = (build_launch_l0_attn(first=True), build_launch_l0_ffn(),
                           build_launch_l1_attn(), build_launch_l1_ffn(last=True))
    return _NC_CACHE["v1"]


def kernel_unfused(**inputs):
    consts = make_consts()
    x = np.ascontiguousarray(inputs["L0_attn_x"], dtype=np.float32)
    ncA, ncF0, ncA1, ncF1 = _get_ncs()
    cores = list(range(8))

    def im_for(wprefix, wspecs, extra):
        ims = []
        for b in range(8):
            d = {}
            for name, _ in wspecs:
                d[f"{wprefix}_{name}"] = inputs[f"{wprefix}_{name}"]
            d.update(consts)
            d.update(extra[b])
            ims.append(d)
        return ims

    resA = run_bass_kernel_spmd(ncA, im_for("L0_attn", L0A_WSPECS, [{"x": x[b]} for b in range(8)]), core_ids=cores)
    x1 = [resA.results[b]["xout_fm"] for b in range(8)]
    resF0 = run_bass_kernel_spmd(ncF0, im_for("L0_ffn", L0F_WSPECS, [{"xin_fm": x1[b]} for b in range(8)]), core_ids=cores)
    x2 = [resF0.results[b]["xout_fm"] for b in range(8)]
    resA1 = run_bass_kernel_spmd(ncA1, im_for("L1_attn", L1A_WSPECS, [{"xin_fm": x2[b]} for b in range(8)]), core_ids=cores)
    x3 = [resA1.results[b]["xout_fm"] for b in range(8)]
    resF1 = run_bass_kernel_spmd(ncF1, im_for("L1_ffn", L1F_WSPECS, [{"xin_fm": x3[b]} for b in range(8)]), core_ids=cores)
    out = np.stack([np.asarray(resF1.results[b]["out"]) for b in range(8)])
    return out.astype(np.float32)


def build_fused():
    nc = bass.Bass("TRN2", target_bir_lowering=False)
    cap = declare_consts(nc)
    WA, WF0, WA1, WF1 = {}, {}, {}, {}
    for pref, specs, Wd in (("L0_attn", L0A_WSPECS, WA), ("L0_ffn", L0F_WSPECS, WF0),
                            ("L1_attn", L1A_WSPECS, WA1), ("L1_ffn", L1F_WSPECS, WF1)):
        for name, shape in specs:
            ap = nc.dram_tensor(f"{pref}_{name}", shape, F32, kind="ExternalInput").ap()
            Wd[name] = (ap, DramRes(f"{pref}_{name}"))
    x_tm = nc.dram_tensor("x", [T, D], F32, kind="ExternalInput").ap()
    out_tm = nc.dram_tensor("out", [T, D], F32, kind="ExternalOutput").ap()
    scr = {}
    for nm in ("xfm0", "x1", "x2", "x3", "x4"):
        scr[nm] = (nc.dram_tensor(nm, [128, 16384], F32, kind="Internal").ap(), DramRes(nm))
    es = ExitStack()
    pr = Prog(nc, 21504, 58368)
    pr.open(es)
    cn = load_consts(pr, cap)
    build_entry_transpose(pr, cn, (x_tm, DramRes("x")), scr["xfm0"])
    build_l0_attn(pr, cn, scr["xfm0"], WA, scr["x1"])
    pr.freebuf(cn["posrow_f"])
    pr.freebuf(cn["iexp_f"])
    build_l0_ffn(pr, cn, scr["x1"], WF0, scr["x2"])
    build_l1_attn(pr, cn, scr["x2"], WA1, scr["x3"])
    build_l1_ffn(pr, cn, scr["x3"], WF1, scr["x4"])
    build_exit_transpose(pr, cn, scr["x4"], (out_tm, DramRes("out")))
    blk = es.enter_context(nc.Block())
    pr.emit_all(blk)
    es.close()
    return nc


def kernel(**inputs):
    consts = make_consts()
    x = np.ascontiguousarray(inputs["L0_attn_x"], dtype=np.float32)
    if "fused" not in _NC_CACHE:
        _NC_CACHE["fused"] = build_fused()
    nc = _NC_CACHE["fused"]
    ims = []
    for b in range(8):
        d = {"x": x[b]}
        for pref, specs in (("L0_attn", L0A_WSPECS), ("L0_ffn", L0F_WSPECS),
                            ("L1_attn", L1A_WSPECS), ("L1_ffn", L1F_WSPECS)):
            for name, _ in specs:
                d[f"{pref}_{name}"] = inputs[f"{pref}_{name}"]
        d.update(consts)
        ims.append(d)
    res = run_bass_kernel_spmd(nc, ims, core_ids=list(range(8)))
    out = np.stack([np.asarray(res.results[b]["out"]) for b in range(8)])
    return out.astype(np.float32)


# version 06
# speedup vs baseline: 1.0844x; 1.0844x over previous
import math
import numpy as np
import ml_dtypes
import concourse.bass as bass
import concourse.mybir as mybir
from concourse.bass_utils import run_bass_kernel_spmd
from concourse.alu_op_type import AluOpType as ALU

F32 = mybir.dt.float32
BF16 = mybir.dt.bfloat16
I32 = mybir.dt.int32
AF = mybir.ActivationFunctionType
AX = mybir.AxisListType

B, T, D = 8, 2048, 1024
NDT = D // 128
P = 128


ENGINES = ("tensor", "vector", "scalar", "gpsimd", "sync")


class Buf:
    __slots__ = ("name", "arena", "c0", "cols", "p0", "pn", "dtype", "prog")

    def __init__(self, prog, name, arena, c0, cols, p0, pn, dtype):
        self.prog, self.name, self.arena = prog, name, arena
        self.c0, self.cols, self.p0, self.pn, self.dtype = c0, cols, p0, pn, dtype

    def ap(self, ps=None, cs=None):
        p0 = self.p0 if ps is None else self.p0 + ps[0]
        pn = self.pn if ps is None else ps[1]
        c0 = self.c0 if cs is None else self.c0 + cs[0]
        cn = self.cols if cs is None else cs[1]
        return self.arena.t[p0:p0 + pn, c0:c0 + cn]

    def view(self, p0, pn):
        return Buf(self.prog, self.name, self.arena, self.c0, self.cols, self.p0 + p0, pn, self.dtype)

    def reg(self, ps=None, cs=None):
        p0 = self.p0 if ps is None else self.p0 + ps[0]
        pn = self.pn if ps is None else ps[1]
        c0 = self.c0 if cs is None else self.c0 + cs[0]
        cn = self.cols if cs is None else cs[1]
        return (self.arena, p0, p0 + pn, c0, c0 + cn)


class Arena:
    def __init__(self, name, t, ncols, dtype, psum=False):
        self.name, self.t, self.ncols, self.dtype, self.psum = name, t, ncols, dtype, psum
        self.free_list = [(0, ncols)]

    def alloc(self, cols, align=16):
        for i, (c0, n) in enumerate(self.free_list):
            a0 = (c0 + align - 1) // align * align
            if a0 + cols <= c0 + n:
                pre = (c0, a0 - c0)
                post = (a0 + cols, (c0 + n) - (a0 + cols))
                repl = []
                if pre[1] > 0:
                    repl.append(pre)
                if post[1] > 0:
                    repl.append(post)
                self.free_list[i:i + 1] = repl
                return a0
        raise MemoryError(f"arena {self.name}: cannot alloc {cols} cols; free={self.free_list}")

    def free(self, c0, cols):
        self.free_list.append((c0, cols))
        self.free_list.sort()
        merged = []
        for c, n in self.free_list:
            if merged and merged[-1][0] + merged[-1][1] == c:
                merged[-1] = (merged[-1][0], merged[-1][1] + n)
            else:
                merged.append((c, n))
        self.free_list = [tuple(x) for x in merged]


class Node:
    __slots__ = ("idx", "engine", "emit", "reads", "writes", "sem_val", "is_dma",
                 "waits", "needs_inc", "tag")

    def __init__(self, idx, engine, emit, reads, writes, is_dma, tag):
        self.idx, self.engine, self.emit = idx, engine, emit
        self.reads, self.writes, self.is_dma = reads, writes, is_dma
        self.sem_val = None
        self.waits = {}
        self.needs_inc = False
        self.tag = tag


class DramRes:

    def __init__(self, name):
        self.name = name


I32RES = None


I32RES = DramRes("i32scratch")


class Prog:
    def __init__(self, nc, f32_cols, bf16_cols):
        self.nc = nc
        self.nodes = []
        self.ctx = []
        self.a32 = None
        self.a16 = None
        self._a32_cols = f32_cols
        self._b16_cols = bf16_cols
        self.hist = {}
        self.psum_banks = []
        self._sems = {}
        self._closed = False

    def open(self, es):
        nc = self.nc
        t32 = es.enter_context(nc.sbuf_tensor("arena32", [128, self._a32_cols], F32))
        t16 = es.enter_context(nc.sbuf_tensor("arena16", [128, self._b16_cols], BF16))
        self.a32 = Arena("a32", t32, self._a32_cols, F32)
        self.a16 = Arena("a16", t16, self._b16_cols, BF16)
        self.psB = []
        for i in range(3):
            t = es.enter_context(nc.psum_tensor(f"psf{i}", [128, 512], F32))
            a = Arena(f"psf{i}", t, 512, F32, psum=True)
            self.psB.append(a)
        self.psW = []
        for i in range(2):
            t = es.enter_context(nc.psum_tensor(f"psw{i}", [128, 1024], F32))
            self.psW.append(Arena(f"psw{i}", t, 1024, F32, psum=True))
        self.psT = []
        for i in range(1):
            t = es.enter_context(nc.psum_tensor(f"psb{i}", [128, 1024], BF16))
            a = Arena(f"psb{i}", t, 1024, BF16, psum=True)
            self.psT.append(a)
        for e in ENGINES:
            self._sems[e] = es.enter_context(nc.semaphore(f"s_{e}"))
        self._lanes = {}
        self._lane_rr = {}
        self._lane_prev = {}
        for q in ("qs", "qc", "qg"):
            self._lanes[q] = []
            self._lane_rr[q] = 0
            for i in range(3):
                ln = f"{q}{i}"
                self._sems[ln] = es.enter_context(nc.semaphore(ln))
                self._lanes[q].append(ln)
                self._lane_prev[ln] = None
        self.i32buf = es.enter_context(nc.sbuf_tensor("i32buf", [64, 512], I32))

    def alloc(self, name, pn, cols, dtype, p0=0):
        ar = self.a32 if dtype in (F32, I32) else self.a16
        c0 = ar.alloc(cols)
        return Buf(self, name, ar, c0, cols, p0, pn, dtype)

    def freebuf(self, buf):
        buf.arena.free(buf.c0, buf.cols)

    def psum(self, which, i):
        if which == "f" and i == 3:
            ar = self.psW[1]
        else:
            ar = {"f": self.psB, "b": self.psT, "w": self.psW}[which][i]
        return Buf(self, f"{ar.name}", ar, 0, ar.ncols, 0, 128, ar.dtype)

    @staticmethod
    def _overlap(r1, r2):
        if r1[0] is not r2[0]:
            return False
        if isinstance(r1[0], DramRes):
            return True
        return not (r1[2] <= r2[1] or r2[2] <= r1[1] or r1[4] <= r2[3] or r2[4] <= r1[3])

    def _expand(self, regs):
        out = []
        for r in regs:
            if isinstance(r, Buf):
                r = r.reg()
            if isinstance(r, DramRes):
                r = (r,)
            ar = r[0]
            if isinstance(ar, Arena) and ar.psum:
                r = (ar, 0, 128, 0, ar.ncols)
            out.append(r)
        return out

    def op(self, engine, emit, reads=(), writes=(), is_dma=False, tag=""):
        reads = self._expand(reads)
        writes = self._expand(writes)
        if is_dma:
            q = is_dma
            lane = self._lanes[q][self._lane_rr[q] % len(self._lanes[q])]
            self._lane_rr[q] += 1
            is_dma = lane
        nd = Node(len(self.nodes), engine, emit, reads, writes, is_dma, tag)
        self.nodes.append(nd)
        if is_dma:
            prev = self._lane_prev[is_dma]
            if prev is not None:
                prev.needs_inc = True
                nd.waits[is_dma] = prev
            self._lane_prev[is_dma] = nd
        deps = set()
        for r in reads:
            h = self.hist.setdefault(r[0], {"w": [], "r": []})
            for n2, r2 in h["w"]:
                if self._overlap(r, r2):
                    deps.add(n2)
            if isinstance(r[0], Arena) and r[0].psum:
                for n2, r2 in h["r"]:
                    if n2.engine != engine and self._overlap(r, r2):
                        deps.add(n2)
        for w in writes:
            h = self.hist.setdefault(w[0], {"w": [], "r": []})
            for n2, r2 in h["w"]:
                if self._overlap(w, r2):
                    deps.add(n2)
            for n2, r2 in h["r"]:
                if self._overlap(w, r2):
                    deps.add(n2)
        for r in reads:
            self.hist[r[0]]["r"].append((nd, r))
        for w in writes:
            h = self.hist[w[0]]
            h["w"] = [e for e in h["w"] if not self._covers(w, e[1])]
            h["r"] = [e for e in h["r"] if not self._covers(w, e[1])]
            h["w"].append((nd, w))
        for n2 in deps:
            if n2.engine == "tensor" and engine == "tensor" and not n2.is_dma and not is_dma:
                continue
            n2.needs_inc = True
            key = n2.engine if not n2.is_dma else n2.is_dma
            cur = nd.waits.get(key)
            if cur is None or n2.idx > cur.idx:
                nd.waits[key] = n2
        return nd

    @staticmethod
    def _covers(w, r):
        if w[0] is not r[0]:
            return False
        if isinstance(w[0], DramRes):
            return True
        return w[1] <= r[1] and w[2] >= r[2] and w[3] <= r[3] and w[4] >= r[4]

    def emit_all(self, block, final_waits=()):
        counters = {}
        for nd in self.nodes:
            key = nd.engine if not nd.is_dma else nd.is_dma
            if nd.is_dma:
                nd.needs_inc = True
            if nd.needs_inc:
                counters[key] = counters.get(key, 0) + (16 if nd.is_dma else 1)
                nd.sem_val = counters[key]
        by_engine = {e: [] for e in ENGINES}
        for nd in self.nodes:
            by_engine[nd.engine].append(nd)
        sems = self._sems
        hw = {}

        def emit_stream(eng_obj, engine_name, nodes):
            for nd in nodes:
                for key, n2 in nd.waits.items():
                    v = n2.sem_val
                    if v is None:
                        continue
                    if hw.get((engine_name, key), 0) >= v:
                        continue
                    eng_obj.wait_ge(sems[key], v)
                    hw[(engine_name, key)] = v
                inst = nd.emit()
                if nd.needs_inc and inst is not None:
                    key = nd.engine if not nd.is_dma else nd.is_dma
                    inst.then_inc(sems[key], 16 if nd.is_dma else 1)
            for key, v in (final_waits if engine_name == "sync" else ()):
                eng_obj.wait_ge(sems[key], v)

        if not final_waits:
            final_waits = [(k, v) for k, v in counters.items()
                           if k[:2] in ("qs", "qc", "qg") and len(k) == 3]
        for name in ENGINES:
            sec = getattr(block, name)
            sec(lambda eng, _n=by_engine[name], _e=name: emit_stream(eng, _e, _n))

    def dma_count(self, q):
        c = 0
        for nd in self.nodes:
            if nd.is_dma == q:
                c += 16
        return c

    def mm(self, out, lhsT, rhs, start=True, stop=True, tag="mm"):
        oap, oreg = out if isinstance(out, tuple) else (out.ap(), out.reg())
        lap, lreg = lhsT if isinstance(lhsT, tuple) else (lhsT.ap(), lhsT.reg())
        rap, rreg = rhs if isinstance(rhs, tuple) else (rhs.ap(), rhs.reg())
        rw = [oreg] if start else [oreg]
        rd = [lreg, rreg] + ([] if start else [oreg])
        return self.op("tensor", lambda: self.nc.tensor.matmul(oap, lap, rap, start=start, stop=stop),
                       reads=rd, writes=rw, tag=tag)

    def transpose(self, out, in_, ident, tag="tp"):
        oap, oreg = out
        iap, ireg = in_
        if isinstance(ident, Buf):
            pn = ireg[2] - ireg[1]
            ib = ireg[1]
            idap, idreg = ident.ap((ib, pn), (ib, pn)), ident.reg((ib, pn), (ib, pn))
        else:
            idap, idreg = ident
            pn = ireg[2] - ireg[1]
            if idreg[2] - idreg[1] != pn:
                idap = idap
        return self.op("tensor", lambda: self.nc.tensor.transpose(oap, iap, idap),
                       reads=[ireg, idreg], writes=[oreg], tag=tag)

    def act(self, out, in_, func, bias=None, scale=None, accum_out=None, engine="scalar", tag="act"):
        oap, oreg = out
        iap, ireg = in_
        kw = {}
        rd = [ireg]
        wr = [oreg]
        if bias is not None:
            if isinstance(bias, tuple):
                kw["bias"] = bias[0]
                rd.append(bias[1])
            else:
                kw["bias"] = bias
        if scale is not None:
            if isinstance(scale, tuple):
                kw["scale"] = scale[0]
                rd.append(scale[1])
            else:
                kw["scale"] = scale
        if accum_out is not None:
            kw["accum_out"] = accum_out[0]
            wr.append(accum_out[1])
        return self.op(engine, lambda: self.nc.scalar.activation(oap, iap, func, **kw),
                       reads=rd, writes=wr, tag=tag)

    def tt(self, out, a, b, alu, engine="vector", tag="tt"):
        oap, oreg = out
        aap, areg = a
        bap, breg = b
        eng = getattr(self.nc, engine)
        return self.op(engine, lambda: eng.tensor_tensor(oap, aap, bap, op=alu),
                       reads=[areg, breg], writes=[oreg], tag=tag)

    def ts(self, out, in_, s1, op0, s2=None, op1=ALU.bypass, accum_out=None, engine="vector", tag="ts"):
        oap, oreg = out
        iap, ireg = in_
        rd = [ireg]
        wr = [oreg]
        if isinstance(s1, tuple):
            rd.append(s1[1]); s1 = s1[0]
        if isinstance(s2, tuple):
            rd.append(s2[1]); s2 = s2[0]
        ao = None
        if accum_out is not None:
            ao = accum_out[0]
            wr.append(accum_out[1])
        eng = getattr(self.nc, engine)
        return self.op(engine, lambda: eng.tensor_scalar(oap, iap, s1, s2, op0, op1=op1, accum_out=ao),
                       reads=rd, writes=wr, tag=tag)

    def stt(self, out, in0, scalar, in1, op0, op1, engine="vector", tag="stt"):
        oap, oreg = out
        i0ap, i0reg = in0
        i1ap, i1reg = in1
        rd = [i0reg, i1reg]
        if isinstance(scalar, tuple):
            rd.append(scalar[1]); scalar = scalar[0]
        eng = getattr(self.nc, engine)
        return self.op(engine, lambda: eng.scalar_tensor_tensor(oap, i0ap, scalar, i1ap, op0, op1),
                       reads=rd, writes=[oreg], tag=tag)

    def copy(self, out, in_, engine="vector", tag="cp"):
        oap, oreg = out
        iap, ireg = in_
        eng = getattr(self.nc, engine)
        return self.op(engine, lambda: eng.tensor_copy(oap, iap), reads=[ireg], writes=[oreg], tag=tag)

    def memset(self, out, val, engine="vector", tag="ms"):
        oap, oreg = out
        eng = getattr(self.nc, engine)
        return self.op(engine, lambda: eng.memset(oap, val), reads=[], writes=[oreg], tag=tag)

    def reduce(self, out, in_, alu, negate=False, tag="red"):
        oap, oreg = out
        iap, ireg = in_
        return self.op("vector", lambda: self.nc.vector.tensor_reduce(oap, iap, AX.X, alu, negate=negate),
                       reads=[ireg], writes=[oreg], tag=tag)

    def max8(self, out, in_, tag="max8"):
        oap, oreg = out
        iap, ireg = in_
        return self.op("vector", lambda: self.nc.vector.max(out=oap, in_=iap), reads=[ireg], writes=[oreg], tag=tag)

    def match_replace(self, out, to_replace, values, imm, tag="mrep"):
        oap, oreg = out
        tap, treg = to_replace
        vap, vreg = values
        return self.op("vector", lambda: self.nc.vector.match_replace(out=oap, in_to_replace=tap, in_values=vap, imm_value=imm),
                       reads=[treg, vreg], writes=[oreg], tag=tag)

    def recip(self, out, in_, tag="recip"):
        oap, oreg = out
        iap, ireg = in_
        return self.op("vector", lambda: self.nc.vector.reciprocal(oap, iap), reads=[ireg], writes=[oreg], tag=tag)

    def dma(self, q, out, in_, accum=False, tag="dma"):
        oap, oreg = out
        iap, ireg = in_
        engine = {"qs": "sync", "qc": "scalar", "qg": "gpsimd"}[q]
        eng = getattr(self.nc, engine)
        kw = {}
        if accum:
            kw["accum_op"] = ALU.add
        return self.op(engine, lambda: eng.dma_start(out=oap, in_=iap, **kw),
                       reads=[ireg] + ([oreg] if accum else []), writes=[oreg], is_dma=q, tag=tag)



def make_consts():
    c = {}
    c["ident_b"] = np.eye(128, dtype=ml_dtypes.bfloat16)
    c["ident_f"] = np.eye(128, dtype=np.float32)
    c["ones_f"] = np.ones((128, 128), np.float32)
    c["e127_f"] = (np.arange(128) == 127).astype(np.float32).reshape(128, 1)
    c["utri_f"] = (np.arange(128)[:, None] <= np.arange(128)[None, :]).astype(np.float32)
    c["maskneg_f"] = np.where(np.arange(128)[None, :] >= np.arange(128)[:, None], 0.0, -1e30).astype(np.float32)
    c["posrow_f"] = np.arange(T, dtype=np.float32).reshape(1, T)
    esel = np.zeros((8, 1024), np.float32)
    for e in range(8):
        esel[e, e * 128:(e + 1) * 128] = 1.0
    c["esel8_f"] = esel
    c["iexp_f"] = (np.arange(64, dtype=np.float32) / 64.0).reshape(64, 1)
    return c


def declare_consts(nc):
    aps = {}
    for name, shape, dt in [("ident_b", [128, 128], BF16), ("ident_f", [128, 128], F32),
                            ("ones_f", [128, 128], F32), ("e127_f", [128, 1], F32),
                            ("utri_f", [128, 128], F32), ("maskneg_f", [128, 128], F32),
                            ("posrow_f", [1, T], F32), ("iexp_f", [64, 1], F32),
                            ("esel8_f", [8, 1024], F32)]:
        aps[name] = nc.dram_tensor(name, shape, dt, kind="ExternalInput").ap()
    return aps


def hilo(v):
    hi = np.float32(v)
    lo = np.float32(v - np.float64(hi))
    return float(hi), float(lo)


LN1E4_HI, LN1E4_LO = hilo(math.log(10000.0))
TWO_PI = 2 * math.pi
TWOPI_HI, TWOPI_LO = hilo(TWO_PI)
INV2PI = float(np.float32(1.0 / TWO_PI))
PI_F = float(np.float32(math.pi))
HALFPI_F = float(np.float32(math.pi / 2))
SCALE_L0 = float(np.float32(1.0 / math.sqrt(128.0)))
SCALE_L1 = 0.125



CONST_SPECS = [("ident_b", 128, 128, BF16), ("ident_f", 128, 128, F32), ("ones_f", 128, 128, F32),
               ("e127_f", 128, 1, F32), ("utri_f", 128, 128, F32), ("maskneg_f", 128, 128, F32),
               ("posrow_f", 1, T, F32), ("iexp_f", 64, 1, F32), ("esel8_f", 8, 1024, F32)]


def load_consts(pr, cap, names=None):
    cn = {}
    for name, pn, cols, dt in CONST_SPECS:
        if names is not None and name not in names:
            continue
        b = pr.alloc(name, pn, cols, dt)
        pr.dma("qs", (b.ap(), b.reg()), (cap[name], DramRes(name)), tag=f"ld_{name}")
        cn[name] = b
    return cn


def t_(buf, ps=None, cs=None):
    return (buf.ap(ps, cs), buf.reg(ps, cs))


def dram(ap_, res):
    return (ap_, res)


def ln_stats(pr, cn, xsrc_tiles, Tc, chunk, with_rms=False, tag="ln"):
    ps_s1 = pr.psum("f", 0)
    ps_s2 = pr.psum("f", 1)
    sq = pr.alloc(f"{tag}_sq", 128, Tc, F32)
    onescol = t_(cn["ones_f"], (0, 128), (0, 1))
    for dt_i in range(NDT):
        xt = xsrc_tiles[dt_i]
        pr.mm(t_(ps_s1, (0, 1), (0, Tc)), onescol, xt, start=(dt_i == 0), stop=(dt_i == NDT - 1), tag=f"{tag}_s1")
    for dt_i in range(NDT):
        xt = xsrc_tiles[dt_i]
        pr.act(t_(sq), xt, AF.Square, tag=f"{tag}_sq")
        pr.mm(t_(ps_s2, (0, 1), (0, Tc)), onescol, t_(sq), start=(dt_i == 0), stop=(dt_i == NDT - 1), tag=f"{tag}_s2")
    pr.freebuf(sq)
    r = pr.alloc(f"{tag}_rows", 1, 6 * Tc, F32)
    m = lambda: t_(r, (0, 1), (0, Tc))
    var = lambda: t_(r, (0, 1), (Tc, Tc))
    tmp = lambda: t_(r, (0, 1), (2 * Tc, Tc))
    a = lambda: t_(r, (0, 1), (3 * Tc, Tc))
    bb = lambda: t_(r, (0, 1), (4 * Tc, Tc))
    t2 = lambda: t_(r, (0, 1), (5 * Tc, Tc))
    pr.ts(m(), t_(ps_s1, (0, 1), (0, Tc)), 1.0 / D, ALU.mult, tag=f"{tag}_m")
    pr.ts(tmp(), t_(ps_s2, (0, 1), (0, Tc)), 1.0 / D, ALU.mult, tag=f"{tag}_e2")
    pr.tt(t2(), m(), m(), ALU.mult, tag=f"{tag}_m2")
    pr.tt(var(), tmp(), t2(), ALU.subtract, tag=f"{tag}_var")
    pr.ts(tmp(), var(), 1e-5, ALU.add, tag=f"{tag}_z")
    pr.act(t2(), tmp(), AF.Ln, tag=f"{tag}_ln")
    pr.act(a(), t2(), AF.Exp, scale=-0.5, tag=f"{tag}_rstd")
    if with_rms:
        pr.recip(t2(), tmp(), tag=f"{tag}_rc")
        pr.tt(tmp(), var(), t2(), ALU.mult, tag=f"{tag}_q1")
        pr.ts(tmp(), tmp(), 1e-6, ALU.add, tag=f"{tag}_q2")
        pr.act(t2(), tmp(), AF.Ln, tag=f"{tag}_ln2")
        pr.act(tmp(), t2(), AF.Exp, scale=-0.5, tag=f"{tag}_r2")
        pr.tt(a(), a(), tmp(), ALU.mult, tag=f"{tag}_a2")
    pr.stt(bb(), m(), -1.0, a(), ALU.mult, ALU.mult, tag=f"{tag}_b")
    return r, a, bb


def bc_row(pr, cn, psum_buf, row, Tc, tag="bc"):
    onesrow = t_(cn["ones_f"], (0, 1), (0, 128))
    return pr.mm(t_(psum_buf, (0, 128), (0, Tc)), onesrow, row, tag=tag)


def build_rope(pr, cn):
    cos = pr.alloc("rope_cos", 64, T, F32)
    sin = pr.alloc("rope_sin", 64, T, F32)
    tmp = pr.alloc("rope_tmp", 64, T, F32)
    inv = pr.alloc("rope_inv", 64, 2, F32)
    invT = pr.alloc("rope_invT", 1, 128, F32)
    iexp = cn["iexp_f"]
    pr.ts(t_(inv, (0, 64), (0, 1)), t_(iexp), LN1E4_HI, ALU.mult, tag="rp_t1")
    pr.stt(t_(inv, (0, 64), (1, 1)), t_(iexp), LN1E4_LO, t_(inv, (0, 64), (0, 1)), ALU.mult, ALU.add, tag="rp_t2")
    pr.act(t_(inv, (0, 64), (0, 1)), t_(inv, (0, 64), (1, 1)), AF.Exp, scale=-1.0, tag="rp_exp")
    psf = pr.psum("w", 0)
    pr.transpose(t_(psf, (0, 1), (0, 64)), t_(inv, (0, 64), (0, 1)), cn["ident_f"], tag="rp_tp")
    pr.copy(t_(invT, (0, 1), (0, 64)), t_(psf, (0, 1), (0, 64)), tag="rp_invT")
    psF = pr.psum("f", 0)
    for c in range(4):
        pr.mm(t_(psF, (0, 64), (0, 512)), t_(invT, (0, 1), (0, 64)),
              (cn["posrow_f"].ap((0, 1), (c * 512, 512)), cn["posrow_f"].reg((0, 1), (c * 512, 512))), tag="rp_f")
        kqap = pr.i32buf[0:64, 0:512]
        pr.op("vector", (lambda o=kqap, i=psF: pr.nc.vector.tensor_scalar(o, i.ap((0, 64), (0, 512)), INV2PI, None, ALU.mult)),
              reads=[psF.reg((0, 64), (0, 512))], writes=[I32RES], tag="rp_k")
        km = t_(tmp, (0, 64), (c * 512, 512))
        pr.op("vector", (lambda o=km[0], i=kqap: pr.nc.vector.tensor_copy(o, i)),
              reads=[I32RES], writes=[km[1]], tag="rp_km")
        m1 = t_(sin, (0, 64), (c * 512, 512))
        pr.stt(m1, km, -TWOPI_HI, t_(psF, (0, 64), (0, 512)), ALU.mult, ALU.add, tag="rp_m1")
        m2 = t_(cos, (0, 64), (c * 512, 512))
        pr.stt(m2, km, -TWOPI_LO, m1, ALU.mult, ALU.add, tag="rp_m2")
        gt = t_(tmp, (0, 64), (c * 512, 512))
        pr.ts(gt, m2, PI_F, ALU.is_gt, tag="rp_gt")
        pr.stt(m1, gt, -TWOPI_HI, m2, ALU.mult, ALU.add, tag="rp_f1")
        pr.stt(m2, gt, -TWOPI_LO, m1, ALU.mult, ALU.add, tag="rp_f2")
        pr.ts(gt, m2, -PI_F, ALU.is_lt, tag="rp_lt")
        pr.stt(m1, gt, TWOPI_HI, m2, ALU.mult, ALU.add, tag="rp_g1")
        pr.stt(m2, gt, TWOPI_LO, m1, ALU.mult, ALU.add, tag="rp_g2")
        pr.act(m1, m2, AF.Sin, tag="rp_sin")
        pr.act(gt, m2, AF.Abs, tag="rp_abs")
        pr.ts(gt, gt, -1.0, ALU.mult, HALFPI_F, ALU.add, tag="rp_hp")
        pr.act(m2, gt, AF.Sin, tag="rp_cos")
    pr.freebuf(tmp)
    pr.freebuf(inv)
    pr.freebuf(invT)
    return cos, sin


def load_wT(pr, cn, name, w_ap, rows, q="qg"):
    res = DramRes(name)
    nr_t = max(1, rows // 128)
    wT = pr.alloc(f"{name}_T", 128, NDT * rows, BF16)
    stage = pr.alloc(f"{name}_st", 128, 1024, BF16)
    pst = pr.psum("b", 0)
    for et in range(nr_t):
        rpart = min(128, rows)
        pr.dma(q, (stage.ap((0, rpart), (0, 1024)), stage.reg((0, rpart), (0, 1024))),
               (w_ap[et * 128:et * 128 + rpart, :], res), tag=f"ld_{name}{et}")
        for dt_i in range(NDT):
            pr.transpose(t_(pst, (0, 128), (0, rpart)), t_(stage, (0, rpart), (dt_i * 128, 128)),
                         cn["ident_b"], tag=f"tp_{name}")
            pr.copy(t_(wT, (0, 128), (dt_i * rows + et * 128, rpart)), t_(pst, (0, 128), (0, rpart)),
                    tag=f"ev_{name}")
    pr.freebuf(stage)
    return wT


def load_w_nat(pr, name, w_ap, q="qg"):
    res = DramRes(name)
    kdim, outdim = w_ap.shape
    nkt = kdim // 128
    wT = pr.alloc(f"{name}_N", 128, nkt * outdim, BF16)
    pr.dma(q, (wT.ap().rearrange("p (t j) -> p t j", t=nkt),
               wT.reg()),
           (w_ap.rearrange("(t p) j -> p t j", p=128), res), tag=f"ld_{name}")
    return wT


def load_vecT(pr, cn, name, v_ap, q="qs"):
    res = DramRes(name)
    st = pr.alloc(f"{name}_vst", 8, 128, F32)
    out = pr.alloc(f"{name}_v", 128, 8, F32)
    pr.dma(q, (st.ap((0, 8), (0, 128)), st.reg((0, 8), (0, 128))),
           (v_ap.rearrange("(t j) -> t j", t=8), res), tag=f"ld_{name}")
    psf = pr.psum("w", 0)
    pr.transpose(t_(psf, (0, 128), (0, 8)), t_(st, (0, 8), (0, 128)), cn["ident_f"], tag=f"tp_{name}")
    pr.copy(t_(out, (0, 128), (0, 8)), t_(psf, (0, 128), (0, 8)), tag=f"ev_{name}")
    pr.freebuf(st)
    return out


def load_vec_small(pr, name, v_ap, n, q="qs"):
    res = DramRes(name)
    out = pr.alloc(f"{name}_vs", n, 1, F32)
    pr.dma(q, (out.ap((0, n), (0, 1)), out.reg((0, n), (0, 1))), (v_ap.rearrange("(n o) -> n o", o=1), res), tag=f"ld_{name}")
    return out


def xcol(dt_i, c4):
    return c4 * 4096 + dt_i * 512


def build_entry_transpose(pr, cn, x_tm, x_fm):
    xstage = pr.alloc("ent_st", 128, 1024, F32)
    cstage = pr.alloc("ent_cs", 128, 4096, F32)
    ps = pr.psum("f", 0)
    xap, xres = x_tm
    oap, ores = x_fm
    for tt in range(16):
        pr.dma("qs", t_(xstage), (xap[tt * 128:(tt + 1) * 128, :], xres), tag="ent_ld")
        for dt_i in range(NDT):
            pr.transpose(t_(ps, (0, 128), (0, 128)), t_(xstage, (0, 128), (dt_i * 128, 128)),
                         cn["ident_f"], tag="ent_tp")
            pr.copy(t_(cstage, (0, 128), (dt_i * 512 + (tt % 4) * 128, 128)), t_(ps, (0, 128), (0, 128)),
                    tag="ent_ev")
        if tt % 4 == 3:
            c4 = tt // 4
            pr.dma("qs", (oap[:, c4 * 4096:(c4 + 1) * 4096], ores), t_(cstage), tag="ent_st")
    pr.freebuf(xstage)
    pr.freebuf(cstage)


def build_exit_transpose(pr, cn, x_fm, out_tm):
    cstage = pr.alloc("ex_cs", 128, 4096, F32)
    ostage = pr.alloc("ex_os", 128, 1024, F32)
    ps = pr.psum("f", 0)
    iap, ires = x_fm
    oap, ores = out_tm
    for c4 in range(4):
        pr.dma("qs", t_(cstage), (iap[:, c4 * 4096:(c4 + 1) * 4096], ires), tag="ex_ld")
        for tsub in range(4):
            tt = c4 * 4 + tsub
            for dt_i in range(NDT):
                pr.transpose(t_(ps, (0, 128), (0, 128)), t_(cstage, (0, 128), (dt_i * 512 + tsub * 128, 128)),
                             cn["ident_f"], tag="ex_tp")
                pr.copy(t_(ostage, (0, 128), (dt_i * 128, 128)), t_(ps, (0, 128), (0, 128)), tag="ex_ev")
            pr.dma("qs", (oap[tt * 128:(tt + 1) * 128, :], ores), t_(ostage), tag="ex_st")
    pr.freebuf(cstage)
    pr.freebuf(ostage)


def build_l0_attn(pr, cn, x_in, W, x_out, stage="full"):
    NH = 8
    if stage.startswith("ln"):
        cos = pr.alloc("rope_cos", 64, T, F32)
        sin = pr.alloc("rope_sin", 64, T, F32)
        pr.memset(t_(cos), 1.0, tag="rp_fake")
        pr.memset(t_(sin), 0.0, tag="rp_fake")
    else:
        cos, sin = build_rope(pr, cn)
    nwT = load_vecT(pr, cn, "l0nw", W["norm_w"][0])
    boutT = load_vecT(pr, cn, "l0bout", W["bout"][0])
    bbeta = load_vec_small(pr, "bbeta", W["bbeta"][0], 8)
    big = load_vec_small(pr, "big", W["big"][0], 8)
    bog = load_vec_small(pr, "bog", W["bog"][0], 8)
    WbT = load_wT(pr, cn, "Wbeta", W["Wbeta"][0], 8)
    WiT = load_wT(pr, cn, "Wig", W["Wig"][0], 8)
    WgT = load_wT(pr, cn, "Wog", W["Wog"][0], 8)
    h_sb = pr.alloc("h", 128, NDT * T, BF16)
    gate3 = pr.alloc("gate3", 72, T, F32)
    logf = gate3.view(0, 8)
    g_sb = gate3.view(32, 8)
    o_sb = gate3.view(64, 8)
    xc = pr.alloc("xc", 128, 4096, F32)
    bca = pr.alloc("bca", 128, 512, F32)
    bcb = pr.alloc("bcb", 128, 512, F32)
    tmpv = pr.alloc("tmp1", 128, 512, F32)
    tmp8a = pr.alloc("tmp8a", 8, 512, F32)
    tmp8b = pr.alloc("tmp8b", 8, 512, F32)
    xiap, xires = x_in
    for c4 in range(4):
        pr.dma("qs", t_(xc), (xiap[:, c4 * 4096:(c4 + 1) * 4096], xires), tag="l0a_ldx")
        tiles = [t_(xc, (0, 128), (dt_i * 512, 512)) for dt_i in range(NDT)]
        rows, a_f, b_f = ln_stats(pr, cn, tiles, 512, c4, with_rms=True, tag=f"l0ln{c4}")
        psA = pr.psum("f", 2)
        psBk = pr.psum("f", 3)
        bc_row(pr, cn, psA, a_f(), 512, tag="l0_bca")
        bc_row(pr, cn, psBk, b_f(), 512, tag="l0_bcb")
        pr.copy(t_(bca), t_(psA, (0, 128), (0, 512)), tag="l0_bcae")
        pr.copy(t_(bcb), t_(psBk, (0, 128), (0, 512)), tag="l0_bcbe")
        pr.freebuf(rows)
        for dt_i in range(NDT):
            pr.tt(t_(tmpv), tiles[dt_i], t_(bca), ALU.mult, tag="l0_h1")
            pr.tt(t_(tmpv), t_(tmpv), t_(bcb), ALU.add, tag="l0_h2")
            pr.ts((h_sb.ap((0, 128), (dt_i * T + c4 * 512, 512)), h_sb.reg((0, 128), (dt_i * T + c4 * 512, 512))),
                  t_(tmpv), (nwT.ap((0, 128), (dt_i, 1)), nwT.reg((0, 128), (dt_i, 1))), ALU.mult, tag="l0_h3")
        gsel = {"lnA": (), "lnB": (("g", WiT, big),)}
        if stage.startswith("lnC"):
            gsel[stage] = (("b", WbT, bbeta),)
        for kind, wT, bias in gsel.get(stage, (("b", WbT, bbeta), ("g", WiT, big), ("o", WgT, bog))):
            psg = pr.psum("f", 2)
            for dt_i in range(NDT):
                pr.mm(t_(psg, (0, 8), (0, 512)),
                      (wT.ap((0, 128), (dt_i * 8, 8)), wT.reg((0, 128), (dt_i * 8, 8))),
                      (h_sb.ap((0, 128), (dt_i * T + c4 * 512, 512)), h_sb.reg((0, 128), (dt_i * T + c4 * 512, 512))),
                      start=(dt_i == 0), stop=(dt_i == NDT - 1), tag=f"l0_g{kind}")
            bcol = (bias.ap((0, 8), (0, 1)), bias.reg((0, 8), (0, 1)))
            if kind == "b":
                depth = int(stage[3]) if stage.startswith("lnC") and len(stage) > 3 else 9
                pr.act(t_(tmp8a, (0, 8), (0, 512)), t_(psg, (0, 8), (0, 512)), AF.Abs, bias=bcol, tag="l0_abs")
                if depth >= 2:
                    pr.act(t_(tmp8a, (0, 8), (0, 512)), t_(tmp8a, (0, 8), (0, 512)), AF.Exp, scale=-1.0, tag="l0_e")
                if depth >= 3:
                    pr.act(t_(tmp8a, (0, 8), (0, 512)), t_(tmp8a, (0, 8), (0, 512)), AF.Ln, bias=1.0, tag="l0_l")
                if depth >= 4:
                    pr.ts(t_(tmp8b, (0, 8), (0, 512)), t_(psg, (0, 8), (0, 512)), bcol, ALU.add, 0.0, ALU.min, tag="l0_mn")
                if depth >= 5:
                    pr.tt((logf.ap((0, 8), (c4 * 512, 512)), logf.reg((0, 8), (c4 * 512, 512))),
                          t_(tmp8b, (0, 8), (0, 512)), t_(tmp8a, (0, 8), (0, 512)), ALU.subtract, tag="l0_lf")
                else:
                    pr.copy((logf.ap((0, 8), (c4 * 512, 512)), logf.reg((0, 8), (c4 * 512, 512))),
                            t_(tmp8a, (0, 8), (0, 512)), tag="l0_lfcp")
            else:
                dst = g_sb if kind == "g" else o_sb
                pr.act((dst.ap((0, 8), (c4 * 512, 512)), dst.reg((0, 8), (c4 * 512, 512))),
                       t_(psg, (0, 8), (0, 512)), AF.Sigmoid, bias=bcol, tag=f"l0_sig{kind}")
    pr.freebuf(tmp8a)
    pr.freebuf(tmp8b)
    pr.freebuf(bca)
    pr.freebuf(bcb)
    pr.freebuf(tmpv)
    pr.freebuf(xc)
    if stage.startswith("ln"):
        xoap_, xores_ = x_out
        dmp = pr.alloc("dmp", 128, 4096, F32)
        for c4 in range(4):
            for dt_i in range(NDT):
                pr.copy(t_(dmp, (0, 128), (dt_i * 512, 512)),
                        (h_sb.ap((0, 128), (dt_i * T + c4 * 512, 512)), h_sb.reg((0, 128), (dt_i * T + c4 * 512, 512))), tag="dmp")
            pr.dma("qs", (xoap_[:, c4 * 4096:(c4 + 1) * 4096], xores_), t_(dmp), tag="dmp_st")
        return
    Gt = pr.alloc("Gt", 128, 128, F32)
    Ot = pr.alloc("Ot", 128, 128, F32)
    lfT = pr.alloc("lfT", 128, 128, F32)
    psf4 = pr.psum("w", 0)
    for src, dst, scale in ((logf, lfT, None), (g_sb, Gt, SCALE_L0), (o_sb, Ot, None)):
        for c in range(16):
            pr.transpose(t_(psf4, (0, 128), (0, 8)),
                         (src.ap((0, 8), (c * 128, 128)), src.reg((0, 8), (c * 128, 128))),
                         cn["ident_f"], tag="l0_gtp")
            dstt = (dst.ap((0, 128), (c * 8, 8)), dst.reg((0, 128), (c * 8, 8)))
            if scale is None:
                pr.copy(dstt, t_(psf4, (0, 128), (0, 8)), tag="l0_gte")
            else:
                pr.ts(dstt, t_(psf4, (0, 128), (0, 8)), scale, ALU.mult, tag="l0_gte")
    pr.freebuf(gate3)
    Lc = pr.alloc("Lc", 128, 128, F32)
    negLc = pr.alloc("negLc", 128, 128, F32)
    expLc = pr.alloc("expLc", 128, 128, F32)
    Rt = pr.alloc("Rt", 128, 128, F32)
    expll = pr.alloc("expll", 128, 128, F32)
    ps0 = pr.psum("f", 0)
    pr.mm(t_(ps0, (0, 128), (0, 128)), t_(cn["utri_f"]), t_(lfT), tag="l0_cum")
    pr.copy(t_(Lc), t_(ps0, (0, 128), (0, 128)), tag="l0_Lc")
    pr.ts(t_(negLc), t_(Lc), -1.0, ALU.mult, tag="l0_negLc")
    pr.act(t_(expLc), t_(Lc), AF.Exp, tag="l0_expLc")
    ps1 = pr.psum("f", 1)
    pr.mm(t_(ps1, (0, 1), (0, 128)), t_(cn["e127_f"]), t_(Lc), tag="l0_ll")
    llrow = pr.alloc("llrow", 1, 128, F32)
    pr.copy(t_(llrow, (0, 1), (0, 128)), t_(ps1, (0, 1), (0, 128)), tag="l0_llr")
    pr.mm(t_(ps1, (0, 128), (0, 128)), (cn["ones_f"].ap((0, 1), (0, 128)), cn["ones_f"].reg((0, 1), (0, 128))),
          t_(llrow, (0, 1), (0, 128)), tag="l0_bcll")
    bcll = pr.alloc("bcll", 128, 128, F32)
    pr.copy(t_(bcll), t_(ps1, (0, 128), (0, 128)), tag="l0_bclle")
    pr.tt(t_(Rt), t_(bcll), t_(Lc), ALU.subtract, tag="l0_Rarg")
    pr.act(t_(Rt), t_(Rt), AF.Exp, tag="l0_R")
    pr.act(t_(expll), t_(bcll), AF.Exp, tag="l0_expll")
    pr.freebuf(bcll)
    pr.freebuf(llrow)
    pr.freebuf(lfT)
    y_sb = pr.alloc("y", 128, NDT * T, BF16)
    rt1 = pr.alloc("rt1", 64, 512, F32)
    rt2 = pr.alloc("rt2", 64, 512, F32)
    M32 = pr.alloc("M32", 128, 256, F32)
    Mbf = pr.alloc("Mbf", 128, 128, BF16)
    row_sb = pr.alloc("row", 1, 128, F32)
    darg = pr.alloc("darg", 128, 128, F32)
    dtile = pr.alloc("dtile", 128, 128, F32)
    Pbf = pr.alloc("Pbf", 128, 128, BF16)
    vtl = pr.alloc("vtl", 128, 128, BF16)
    ktl = pr.alloc("ktl", 128, 128, BF16)
    t128 = pr.alloc("t128", 128, 128, F32)
    yTo = pr.alloc("yTo", 128, 128, BF16)
    psSt = pr.psum("f", 0)
    psY = pr.psum("f", 1)
    psYI = pr.psum("f", 2)
    psM = pr.psum("f", 3)
    psAux = pr.psum("w", 0)
    pbV = pr.psum("b", 0)
    pbK = pr.psum("b", 0)
    pbB = pr.psum("b", 0)

    def load_w_headslice(wname, et):
        res = DramRes(wname)
        stage = pr.alloc(f"{wname}hs", 128, 1024, BF16)
        out = pr.alloc(f"{wname}hT", 128, 1024, BF16)
        pr.dma("qg", t_(stage), (W[wname][0][et * 128:(et + 1) * 128, :], res), tag=f"ldh_{wname}")
        for dt_i in range(NDT):
            pr.transpose(t_(pbB, (0, 128), (0, 128)), t_(stage, (0, 128), (dt_i * 128, 128)),
                         cn["ident_b"], tag=f"tph_{wname}")
            pr.copy(t_(out, (0, 128), (dt_i * 128, 128)), t_(pbB, (0, 128), (0, 128)), tag=f"evh_{wname}")
        pr.freebuf(stage)
        return out

    qh = pr.alloc("qh", 128, T, BF16)
    kh = pr.alloc("kh", 128, T, BF16)
    vh = pr.alloc("vh", 128, T, BF16)
    for hh in range(NH):
        for wname, dst, rope in (("Wq", qh, True), ("Wk", kh, True), ("Wv", vh, False)):
            wT = load_w_headslice(wname, hh)
            for c4 in range(4):
                psp = pr.psum("f", c4 % 2)
                for dt_i in range(NDT):
                    pr.mm(t_(psp, (0, 128), (0, 512)),
                          (wT.ap((0, 128), (dt_i * 128, 128)), wT.reg((0, 128), (dt_i * 128, 128))),
                          (h_sb.ap((0, 128), (dt_i * T + c4 * 512, 512)), h_sb.reg((0, 128), (dt_i * T + c4 * 512, 512))),
                          start=(dt_i == 0), stop=(dt_i == NDT - 1), tag=f"l0_p{wname}")
                dcol = c4 * 512
                if rope:
                    cosc = (cos.ap((0, 64), (c4 * 512, 512)), cos.reg((0, 64), (c4 * 512, 512)))
                    sinc = (sin.ap((0, 64), (c4 * 512, 512)), sin.reg((0, 64), (c4 * 512, 512)))
                    up = t_(psp, (0, 64), (0, 512))
                    lo = t_(psp, (64, 64), (0, 512))
                    pr.tt(t_(rt1, (0, 64), (0, 512)), up, cosc, ALU.mult, tag="l0_r1")
                    pr.tt(t_(rt2, (0, 64), (0, 512)), lo, sinc, ALU.mult, tag="l0_r2")
                    pr.tt((dst.ap((0, 64), (dcol, 512)), dst.reg((0, 64), (dcol, 512))),
                          t_(rt1, (0, 64), (0, 512)), t_(rt2, (0, 64), (0, 512)), ALU.subtract, tag="l0_r3")
                    pr.tt(t_(rt1, (0, 64), (0, 512)), up, sinc, ALU.mult, tag="l0_r4")
                    pr.tt(t_(rt2, (0, 64), (0, 512)), lo, cosc, ALU.mult, tag="l0_r5")
                    pr.tt((dst.ap((64, 64), (dcol, 512)), dst.reg((64, 64), (dcol, 512))),
                          t_(rt1, (0, 64), (0, 512)), t_(rt2, (0, 64), (0, 512)), ALU.add, tag="l0_r6")
                else:
                    pr.copy((dst.ap((0, 128), (dcol, 512)), dst.reg((0, 128), (dcol, 512))),
                            t_(psp, (0, 128), (0, 512)), tag="l0_vev")
            pr.freebuf(wT)
        ping, pong = 0, 128
        pr.memset(t_(M32, (0, 128), (0, 128)), 0.0, tag="l0_m0")
        pr.memset(t_(Mbf), 0.0, tag="l0_m0b")
        for c in range(16):
            hc = c * 8 + hh
            col = c * 128
            qc = (qh.ap((0, 128), (col, 128)), qh.reg((0, 128), (col, 128)))
            kc = (kh.ap((0, 128), (col, 128)), kh.reg((0, 128), (col, 128)))
            vc = (vh.ap((0, 128), (col, 128)), vh.reg((0, 128), (col, 128)))
            Gcol = (Gt.ap((0, 128), (hc, 1)), Gt.reg((0, 128), (hc, 1)))
            Ocol = (Ot.ap((0, 128), (hc, 1)), Ot.reg((0, 128), (hc, 1)))
            Rcol = (Rt.ap((0, 128), (hc, 1)), Rt.reg((0, 128), (hc, 1)))
            eLcol = (expLc.ap((0, 128), (hc, 1)), expLc.reg((0, 128), (hc, 1)))
            eLLcol = (expll.ap((0, 128), (hc, 1)), expll.reg((0, 128), (hc, 1)))
            nLcol = (negLc.ap((0, 128), (hc, 1)), negLc.reg((0, 128), (hc, 1)))
            pr.mm(t_(psSt, (0, 128), (0, 128)), kc, qc, tag="l0_st")
            pr.transpose(t_(psAux, (0, 1), (0, 128)),
                         (Lc.ap((0, 128), (hc, 1)), Lc.reg((0, 128), (hc, 1))), cn["ident_f"], tag="l0_ctp")
            pr.copy(t_(row_sb, (0, 1), (0, 128)), t_(psAux, (0, 1), (0, 128)), tag="l0_crow")
            pr.mm(t_(psAux, (0, 128), (0, 128)),
                  (cn["ones_f"].ap((0, 1), (0, 128)), cn["ones_f"].reg((0, 1), (0, 128))),
                  t_(row_sb, (0, 1), (0, 128)), tag="l0_bcLi")
            pr.stt(t_(darg), t_(psAux, (0, 128), (0, 128)), nLcol, t_(cn["maskneg_f"]), ALU.add, ALU.add, tag="l0_darg")
            pr.act(t_(dtile), t_(darg), AF.Exp, tag="l0_dexp")
            pr.tt(t_(Pbf), t_(psSt, (0, 128), (0, 128)), t_(dtile), ALU.mult, tag="l0_P")
            pr.transpose(t_(pbV, (0, 128), (0, 128)), vc, cn["ident_b"], tag="l0_vT")
            pr.ts(t_(vtl), t_(pbV, (0, 128), (0, 128)), Gcol, ALU.mult, tag="l0_vts")
            pr.transpose(t_(pbK, (0, 128), (0, 128)), kc, cn["ident_b"], tag="l0_kT")
            pr.ts(t_(ktl), t_(pbK, (0, 128), (0, 128)), Rcol, ALU.mult, tag="l0_kts")
            pr.mm(t_(psY, (0, 128), (0, 128)), t_(Pbf), t_(vtl), tag="l0_yintra")
            pr.mm(t_(psYI, (0, 128), (0, 128)), qc, t_(Mbf), tag="l0_yinter")
            pr.ts(t_(t128), t_(psYI, (0, 128), (0, 128)), eLcol, ALU.mult, Ocol, ALU.mult, tag="l0_yi2")
            pr.stt(t_(yTo), t_(psY, (0, 128), (0, 128)), Ocol, t_(t128), ALU.mult, ALU.add, tag="l0_ycmb")
            pr.transpose(t_(pbB, (0, 128), (0, 128)), t_(yTo), cn["ident_b"], tag="l0_yb")
            pr.copy((y_sb.ap((0, 128), (hh * T + col, 128)), y_sb.reg((0, 128), (hh * T + col, 128))),
                    t_(pbB, (0, 128), (0, 128)), tag="l0_yev")
            pr.mm(t_(psM, (0, 128), (0, 128)), t_(ktl), t_(vtl), tag="l0_mup")
            pr.stt(t_(M32, (0, 128), (pong, 128)), t_(M32, (0, 128), (ping, 128)), eLLcol,
                   t_(psM, (0, 128), (0, 128)), ALU.mult, ALU.add, tag="l0_mnew")
            pr.copy(t_(Mbf), t_(M32, (0, 128), (pong, 128)), tag="l0_mbf")
            ping, pong = pong, ping
    for b_ in (M32, Mbf, row_sb, darg, dtile, Pbf, vtl, ktl, t128, yTo, Gt, Ot, Lc, negLc, expLc, Rt, expll,
               qh, kh, vh, rt1, rt2, cos, sin):
        pr.freebuf(b_)
    pr.freebuf(h_sb)
    woT = load_wT(pr, cn, "Wout", W["Wout"][0], 1024)
    xc2 = pr.alloc("xc2", 128, 4096, F32)
    ost = pr.alloc("ost", 128, 1024, F32)
    xoap, xores = x_out
    for c4 in range(4):
        pr.dma("qs", t_(xc2), (xiap[:, c4 * 4096:(c4 + 1) * 4096], xires), tag="l0_ldx2")
        for et in range(NDT):
            psp = pr.psum("f", et % 2)
            for dt_i in range(NDT):
                pr.mm(t_(psp, (0, 128), (0, 512)),
                      (woT.ap((0, 128), (dt_i * 1024 + et * 128, 128)), woT.reg((0, 128), (dt_i * 1024 + et * 128, 128))),
                      (y_sb.ap((0, 128), (dt_i * T + c4 * 512, 512)), y_sb.reg((0, 128), (dt_i * T + c4 * 512, 512))),
                      start=(dt_i == 0), stop=(dt_i == NDT - 1), tag="l0_po")
            stg = (ost.ap((0, 128), ((et % 2) * 512, 512)), ost.reg((0, 128), ((et % 2) * 512, 512)))
            pr.stt(stg, t_(psp, (0, 128), (0, 512)),
                   (boutT.ap((0, 128), (et, 1)), boutT.reg((0, 128), (et, 1))),
                   t_(xc2, (0, 128), (et * 512, 512)), ALU.add, ALU.add, tag="l0_res")
            pr.dma("qs", (xoap[:, c4 * 4096 + et * 512:c4 * 4096 + (et + 1) * 512], xores), stg, tag="l0_stx")
    pr.freebuf(woT)
    pr.freebuf(xc2)
    pr.freebuf(ost)
    pr.freebuf(y_sb)
    for b_ in (nwT, boutT, bbeta, big, bog, WbT, WiT, WgT):
        pr.freebuf(b_)



from contextlib import ExitStack

L0A_WSPECS = [("norm_w", [D]), ("Wq", [D, D]), ("Wk", [D, D]), ("Wv", [D, D]),
              ("Wbeta", [8, D]), ("bbeta", [8]), ("Wig", [8, D]), ("big", [8]),
              ("Wog", [8, D]), ("bog", [8]), ("Wout", [D, D]), ("bout", [D])]


def build_launch_l0_attn(first=True, stage="full"):
    nc = bass.Bass("TRN2", target_bir_lowering=False)
    cap = declare_consts(nc)
    W = {}
    for name, shape in L0A_WSPECS:
        ap = nc.dram_tensor(f"L0_attn_{name}", shape, F32, kind="ExternalInput").ap()
        W[name] = (ap, DramRes(f"L0_attn_{name}"))
    xout = nc.dram_tensor("xout_fm", [128, 16384], F32, kind="ExternalOutput").ap()
    xout_t = (xout, DramRes("xout_fm"))
    es = ExitStack()
    pr = Prog(nc, 19968, 45056)
    pr.open(es)
    cn = load_consts(pr, cap)
    if first:
        x_tm = nc.dram_tensor("x", [T, D], F32, kind="ExternalInput").ap()
        xfm = nc.dram_tensor("xfm", [128, 16384], F32, kind="Internal").ap()
        xfm_t = (xfm, DramRes("xfm"))
        build_entry_transpose(pr, cn, (x_tm, DramRes("x")), xfm_t)
    else:
        xfm = nc.dram_tensor("xin_fm", [128, 16384], F32, kind="ExternalInput").ap()
        xfm_t = (xfm, DramRes("xin_fm"))
    if stage == "entry":
        tmp = pr.alloc("cpy", 128, 4096, F32)
        for c4 in range(4):
            pr.dma("qs", t_(tmp), (xfm[:, c4 * 4096:(c4 + 1) * 4096], xfm_t[1]), tag="cp_ld")
            pr.dma("qs", (xout[:, c4 * 4096:(c4 + 1) * 4096], xout_t[1]), t_(tmp), tag="cp_st")
    else:
        build_l0_attn(pr, cn, xfm_t, W, xout_t, stage=stage)
    blk = es.enter_context(nc.Block())
    pr.emit_all(blk)
    es.close()
    return nc


def _finish(pr, nc, es):
    blk = es.enter_context(nc.Block())
    pr.emit_all(blk)
    es.close()
    return nc


L1F_WSPECS = [("W_r", [D, D]), ("W_k", [D, D]), ("W_v", [D, D]), ("mu_r", [D]), ("mu_k", [D])]


def build_l1_ffn(pr, cn, x_in, W, x_out):
    WrT = load_wT(pr, cn, "W_r", W["W_r"][0], 1024)
    WkT = load_wT(pr, cn, "W_k", W["W_k"][0], 1024)
    WvT = load_wT(pr, cn, "W_v", W["W_v"][0], 1024)
    mur = load_vecT(pr, cn, "mu_r", W["mu_r"][0])
    muk = load_vecT(pr, cn, "mu_k", W["mu_k"][0])
    mur1 = pr.alloc("mur1", 128, 8, F32)
    muk1 = pr.alloc("muk1", 128, 8, F32)
    pr.ts(t_(mur1), t_(mur), -1.0, ALU.mult, 1.0, ALU.add, tag="f1_m1")
    pr.ts(t_(muk1), t_(muk), -1.0, ALU.mult, 1.0, ALU.add, tag="f1_m2")
    xc = pr.alloc("f1xc", 128, 4096, F32)
    tbuf = pr.alloc("f1t", 128, 4096, F32)
    prevc = pr.alloc("f1pc", 128, 8, F32)
    bca = pr.alloc("f1bca", 128, 512, F32)
    bcb = pr.alloc("f1bcb", 128, 512, F32)
    tmpv = pr.alloc("f1tmp", 128, 512, F32)
    xr = pr.alloc("f1xr", 128, 4096, BF16)
    xk = pr.alloc("f1xk", 128, 4096, BF16)
    r_sb = pr.alloc("f1r", 128, 4096, F32)
    ka = pr.alloc("f1ka", 128, 4096, BF16)
    stage = pr.alloc("f1st", 128, 1024, F32)
    pr.memset(t_(prevc), 0.0, tag="f1_pc0")
    xiap, xires = x_in
    xoap, xores = x_out
    for c4 in range(4):
        if c4 > 0:
            for dt_i in range(NDT):
                pr.copy(t_(prevc, (0, 128), (dt_i, 1)), t_(tbuf, (0, 128), (dt_i * 512 + 511, 1)), tag="f1_pcs")
        pr.dma("qs", t_(xc), (xiap[:, c4 * 4096:(c4 + 1) * 4096], xires), tag="f1_ldx")
        tiles = [t_(xc, (0, 128), (dt_i * 512, 512)) for dt_i in range(NDT)]
        rows, a_f, b_f = ln_stats(pr, cn, tiles, 512, c4, with_rms=False, tag=f"f1ln{c4}")
        psA = pr.psum("f", 2)
        psB_ = pr.psum("f", 3)
        bc_row(pr, cn, psA, a_f(), 512, tag="f1_bca")
        bc_row(pr, cn, psB_, b_f(), 512, tag="f1_bcb")
        pr.copy(t_(bca), t_(psA, (0, 128), (0, 512)), tag="f1_bcae")
        pr.copy(t_(bcb), t_(psB_, (0, 128), (0, 512)), tag="f1_bcbe")
        pr.freebuf(rows)
        for dt_i in range(NDT):
            tt_ = (tbuf.ap((0, 128), (dt_i * 512, 512)), tbuf.reg((0, 128), (dt_i * 512, 512)))
            pr.tt(tt_, tiles[dt_i], t_(bca), ALU.mult, tag="f1_t1")
            pr.tt(tt_, tt_, t_(bcb), ALU.add, tag="f1_t2")
        for dt_i in range(NDT):
            tfull = (tbuf.ap((0, 128), (dt_i * 512, 512)), tbuf.reg((0, 128), (dt_i * 512, 512)))
            tshift = (tbuf.ap((0, 128), (dt_i * 512, 511)), tbuf.reg((0, 128), (dt_i * 512, 511)))
            pcol = (prevc.ap((0, 128), (dt_i, 1)), prevc.reg((0, 128), (dt_i, 1)))
            for mu, mu1, dst in ((mur, mur1, xr), (muk, muk1, xk)):
                mcol = (mu.ap((0, 128), (dt_i, 1)), mu.reg((0, 128), (dt_i, 1)))
                m1col = (mu1.ap((0, 128), (dt_i, 1)), mu1.reg((0, 128), (dt_i, 1)))
                pr.ts(t_(tmpv, (0, 128), (1, 511)), tshift, m1col, ALU.mult, tag="f1_z1")
                pr.ts(t_(tmpv, (0, 128), (0, 1)), pcol, m1col, ALU.mult, tag="f1_z0")
                pr.stt((dst.ap((0, 128), (dt_i * 512, 512)), dst.reg((0, 128), (dt_i * 512, 512))),
                       tfull, mcol, t_(tmpv, (0, 128), (0, 512)), ALU.mult, ALU.add, tag="f1_mix")
        for ot in range(NDT):
            psp = pr.psum("f", ot % 2)
            for dt_i in range(NDT):
                pr.mm(t_(psp, (0, 128), (0, 512)),
                      (WrT.ap((0, 128), (dt_i * 1024 + ot * 128, 128)), WrT.reg((0, 128), (dt_i * 1024 + ot * 128, 128))),
                      (xr.ap((0, 128), (dt_i * 512, 512)), xr.reg((0, 128), (dt_i * 512, 512))),
                      start=(dt_i == 0), stop=(dt_i == NDT - 1), tag="f1_mr")
            pr.act((r_sb.ap((0, 128), (ot * 512, 512)), r_sb.reg((0, 128), (ot * 512, 512))),
                   t_(psp, (0, 128), (0, 512)), AF.Sigmoid, tag="f1_sig")
        for ot in range(NDT):
            psp = pr.psum("f", ot % 2)
            for dt_i in range(NDT):
                pr.mm(t_(psp, (0, 128), (0, 512)),
                      (WkT.ap((0, 128), (dt_i * 1024 + ot * 128, 128)), WkT.reg((0, 128), (dt_i * 1024 + ot * 128, 128))),
                      (xk.ap((0, 128), (dt_i * 512, 512)), xk.reg((0, 128), (dt_i * 512, 512))),
                      start=(dt_i == 0), stop=(dt_i == NDT - 1), tag="f1_mk")
            pr.act(t_(tmpv), t_(psp, (0, 128), (0, 512)), AF.Relu, tag="f1_relu")
            pr.tt((ka.ap((0, 128), (ot * 512, 512)), ka.reg((0, 128), (ot * 512, 512))),
                  t_(tmpv), t_(tmpv), ALU.mult, tag="f1_sq")
        for ot in range(NDT):
            psp = pr.psum("f", ot % 2)
            for dt_i in range(NDT):
                pr.mm(t_(psp, (0, 128), (0, 512)),
                      (WvT.ap((0, 128), (dt_i * 1024 + ot * 128, 128)), WvT.reg((0, 128), (dt_i * 1024 + ot * 128, 128))),
                      (ka.ap((0, 128), (dt_i * 512, 512)), ka.reg((0, 128), (dt_i * 512, 512))),
                      start=(dt_i == 0), stop=(dt_i == NDT - 1), tag="f1_mv")
            stg = (stage.ap((0, 128), ((ot % 2) * 512, 512)), stage.reg((0, 128), ((ot % 2) * 512, 512)))
            pr.tt(stg, t_(psp, (0, 128), (0, 512)),
                  (r_sb.ap((0, 128), (ot * 512, 512)), r_sb.reg((0, 128), (ot * 512, 512))), ALU.mult, tag="f1_rm")
            pr.tt(stg, stg, t_(xc, (0, 128), (ot * 512, 512)), ALU.add, tag="f1_res")
            pr.dma("qs", (xoap[:, c4 * 4096 + ot * 512:c4 * 4096 + (ot + 1) * 512], xores), stg, tag="f1_stx")
    for b_ in (WrT, WkT, WvT, mur, muk, mur1, muk1, xc, tbuf, prevc, bca, bcb, tmpv, xr, xk, r_sb, ka, stage):
        pr.freebuf(b_)


def build_launch_l1_ffn(last=True):
    nc = bass.Bass("TRN2", target_bir_lowering=False)
    cap = declare_consts(nc)
    W = {}
    for name, shape in L1F_WSPECS:
        ap = nc.dram_tensor(f"L1_ffn_{name}", shape, F32, kind="ExternalInput").ap()
        W[name] = (ap, DramRes(f"L1_ffn_{name}"))
    xin = nc.dram_tensor("xin_fm", [128, 16384], F32, kind="ExternalInput").ap()
    xin_t = (xin, DramRes("xin_fm"))
    es = ExitStack()
    pr = Prog(nc, 22528, 40960)
    pr.open(es)
    cn = load_consts(pr, cap)
    if last:
        xout = nc.dram_tensor("xout_fm", [128, 16384], F32, kind="Internal").ap()
        xout_t = (xout, DramRes("xout_fm"))
        out_tm = nc.dram_tensor("out", [T, D], F32, kind="ExternalOutput").ap()
        build_l1_ffn(pr, cn, xin_t, W, xout_t)
        build_exit_transpose(pr, cn, xout_t, (out_tm, DramRes("out")))
    else:
        xout = nc.dram_tensor("xout_fm", [128, 16384], F32, kind="ExternalOutput").ap()
        xout_t = (xout, DramRes("xout_fm"))
        build_l1_ffn(pr, cn, xin_t, W, xout_t)
    blk = es.enter_context(nc.Block())
    pr.emit_all(blk)
    es.close()
    return nc


L0F_WSPECS = [("router_w", [D, 8]), ("w1", [8, D, 2816]), ("v1", [8, D, 2816]),
              ("w2", [8, 2816, D]), ("bias", [D])]
FT = 22
FGROUPS = [(0, 4), (4, 4), (8, 4), (12, 4), (16, 4), (20, 2)]


def build_l0_ffn(pr, cn, x_in, W, x_out, stage="full"):
    NEXP = {"e1": 1, "e2": 2, "e1w": 1, "e1a": 1, "e1b": 1}.get(stage, 8)
    SKIP_H = stage == "e1w"
    SKIP_W2 = stage in ("e1w", "e1a")
    SKIP_ACCD = stage in ("e1w", "e1a", "e1b")
    if stage in ("min1", "min2"):
        xoap_, xores_ = x_out
        mb = pr.alloc("dbgmb", 128, 512, F32)
        pr.memset(t_(mb), 3.25, tag="dbg_m1")
        nst = 1 if stage == "min2" else 32
        for i in range(nst):
            c4, dt_i = i // 8, i % 8
            pr.dma("qs", (xoap_[:, c4 * 4096 + dt_i * 512:c4 * 4096 + (dt_i + 1) * 512], xores_),
                   t_(mb), tag="dbg_m1st")
        return
    biasT = load_vecT(pr, cn, "l0fb", W["bias"][0])
    rbuf = pr.alloc("rw", 128, 64, F32)
    pr.dma("qs", (rbuf.ap().rearrange("p (t j) -> p t j", t=8),
                  rbuf.reg()),
           (W["router_w"][0].rearrange("(t p) j -> p t j", p=128), W["router_w"][1]), tag="ld_rw")
    xiap, xires = x_in
    xoap, xores = x_out
    xc = pr.alloc("f0xc", 128, 4096, F32)
    ln32 = pr.alloc("f0ln", 128, 512, F32)
    bca = pr.alloc("f0bca", 128, 512, F32)
    bcb = pr.alloc("f0bcb", 128, 512, F32)
    xh = pr.alloc("f0xh", 128, NDT * T, BF16)
    g_em = pr.alloc("f0gem", 8, T, F32)
    tokt = pr.alloc("f0tok", 128, 8, F32)
    rsb = pr.alloc("f0rsb", 8, 512, F32)
    msk = pr.alloc("f0msk", 128, 8, F32)
    top8 = pr.alloc("f0top", 128, 8, F32)
    cola = pr.alloc("f0col", 128, 2, F32)
    psR = pr.psum("w", 0)
    if stage == "min0":
        pr.memset(t_(xc, (0, 128), (0, 512)), 3.25, tag="dbg_m0")
        for c4 in range(4):
            for dt_i in range(NDT):
                pr.dma("qs", (xoap[:, c4 * 4096 + dt_i * 512:c4 * 4096 + (dt_i + 1) * 512], xores),
                       t_(xc, (0, 128), (0, 512)), tag="dbg_m0st")
        return
    nchunk = 1 if stage == "ln1" else 4
    for c4 in range(nchunk):
        pr.dma("qs", t_(xc), (xiap[:, c4 * 4096:(c4 + 1) * 4096], xires), tag="f0_ldx")
        tiles = [t_(xc, (0, 128), (dt_i * 512, 512)) for dt_i in range(NDT)]
        rows, a_f, b_f = ln_stats(pr, cn, tiles, 512, c4, with_rms=False, tag=f"f0ln{c4}")
        psA = pr.psum("f", 2)
        psB_ = pr.psum("f", 3)
        bc_row(pr, cn, psA, a_f(), 512, tag="f0_bca")
        bc_row(pr, cn, psB_, b_f(), 512, tag="f0_bcb")
        pr.copy(t_(bca), t_(psA, (0, 128), (0, 512)), tag="f0_bcae")
        pr.copy(t_(bcb), t_(psB_, (0, 128), (0, 512)), tag="f0_bcbe")
        pr.freebuf(rows)
        for dt_i in range(NDT):
            pr.tt(t_(ln32), tiles[dt_i], t_(bca), ALU.mult, tag="f0_l1")
            pr.tt(t_(ln32), t_(ln32), t_(bcb), ALU.add, tag="f0_l2")
            pr.copy((xh.ap((0, 128), (dt_i * T + c4 * 512, 512)), xh.reg((0, 128), (dt_i * T + c4 * 512, 512))),
                    t_(ln32), tag="f0_cast")
            if stage not in ("lnx", "lnz"):
                pr.mm(t_(psR, (0, 8), (0, 512)),
                      (rbuf.ap((0, 128), (dt_i * 8, 8)), rbuf.reg((0, 128), (dt_i * 8, 8))),
                      t_(ln32), start=(dt_i == 0), stop=(dt_i == NDT - 1), tag="f0_rmm")
        if stage in ("lnx", "lnz"):
            continue
        pr.copy(t_(rsb, (0, 8), (0, 512)), t_(psR, (0, 8), (0, 512)), tag="f0_rev")
        for sub in range(4):
            pst = pr.psum("f", 2)
            pr.transpose(t_(pst, (0, 128), (0, 8)), t_(rsb, (0, 8), (sub * 128, 128)), cn["ident_f"], tag="f0_rtp")
            pr.copy(t_(tokt), t_(pst, (0, 128), (0, 8)), tag="f0_tok")
            pr.reduce((cola.ap((0, 128), (0, 1)), cola.reg((0, 128), (0, 1))), t_(tokt), ALU.max, negate=True, tag="f0_rmax")
            pr.act(t_(tokt), t_(tokt), AF.Exp,
                   bias=(cola.ap((0, 128), (0, 1)), cola.reg((0, 128), (0, 1))),
                   accum_out=(cola.ap((0, 128), (1, 1)), cola.reg((0, 128), (1, 1))), tag="f0_exp")
            pr.recip((cola.ap((0, 128), (1, 1)), cola.reg((0, 128), (1, 1))),
                     (cola.ap((0, 128), (1, 1)), cola.reg((0, 128), (1, 1))), tag="f0_rcp")
            pr.ts(t_(tokt), t_(tokt), (cola.ap((0, 128), (1, 1)), cola.reg((0, 128), (1, 1))), ALU.mult, tag="f0_nrm")
            if stage != "lnr":
                pr.max8(t_(top8), t_(tokt), tag="f0_max8")
                pr.ts(t_(msk), t_(tokt), (top8.ap((0, 128), (1, 1)), top8.reg((0, 128), (1, 1))), ALU.is_ge, tag="f0_sel")
                pr.tt(t_(tokt), t_(tokt), t_(msk), ALU.mult, tag="f0_gsel")
            pst2 = pr.psum("f", 3)
            pr.transpose(t_(pst2, (0, 8), (0, 128)), t_(tokt), cn["ident_f"], tag="f0_gtp")
            pr.copy((g_em.ap((0, 8), (c4 * 512 + sub * 128, 128)), g_em.reg((0, 8), (c4 * 512 + sub * 128, 128))),
                    t_(pst2, (0, 8), (0, 128)), tag="f0_gev")
    pr.freebuf(ln32)
    pr.freebuf(bca)
    pr.freebuf(bcb)
    pr.freebuf(tokt)
    pr.freebuf(rsb)
    pr.freebuf(msk)
    pr.freebuf(top8)
    pr.freebuf(cola)
    pr.freebuf(xc)
    moeacc = pr.nc.dram_tensor("moeacc_hbm", [128, 16384], F32, kind="Internal").ap()
    moeacc_res = DramRes("moeacc_hbm")
    accst = pr.alloc("f0accst", 128, 4096, F32)
    bcg = pr.alloc("f0bcg", 128, 4096, F32)
    gtmp = pr.alloc("f0gt", 128, 512, F32)
    ttmp = pr.alloc("f0tt", 128, 512, F32)
    hbuf = pr.alloc("f0h", 128, 4 * 512, BF16)
    wbufs = [None, None]
    w1ap, w1res = W["w1"]
    v1ap, v1res = W["v1"]
    w2ap, w2res = W["w2"]

    def load_group(e, g, slot):
        f0, nf = FGROUPS[g]
        gc = nf * 128
        if wbufs[slot] is not None:
            for b_ in wbufs[slot]:
                pr.freebuf(b_)
        w1b = pr.alloc(f"w1_{slot}", 128, NDT * gc, BF16)
        v1b = pr.alloc(f"v1_{slot}", 128, NDT * gc, BF16)
        w2b = pr.alloc(f"w2_{slot}", 128, nf * 1024, BF16)
        pr.dma("qg", (w1b.ap().rearrange("p (t j) -> p t j", t=NDT), w1b.reg()),
               (w1ap[e, :, f0 * 128:f0 * 128 + gc].rearrange("(t p) j -> p t j", p=128), w1res), tag="f0_lw1")
        pr.dma("qg", (v1b.ap().rearrange("p (t j) -> p t j", t=NDT), v1b.reg()),
               (v1ap[e, :, f0 * 128:f0 * 128 + gc].rearrange("(t p) j -> p t j", p=128), v1res), tag="f0_lv1")
        pr.dma("qg", (w2b.ap().rearrange("p (t j) -> p t j", t=nf), w2b.reg()),
               (w2ap[e, f0 * 128:f0 * 128 + gc, :].rearrange("(t p) j -> p t j", p=128), w2res), tag="f0_lw2")
        wbufs[slot] = (w1b, v1b, w2b)
        return wbufs[slot]

    ng = len(FGROUPS)
    if stage == "ln1":
        for c4 in range(4):
            for dt_i in range(NDT):
                pr.dma("qs", (xoap[:, c4 * 4096 + dt_i * 512:c4 * 4096 + (dt_i + 1) * 512], xores),
                       (xh.ap((0, 128), (0, 512)), xh.reg((0, 128), (0, 512))), tag="dbg1_st")
        return
    if stage == "lnz":
        for c4 in range(4):
            for dt_i in range(NDT):
                pr.copy(t_(bcg, (0, 128), (0, 512)),
                        (xh.ap((0, 128), (dt_i * T + c4 * 512, 512)), xh.reg((0, 128), (dt_i * T + c4 * 512, 512))), tag="dbgz")
                pr.dma("qs", (xoap[:, c4 * 4096 + dt_i * 512:c4 * 4096 + (dt_i + 1) * 512], xores),
                       t_(bcg, (0, 128), (0, 512)), tag="dbgz_st")
        for b_ in (biasT, rbuf, xh, g_em, accst, bcg, gtmp, ttmp, hbuf):
            pr.freebuf(b_)
        return
    if stage in ("ln", "lnx", "lnr"):
        for c4 in range(4):
            psG = pr.psum("w", 0)
            pr.mm(t_(psG, (0, 128), (0, 512)),
                  (cn["esel8_f"].ap((0, 8), (0, 128)), cn["esel8_f"].reg((0, 8), (0, 128))),
                  (g_em.ap((0, 8), (c4 * 512, 512)), g_em.reg((0, 8), (c4 * 512, 512))), tag="dbg_bcg")
            pr.copy(t_(bcg, (0, 128), (0, 512)), t_(psG, (0, 128), (0, 512)), tag="dbg_e")
            for dt_i in range(NDT):
                pr.dma("qs", (xoap[:, c4 * 4096 + dt_i * 512:c4 * 4096 + (dt_i + 1) * 512], xores),
                       t_(bcg, (0, 128), (0, 512)), tag="dbg_st")
        for b_ in (biasT, rbuf, xh, g_em, accst, bcg, gtmp, ttmp, hbuf):
            pr.freebuf(b_)
        return
    E = NEXP
    cur = load_group(0, 0, 0)
    for e in range(E):
        for c4 in range(4):
            psG = pr.psum("w", 0)
            pr.mm(t_(psG, (0, 128), (0, 512)),
                  (cn["esel8_f"].ap((0, 8), (e * 128, 128)), cn["esel8_f"].reg((0, 8), (e * 128, 128))),
                  (g_em.ap((0, 8), (c4 * 512, 512)), g_em.reg((0, 8), (c4 * 512, 512))), tag="f0_bcg")
            pr.copy(t_(bcg, (0, 128), (c4 * 1024, 512)), t_(psG, (0, 128), (0, 512)), tag="f0_bcge")
        for g in range(ng):
            w1b, v1b, w2b = cur
            f0, nf = FGROUPS[g]
            gc = nf * 128
            ne, ngx = (e, g + 1) if g + 1 < ng else (e + 1, 0)
            if ne < E:
                nxt = load_group(ne, ngx, 1 - (g + e * ng) % 2)
            for c4 in range(4):
                if SKIP_H:
                    continue
                ps1 = pr.psum("f", 0)
                ps2 = pr.psum("f", 1)
                for ftl in range(nf):
                    for dt_i in range(NDT):
                        pr.mm(t_(ps1, (0, 128), (0, 512)),
                              (w1b.ap((0, 128), (dt_i * gc + ftl * 128, 128)), w1b.reg((0, 128), (dt_i * gc + ftl * 128, 128))),
                              (xh.ap((0, 128), (dt_i * T + c4 * 512, 512)), xh.reg((0, 128), (dt_i * T + c4 * 512, 512))),
                              start=(dt_i == 0), stop=(dt_i == NDT - 1), tag="f0_h1")
                        pr.mm(t_(ps2, (0, 128), (0, 512)),
                              (v1b.ap((0, 128), (dt_i * gc + ftl * 128, 128)), v1b.reg((0, 128), (dt_i * gc + ftl * 128, 128))),
                              (xh.ap((0, 128), (dt_i * T + c4 * 512, 512)), xh.reg((0, 128), (dt_i * T + c4 * 512, 512))),
                              start=(dt_i == 0), stop=(dt_i == NDT - 1), tag="f0_h2")
                    pr.act(t_(gtmp), t_(ps1, (0, 128), (0, 512)), AF.Gelu, tag="f0_gelu")
                    pr.tt((hbuf.ap((0, 128), (ftl * 512, 512)), hbuf.reg((0, 128), (ftl * 512, 512))),
                          t_(gtmp), t_(ps2, (0, 128), (0, 512)), ALU.mult, tag="f0_hmul")
                if SKIP_W2:
                    continue
                for dt_i in range(NDT):
                    ps3 = pr.psum("f", 2 + dt_i % 2)
                    for ftl in range(nf):
                        pr.mm(t_(ps3, (0, 128), (0, 512)),
                              (w2b.ap((0, 128), (ftl * 1024 + dt_i * 128, 128)), w2b.reg((0, 128), (ftl * 1024 + dt_i * 128, 128))),
                              (hbuf.ap((0, 128), (ftl * 512, 512)), hbuf.reg((0, 128), (ftl * 512, 512))),
                              start=(ftl == 0), stop=(ftl == nf - 1), tag="f0_w2")
                    accsl = (accst.ap((0, 128), (dt_i * 512, 512)), accst.reg((0, 128), (dt_i * 512, 512)))
                    bcgsl = (bcg.ap((0, 128), (c4 * 1024, 512)), bcg.reg((0, 128), (c4 * 1024, 512)))
                    pr.tt(accsl, t_(ps3, (0, 128), (0, 512)), bcgsl, ALU.mult, tag="f0_og")
                if not SKIP_ACCD:
                    for hf in range(2):
                        pr.dma("qg", (moeacc[:, c4 * 4096 + hf * 2048:c4 * 4096 + (hf + 1) * 2048], moeacc_res),
                               (accst.ap((0, 128), (hf * 2048, 2048)), accst.reg((0, 128), (hf * 2048, 2048))),
                               accum=not (e == 0 and g == 0), tag="f0_accd")
            cur = nxt if ne < E else cur
    for b_ in (bcg, gtmp, ttmp, hbuf):
        pr.freebuf(b_)
    accld = accst
    for slot in wbufs:
        if slot is not None:
            for b_ in slot:
                pr.freebuf(b_)
    wbufs[0] = wbufs[1] = None
    xc2 = pr.alloc("f0xc2", 128, 4096, F32)
    stage = pr.alloc("f0st", 128, 1024, F32)
    for c4 in range(4):
        pr.dma("qs", t_(xc2), (xiap[:, c4 * 4096:(c4 + 1) * 4096], xires), tag="f0_ldx2")
        if SKIP_ACCD:
            pr.memset(t_(accld), 0.0, tag="f0_acc0m")
        else:
            pr.dma("qg", t_(accld), (moeacc[:, c4 * 4096:(c4 + 1) * 4096], moeacc_res), tag="f0_accl")
        for dt_i in range(NDT):
            stg = (stage.ap((0, 128), ((dt_i % 2) * 512, 512)), stage.reg((0, 128), ((dt_i % 2) * 512, 512)))
            pr.stt(stg, (accld.ap((0, 128), (dt_i * 512, 512)), accld.reg((0, 128), (dt_i * 512, 512))),
                   (biasT.ap((0, 128), (dt_i, 1)), biasT.reg((0, 128), (dt_i, 1))),
                   t_(xc2, (0, 128), (dt_i * 512, 512)), ALU.add, ALU.add, tag="f0_cmb")
            pr.dma("qs", (xoap[:, c4 * 4096 + dt_i * 512:c4 * 4096 + (dt_i + 1) * 512], xores), stg, tag="f0_stx")
    for b_ in (biasT, rbuf, xh, g_em, accst, xc2, stage):
        pr.freebuf(b_)


def build_launch_l0_ffn(stage="full"):
    nc = bass.Bass("TRN2", target_bir_lowering=False)
    cap = declare_consts(nc)
    W = {}
    for name, shape in L0F_WSPECS:
        ap = nc.dram_tensor(f"L0_ffn_{name}", shape, F32, kind="ExternalInput").ap()
        W[name] = (ap, DramRes(f"L0_ffn_{name}"))
    xin = nc.dram_tensor("xin_fm", [128, 16384], F32, kind="ExternalInput").ap()
    xout = nc.dram_tensor("xout_fm", [128, 16384], F32, kind="ExternalOutput").ap()
    es = ExitStack()
    pr = Prog(nc, 26624, 45056)
    pr.open(es)
    import os
    cn = load_consts(pr, cap, names=(() if os.environ.get("NOCONSTS") else None))
    build_l0_ffn(pr, cn, (xin, DramRes("xin_fm")), W, (xout, DramRes("xout_fm")), stage=stage)
    blk = es.enter_context(nc.Block())
    pr.emit_all(blk)
    es.close()
    return nc


L1A_WSPECS = [("wq_w", [D, D]), ("wq_b", [D]), ("wk_w", [D, D]), ("wk_b", [D]),
              ("wv_w", [D, D]), ("wv_b", [D]), ("wo_w", [D, D]), ("wo_b", [D]),
              ("wqi_w", [D, 64]), ("wqi_b", [64]), ("wki_w", [D, 64]), ("wki_b", [64])]
AL = 1536
NH1, DH1 = 16, 64


def build_l1_attn(pr, cn, x_in, W, x_out):
    xiap, xires = x_in
    xoap, xores = x_out
    wqi = pr.alloc("wqi", 128, 512, F32)
    wki = pr.alloc("wki", 128, 512, F32)
    for name, buf in (("wqi_w", wqi), ("wki_w", wki)):
        pr.dma("qs", (buf.ap().rearrange("p (t j) -> p t j", t=8), buf.reg()),
               (W[name][0].rearrange("(t p) j -> p t j", p=128), W[name][1]), tag=f"ld_{name}")
    qib = load_vec_small(pr, "wqi_b", W["wqi_b"][0], 64)
    kib = load_vec_small(pr, "wki_b", W["wki_b"][0], 64)
    qb = load_vecT(pr, cn, "wq_b", W["wq_b"][0])
    kb = load_vecT(pr, cn, "wk_b", W["wk_b"][0])
    vb = load_vecT(pr, cn, "wv_b", W["wv_b"][0])
    ob = load_vecT(pr, cn, "wo_b", W["wo_b"][0])
    qbs = pr.alloc("qbs", 128, 8, F32)
    pr.ts(t_(qbs), t_(qb), SCALE_L1, ALU.mult, tag="a1_qbs")
    xc = pr.alloc("a1xc", 128, 4096, F32)
    ln32 = pr.alloc("a1ln", 128, 512, F32)
    bca = pr.alloc("a1bca", 128, 512, F32)
    bcb = pr.alloc("a1bcb", 128, 512, F32)
    xh = pr.alloc("a1xh", 128, 4096, BF16)
    xhh = pr.nc.dram_tensor("a1xh_hbm", [128, NDT * T], BF16, kind="Internal").ap()
    xhh_res = DramRes("a1xh_hbm")
    qidx = pr.alloc("a1qi", 64, T, F32)
    kidx = pr.alloc("a1ki", 64, AL, F32)
    psQ = pr.psum("f", 2)
    psK = pr.psum("f", 3)
    for c4 in range(4):
        pr.dma("qs", t_(xc), (xiap[:, c4 * 4096:(c4 + 1) * 4096], xires), tag="a1_ldx")
        tiles = [t_(xc, (0, 128), (dt_i * 512, 512)) for dt_i in range(NDT)]
        rows, a_f, b_f = ln_stats(pr, cn, tiles, 512, c4, with_rms=False, tag=f"a1ln{c4}")
        psA = pr.psum("f", 0)
        psB_ = pr.psum("f", 1)
        bc_row(pr, cn, psA, a_f(), 512, tag="a1_bca")
        bc_row(pr, cn, psB_, b_f(), 512, tag="a1_bcb")
        pr.copy(t_(bca), t_(psA, (0, 128), (0, 512)), tag="a1_bcae")
        pr.copy(t_(bcb), t_(psB_, (0, 128), (0, 512)), tag="a1_bcbe")
        pr.freebuf(rows)
        for dt_i in range(NDT):
            pr.tt(t_(ln32), tiles[dt_i], t_(bca), ALU.mult, tag="a1_l1")
            pr.tt(t_(ln32), t_(ln32), t_(bcb), ALU.add, tag="a1_l2")
            pr.copy((xh.ap((0, 128), (dt_i * 512, 512)), xh.reg((0, 128), (dt_i * 512, 512))),
                    t_(ln32), tag="a1_cast")
            pr.mm(t_(psQ, (0, 64), (0, 512)),
                  (wqi.ap((0, 128), (dt_i * 64, 64)), wqi.reg((0, 128), (dt_i * 64, 64))),
                  t_(ln32), start=(dt_i == 0), stop=(dt_i == NDT - 1), tag="a1_qimm")
            if c4 < 3:
                pr.mm(t_(psK, (0, 64), (0, 512)),
                      (wki.ap((0, 128), (dt_i * 64, 64)), wki.reg((0, 128), (dt_i * 64, 64))),
                      t_(ln32), start=(dt_i == 0), stop=(dt_i == NDT - 1), tag="a1_kimm")
        pr.dma("qg", (xhh[:, c4 * 4096:(c4 + 1) * 4096], xhh_res), t_(xh), tag="a1_xhst")
        pr.ts((qidx.ap((0, 64), (c4 * 512, 512)), qidx.reg((0, 64), (c4 * 512, 512))),
              t_(psQ, (0, 64), (0, 512)), (qib.ap((0, 64), (0, 1)), qib.reg((0, 64), (0, 1))), ALU.add, tag="a1_qie")
        if c4 < 3:
            pr.ts((kidx.ap((0, 64), (c4 * 512, 512)), kidx.reg((0, 64), (c4 * 512, 512))),
                  t_(psK, (0, 64), (0, 512)), (kib.ap((0, 64), (0, 1)), kib.reg((0, 64), (0, 1))), ALU.add, tag="a1_kie")
    pr.freebuf(ln32)
    pr.freebuf(bca)
    pr.freebuf(bcb)
    pr.freebuf(xc)
    pr.freebuf(wqi)
    pr.freebuf(wki)

    def proj(wname, dst, bias_buf, scale, hbm=None):
        wn = load_w_nat(pr, wname, W[wname][0])
        for c4 in range(4):
            pr.dma("qg", t_(xh), (xhh[:, c4 * 4096:(c4 + 1) * 4096], xhh_res), tag="a1_xhld")
            for ot in range(NDT):
                psp = pr.psum("f", ot % 2)
                for dt_i in range(NDT):
                    pr.mm(t_(psp, (0, 128), (0, 512)),
                          (wn.ap((0, 128), (dt_i * 1024 + ot * 128, 128)), wn.reg((0, 128), (dt_i * 1024 + ot * 128, 128))),
                          (xh.ap((0, 128), (dt_i * 512, 512)), xh.reg((0, 128), (dt_i * 512, 512))),
                          start=(dt_i == 0), stop=(dt_i == NDT - 1), tag=f"a1_p{wname}")
                kw = dict(bias=(bias_buf.ap((0, 128), (ot, 1)), bias_buf.reg((0, 128), (ot, 1))))
                if scale is not None:
                    kw["scale"] = scale
                dsl = ((dst.ap((0, 128), (ot * 512, 512)), dst.reg((0, 128), (ot * 512, 512))) if hbm is not None
                       else (dst.ap((0, 128), (ot * T + c4 * 512, 512)), dst.reg((0, 128), (ot * T + c4 * 512, 512))))
                pr.act(dsl, t_(psp, (0, 128), (0, 512)), AF.Identity, tag=f"a1_e{wname}", **kw)
            if hbm is not None:
                pr.dma("qg", (hbm[0][:, c4 * 4096:(c4 + 1) * 4096], hbm[1]), t_(dst), tag=f"a1_{wname}st")
        pr.freebuf(wn)

    v_sb = pr.alloc("a1v", 128, NDT * T, BF16)
    proj("wv_w", v_sb, vb, None)
    vT = pr.alloc("a1vT", 128, NH1 * 16 * 65, BF16)
    pbt = pr.psum("b", 0)
    for hh in range(NH1):
        for jt in range(16):
            src_ = (v_sb.ap(((hh % 2) * 64, 64), ((hh // 2) * T + jt * 128, 128)),
                    v_sb.reg(((hh % 2) * 64, 64), ((hh // 2) * T + jt * 128, 128)))
            pr.transpose(t_(pbt, (0, 128), (0, 64)), src_, cn["ident_b"], tag="a1_vtp")
            col = (hh * 16 + jt) * 65
            pr.copy((vT.ap((0, 128), (col, 64)), vT.reg((0, 128), (col, 64))),
                    t_(pbt, (0, 128), (0, 64)), tag="a1_vte")
            pr.memset((vT.ap((0, 128), (col + 64, 1)), vT.reg((0, 128), (col + 64, 1))), 1.0, tag="a1_vt1")
    pr.freebuf(v_sb)
    qhbm = pr.nc.dram_tensor("a1q_hbm", [128, NDT * T], BF16, kind="Internal").ap()
    qhbm_res = DramRes("a1q_hbm")
    q_st = pr.alloc("a1qst", 128, 4096, BF16)
    proj("wq_w", q_st, qbs, SCALE_L1, hbm=(qhbm, qhbm_res))
    pr.freebuf(q_st)
    k_sb = pr.alloc("a1k", 128, NDT * T, BF16)
    proj("wk_w", k_sb, kb, None)
    pr.freebuf(xh)

    scrs = pr.alloc("a1scr", 128, AL, F32)
    scr2 = pr.alloc("a1scr2", 128, AL, F32)
    top8 = pr.alloc("a1top", 128, 8, F32)
    taucol = pr.alloc("a1tau", 128, 16, F32)
    for g in range(16):
        for j5 in range(3):
            psp = pr.psum("f", j5 % 2)
            pr.mm(t_(psp, (0, 128), (0, 512)),
                  (qidx.ap((0, 64), (g * 128, 128)), qidx.reg((0, 64), (g * 128, 128))),
                  (kidx.ap((0, 64), (j5 * 512, 512)), kidx.reg((0, 64), (j5 * 512, 512))), tag="a1_sq")
            pr.act((scrs.ap((0, 128), (j5 * 512, 512)), scrs.reg((0, 128), (j5 * 512, 512))),
                   t_(psp, (0, 128), (0, 512)), AF.Relu, tag="a1_srelu")
        src_ = scrs
        for rnd in range(8):
            pr.max8(t_(top8), t_(src_), tag="a1_max8")
            pr.match_replace(t_(scr2), t_(top8), t_(src_), -1.0, tag="a1_mrep")
            src_ = scr2
        pr.copy((taucol.ap((0, 128), (g, 1)), taucol.reg((0, 128), (g, 1))),
                (top8.ap((0, 128), (7, 1)), top8.reg((0, 128), (7, 1))), tag="a1_tau")
    pr.freebuf(scrs)
    pr.freebuf(scr2)
    pr.freebuf(top8)

    wo = load_w_nat(pr, "wo_w", W["wo_w"][0])
    q_g = pr.alloc("a1qg", 128, 4096, BF16)
    amask = pr.alloc("a1am", 128, 12 * 512, BF16)
    taurow = pr.alloc("a1trow", 1, 512, F32)
    bctau = pr.alloc("a1bct", 128, 512, F32)
    getmp = pr.alloc("a1ge", 128, 512, F32)
    y_g = pr.alloc("a1y", 128, NDT * 512, BF16)
    exp_sb = pr.alloc("a1ex", 128, 2 * 1024, BF16)
    bcr = pr.alloc("a1bcr", 64, 512, F32)
    recrow = pr.alloc("a1rr", 1, 512, F32, p0=64)
    xc2 = pr.alloc("a1xc2", 128, 4096, F32)
    stage = pr.alloc("a1st", 128, 1024, F32)
    psBC = pr.psum("f", 2)
    psT1 = pr.psum("f", 2)
    for g in range(4):
        pr.dma("qg", t_(q_g), (qhbm[:, g * 4096:(g + 1) * 4096], qhbm_res), tag="a1_qgld")
        for sub in range(4):
            gg = g * 4 + sub
            pr.transpose(t_(psT1, (0, 1), (0, 128)),
                         (taucol.ap((0, 128), (gg, 1)), taucol.reg((0, 128), (gg, 1))), cn["ident_f"], tag="a1_ttp")
            pr.copy((taurow.ap((0, 1), (sub * 128, 128)), taurow.reg((0, 1), (sub * 128, 128))),
                    t_(psT1, (0, 1), (0, 128)), tag="a1_tre")
        bc_row(pr, cn, psBC, (taurow.ap((0, 1), (0, 512)), taurow.reg((0, 1), (0, 512))), 512, tag="a1_bct")
        pr.copy(t_(bctau), t_(psBC, (0, 128), (0, 512)), tag="a1_bcte")
        for jt in range(12):
            psp = pr.psum("f", 2 + jt % 2)
            pr.mm(t_(psp, (0, 128), (0, 512)),
                  (kidx.ap((0, 64), (jt * 128, 128)), kidx.reg((0, 64), (jt * 128, 128))),
                  (qidx.ap((0, 64), (g * 512, 512)), qidx.reg((0, 64), (g * 512, 512))), tag="a1_st")
            pr.tt(t_(getmp), t_(psp, (0, 128), (0, 512)), t_(bctau), ALU.is_ge, tag="a1_ge")
            pr.ts((amask.ap((0, 128), (jt * 512, 512)), amask.reg((0, 128), (jt * 512, 512))),
                  t_(getmp), 1.0, ALU.subtract, 1e30, ALU.mult, tag="a1_am")
        for hh in range(NH1):
            pp, pb_ = (hh % 2) * 64, (hh // 2) * T
            psY = pr.psum("f", hh % 2)
            qsl = (q_g.ap((pp, 64), ((hh // 2) * 512, 512)), q_g.reg((pp, 64), ((hh // 2) * 512, 512)))

            def av_pair(jp):
                for sub in range(2):
                    jt = jp * 2 + sub
                    col = (hh * 16 + jt) * 65
                    pr.mm(t_(psY, (0, 65), (0, 512)),
                          (vT.ap((0, 128), (col, 65)), vT.reg((0, 128), (col, 65))),
                          (exp_sb.ap((0, 128), ((jp % 2) * 1024 + sub * 512, 512)),
                           exp_sb.reg((0, 128), ((jp % 2) * 1024 + sub * 512, 512))),
                          start=(jt == 0), stop=(jt == 15), tag="a1_av")

            for jp in range(8):
                psw = pr.psum("w", jp % 2)
                for sub in range(2):
                    jt = jp * 2 + sub
                    ksl = (k_sb.ap((pp, 64), (pb_ + jt * 128, 128)), k_sb.reg((pp, 64), (pb_ + jt * 128, 128)))
                    pr.mm(t_(psw, (0, 128), (sub * 512, 512)), ksl, qsl, tag="a1_qk")
                    if jt < 12:
                        pr.tt(t_(psw, (0, 128), (sub * 512, 512)), t_(psw, (0, 128), (sub * 512, 512)),
                              (amask.ap((0, 128), (jt * 512, 512)), amask.reg((0, 128), (jt * 512, 512))),
                              ALU.add, tag="a1_ama")
                exsl = (exp_sb.ap((0, 128), ((jp % 2) * 1024, 1024)), exp_sb.reg((0, 128), ((jp % 2) * 1024, 1024)))
                pr.act(exsl, t_(psw, (0, 128), (0, 1024)), AF.Exp, tag="a1_exp")
                if jp >= 1:
                    av_pair(jp - 1)
            av_pair(7)
            pr.recip(t_(recrow, (0, 1), (0, 512)), t_(psY, (64, 1), (0, 512)), tag="a1_rcp")
            pr.mm(t_(psBC, (0, 64), (0, 512)),
                  (cn["ones_f"].ap((64, 1), (0, 64)), cn["ones_f"].reg((64, 1), (0, 64))),
                  t_(recrow, (0, 1), (0, 512)), tag="a1_bcr")
            pr.copy(t_(bcr, (0, 64), (0, 512)), t_(psBC, (0, 64), (0, 512)), tag="a1_bcre")
            pr.tt((y_g.ap((pp, 64), ((hh // 2) * 512, 512)), y_g.reg((pp, 64), ((hh // 2) * 512, 512))),
                  t_(psY, (0, 64), (0, 512)), t_(bcr, (0, 64), (0, 512)), ALU.mult, tag="a1_yn")
        pr.dma("qs", t_(xc2), (xiap[:, g * 4096:(g + 1) * 4096], xires), tag="a1_ldx2")
        for ot in range(NDT):
            psp = pr.psum("f", 2 + ot % 2)
            for dt_i in range(NDT):
                pr.mm(t_(psp, (0, 128), (0, 512)),
                      (wo.ap((0, 128), (dt_i * 1024 + ot * 128, 128)), wo.reg((0, 128), (dt_i * 1024 + ot * 128, 128))),
                      (y_g.ap((0, 128), (dt_i * 512, 512)), y_g.reg((0, 128), (dt_i * 512, 512))),
                      start=(dt_i == 0), stop=(dt_i == NDT - 1), tag="a1_po")
            stg = (stage.ap((0, 128), ((ot % 2) * 512, 512)), stage.reg((0, 128), ((ot % 2) * 512, 512)))
            pr.stt(stg, t_(psp, (0, 128), (0, 512)),
                   (ob.ap((0, 128), (ot, 1)), ob.reg((0, 128), (ot, 1))),
                   t_(xc2, (0, 128), (ot * 512, 512)), ALU.add, ALU.add, tag="a1_res")
            pr.dma("qs", (xoap[:, g * 4096 + ot * 512:g * 4096 + (ot + 1) * 512], xores), stg, tag="a1_stx")
    for b_ in (qidx, kidx, taucol, amask, taurow, bctau, getmp, exp_sb, y_g, bcr, recrow, wo, xc2, stage,
               q_g, k_sb, vT, qib, kib, qb, kb, vb, ob, qbs):
        pr.freebuf(b_)


def build_launch_l1_attn():
    nc = bass.Bass("TRN2", target_bir_lowering=False)
    cap = declare_consts(nc)
    W = {}
    for name, shape in L1A_WSPECS:
        ap = nc.dram_tensor(f"L1_attn_{name}", shape, F32, kind="ExternalInput").ap()
        W[name] = (ap, DramRes(f"L1_attn_{name}"))
    xin = nc.dram_tensor("xin_fm", [128, 16384], F32, kind="ExternalInput").ap()
    xout = nc.dram_tensor("xout_fm", [128, 16384], F32, kind="ExternalOutput").ap()
    es = ExitStack()
    pr = Prog(nc, 16384, 70400)
    pr.open(es)
    cn = load_consts(pr, cap, names=("ident_b", "ident_f", "ones_f"))
    build_l1_attn(pr, cn, (xin, DramRes("xin_fm")), W, (xout, DramRes("xout_fm")))
    blk = es.enter_context(nc.Block())
    pr.emit_all(blk)
    es.close()
    return nc



_NC_CACHE = {}


def _get_ncs():
    if "v1" not in _NC_CACHE:
        _NC_CACHE["v1"] = (build_launch_l0_attn(first=True), build_launch_l0_ffn(),
                           build_launch_l1_attn(), build_launch_l1_ffn(last=True))
    return _NC_CACHE["v1"]


def kernel_unfused(**inputs):
    consts = make_consts()
    x = np.ascontiguousarray(inputs["L0_attn_x"], dtype=np.float32)
    ncA, ncF0, ncA1, ncF1 = _get_ncs()
    cores = list(range(8))

    def im_for(wprefix, wspecs, extra):
        ims = []
        for b in range(8):
            d = {}
            for name, _ in wspecs:
                d[f"{wprefix}_{name}"] = inputs[f"{wprefix}_{name}"]
            d.update(consts)
            d.update(extra[b])
            ims.append(d)
        return ims

    resA = run_bass_kernel_spmd(ncA, im_for("L0_attn", L0A_WSPECS, [{"x": x[b]} for b in range(8)]), core_ids=cores)
    x1 = [resA.results[b]["xout_fm"] for b in range(8)]
    resF0 = run_bass_kernel_spmd(ncF0, im_for("L0_ffn", L0F_WSPECS, [{"xin_fm": x1[b]} for b in range(8)]), core_ids=cores)
    x2 = [resF0.results[b]["xout_fm"] for b in range(8)]
    resA1 = run_bass_kernel_spmd(ncA1, im_for("L1_attn", L1A_WSPECS, [{"xin_fm": x2[b]} for b in range(8)]), core_ids=cores)
    x3 = [resA1.results[b]["xout_fm"] for b in range(8)]
    resF1 = run_bass_kernel_spmd(ncF1, im_for("L1_ffn", L1F_WSPECS, [{"xin_fm": x3[b]} for b in range(8)]), core_ids=cores)
    out = np.stack([np.asarray(resF1.results[b]["out"]) for b in range(8)])
    return out.astype(np.float32)


def build_fused():
    nc = bass.Bass("TRN2", target_bir_lowering=False)
    cap = declare_consts(nc)
    WA, WF0, WA1, WF1 = {}, {}, {}, {}
    for pref, specs, Wd in (("L0_attn", L0A_WSPECS, WA), ("L0_ffn", L0F_WSPECS, WF0),
                            ("L1_attn", L1A_WSPECS, WA1), ("L1_ffn", L1F_WSPECS, WF1)):
        for name, shape in specs:
            ap = nc.dram_tensor(f"{pref}_{name}", shape, F32, kind="ExternalInput").ap()
            Wd[name] = (ap, DramRes(f"{pref}_{name}"))
    x_tm = nc.dram_tensor("x", [T, D], F32, kind="ExternalInput").ap()
    out_tm = nc.dram_tensor("out", [T, D], F32, kind="ExternalOutput").ap()
    scr = {}
    for nm in ("xfm0", "x1", "x2", "x3", "x4"):
        scr[nm] = (nc.dram_tensor(nm, [128, 16384], F32, kind="Internal").ap(), DramRes(nm))
    es = ExitStack()
    pr = Prog(nc, 21504, 58368)
    pr.open(es)
    cn = load_consts(pr, cap)
    build_entry_transpose(pr, cn, (x_tm, DramRes("x")), scr["xfm0"])
    build_l0_attn(pr, cn, scr["xfm0"], WA, scr["x1"])
    pr.freebuf(cn["posrow_f"])
    pr.freebuf(cn["iexp_f"])
    build_l0_ffn(pr, cn, scr["x1"], WF0, scr["x2"])
    build_l1_attn(pr, cn, scr["x2"], WA1, scr["x3"])
    build_l1_ffn(pr, cn, scr["x3"], WF1, scr["x4"])
    build_exit_transpose(pr, cn, scr["x4"], (out_tm, DramRes("out")))
    blk = es.enter_context(nc.Block())
    pr.emit_all(blk)
    es.close()
    return nc


def kernel(**inputs):
    consts = make_consts()
    x = np.ascontiguousarray(inputs["L0_attn_x"], dtype=np.float32)
    if "fused" not in _NC_CACHE:
        _NC_CACHE["fused"] = build_fused()
    nc = _NC_CACHE["fused"]
    ims = []
    for b in range(8):
        d = {"x": x[b]}
        for pref, specs in (("L0_attn", L0A_WSPECS), ("L0_ffn", L0F_WSPECS),
                            ("L1_attn", L1A_WSPECS), ("L1_ffn", L1F_WSPECS)):
            for name, _ in specs:
                d[f"{pref}_{name}"] = inputs[f"{pref}_{name}"]
        d.update(consts)
        ims.append(d)
    res = run_bass_kernel_spmd(nc, ims, core_ids=list(range(8)))
    out = np.stack([np.asarray(res.results[b]["out"]) for b in range(8)])
    return out.astype(np.float32)
